# version 03
# baseline (speedup 1.0000x reference)
import math
import contextlib

import numpy as np
import ml_dtypes

import bass_rust
import concourse.bass as bass
import concourse.mybir as mybir
from concourse.bass_utils import run_bass_kernel_spmd

F32 = mybir.dt.float32
BF16 = mybir.dt.bfloat16
U16 = mybir.dt.uint16
I32 = mybir.dt.int32
U8 = mybir.dt.uint8
OP = mybir.AluOpType
AF = mybir.ActivationFunctionType
BF = ml_dtypes.bfloat16

B, S, D = 2, 4096, 4096
H, DH = 32, 128
SCALE = 1.0 / math.sqrt(DH)
NCORES = 8
TOK = 1024
HTOK = 1536
NCH = 32
EPS = 1e-5

HEAP_BYTES = 212832


def _al(x, a=32):
    return (x + a - 1) // a * a


class _Off:
    def __init__(self, base=0):
        self.cur = base
        self.map = {}

    def add(self, name, nbytes):
        self.map[name] = self.cur
        self.cur = _al(self.cur + nbytes)
        return self.map[name]


_P = _Off()
_P.add("cos", 1536 * 2)
_P.add("sin", 1536 * 2)
_P.add("wk", 4096 * 2)
_P.add("wv", 4096 * 2)
_P.add("kr", 1536 * 2)
_P.add("vext", 1548 * 2)
_P.add("rsign", 128 * 2)
_P.add("idn", 128 * 2)
_P.add("epad", 256 * 2)
_P.add("e4", 128 * 2)
_P.add("ones1", 128 * 4)
_P.add("onesc", 32)
_P.add("alham", 8 * 4)
_P.add("alpha", 8 * 4)
_P.add("maskm", 256 * 2)
_P.add("onesr", 256 * 2)
_P.add("fbias", 32 * 4)
_P.add("consts", 8 * 4)
_P.add("shiftv", 64)
_P.add("stat", 2048 * 4)
_P.add("scr", 16 * 4)
_P.add("kb", 256 * 2)
_P.add("blnd", 256 * 2)
_P.add("atA", 1024 * 2)
_P.add("atB", 1024 * 2)
_P.add("qb", 1024 * 2)
PERSIST_END = _P.cur

R3 = _al(PERSIST_END)
R3_SZ = 2 * 8192
R1 = _al(R3 + R3_SZ)
R1_SZ = 65536
R2 = _al(R1 + R1_SZ)
R2_SZ = 65536
RQ = _al(R2 + R2_SZ)
RQ_SZ = 16384
assert RQ + RQ_SZ <= HEAP_BYTES, (RQ + RQ_SZ, HEAP_BYTES)

_dbg_taps = []


def _rope_cache():
    inv_freq = 1.0 / 10000.0 ** (np.arange(0, DH, 2, dtype=np.float32) / DH)
    t = np.arange(S, dtype=np.float32)
    f = np.outer(t, inv_freq)
    cos = np.zeros((S, DH), dtype=np.float32)
    sin = np.zeros((S, DH), dtype=np.float32)
    cos[:, 0::2] = np.cos(f)
    cos[:, 1::2] = np.cos(f)
    sin[:, 0::2] = np.sin(f)
    sin[:, 1::2] = np.sin(f)
    return cos, sin


def _prep_shared(Wq, Wk, Wv, Wo, wq3, wnorm, fbias):
    d = {}
    d["wq"] = np.ascontiguousarray(
        Wq.reshape(32, 128, 32, 128).transpose(0, 3, 2, 1))
    d["wo"] = np.ascontiguousarray(
        Wo.reshape(32, 128, 32, 128).transpose(0, 3, 2, 1))
    d["wk"] = np.ascontiguousarray(Wk.reshape(128, 32, 128).transpose(2, 1, 0))
    d["wv"] = np.ascontiguousarray(Wv.reshape(128, 32, 128).transpose(2, 1, 0))

    pk = np.ascontiguousarray(wq3).view(np.uint8)
    pk13 = np.concatenate(
        [pk, np.zeros((pk.shape[0], 1), np.uint8)], axis=1)
    j = np.arange(32)
    bidx = (3 * j) // 8
    lo = pk13[:, bidx]
    hi = pk13[:, bidx + 1]
    b2 = np.stack([lo, hi], axis=-1)
    u16 = np.ascontiguousarray(b2).reshape(-1, 64).view(np.uint16)
    u16oi = u16.reshape(4096, 4096)
    d["uq"] = np.ascontiguousarray(
        u16oi.reshape(32, 128, 32, 128).transpose(0, 3, 2, 1))

    n_og = wnorm.reshape(4096, 128)
    d["n4"] = np.ascontiguousarray(
        n_og.reshape(32, 128, 32, 4).transpose(0, 3, 2, 1))
    d["nt"] = np.ascontiguousarray(n_og.T)
    d["fbias"] = np.ascontiguousarray(fbias.reshape(32, 128).T)

    e4 = np.zeros((4, 128), dtype=BF)
    for g in range(4):
        e4[g, g * 32:(g + 1) * 32] = 1
    d["e4"] = e4
    epad = np.zeros((128, 256), dtype=BF)
    p = np.arange(128)
    epad[p, 124 + p // 32] = 1
    d["epad"] = epad
    rsign = np.zeros((128, 128), dtype=np.float32)
    m = np.arange(64)
    rsign[m + 64, m] = -1.0
    rsign[m, m + 64] = 1.0
    d["rsign"] = rsign.astype(BF)
    d["idn"] = np.eye(128, dtype=BF)
    d["ones1"] = np.ones((1, 128), np.float32)
    d["onesc"] = np.ones((128, 1), dtype=BF)
    d["onesr"] = np.ones((1, 256), dtype=BF)
    consts = np.zeros((128, 8), np.float32)
    consts[:, 0] = EPS
    d["consts"] = consts
    s_ = (3 * (np.arange(128) % 32)) % 8
    d["mask32"] = (7 << s_).astype(np.int32).reshape(128, 1)
    d["inv2s"] = (0.5 ** s_).astype(np.float32).reshape(128, 1)
    return d


def _prep_core(x, core):
    b = core // 4
    T0 = 1024 * (core % 4)
    lo = T0 - 256
    xh = np.zeros((HTOK, D), dtype=np.float32)
    s0 = max(lo, 0)
    s1 = min(T0 + 1280, S)
    xh[s0 - lo:s1 - lo] = x[b, s0:s1]
    xh_arr = np.ascontiguousarray(
        xh.reshape(6, 256, 32, 128).transpose(0, 3, 2, 1))

    cos, sin = _rope_cache()
    pos = np.clip(np.arange(lo, lo + HTOK), 0, S - 1)
    cosT = np.ascontiguousarray(cos[pos].T)
    sinT = np.ascontiguousarray(sin[pos].T)

    alpha = np.zeros((128, 8), np.float32)
    alham = np.zeros((128, 8), np.float32)
    for bl in range(4):
        gb = T0 // 256 + bl
        if gb == 0:
            a = np.ones(256, np.float32)
        else:
            a = np.linspace(0.0, 1.0, 256).astype(np.float32)
        for qs in range(2):
            alpha[:, 2 * bl + qs] = a[qs * 128:(qs + 1) * 128]
            alham[:, 2 * bl + qs] = (1.0 - a)[qs * 128:(qs + 1) * 128]

    maskm = np.zeros((1, 256), dtype=BF)
    if core % 4 == 3:
        maskm[:] = np.float32(-30.0 / SCALE).astype(BF)
    return dict(xh=xh_arr, cosT=cosT, sinT=sinT, alpha=alpha, alham=alham,
                maskm=maskm)


class Sched:
    ENGINES = ("y", "g", "t", "v", "s")

    def __init__(self):
        self.q = {e: [] for e in self.ENGINES}
        self.cnt = {}
        self.hw = {}
        self.semh = {}

    def new_sem(self, name, handle):
        self.semh[name] = handle
        self.cnt[name] = 0
        for e in self.ENGINES:
            self.hw[(e, name)] = 0

    def wait(self, e, sem, val):
        if val is None or val <= 0:
            return
        if self.hw[(e, sem)] >= val:
            return
        self.hw[(e, sem)] = val
        h = self.semh[sem]
        self.q[e].append(lambda eng, h=h, v=val: eng.wait_ge(h, v))

    def emit(self, e, fn, inc=None, k=1):
        if inc is None:
            self.q[e].append(fn)
            return None
        self.cnt[inc] += k
        tgt = self.cnt[inc]
        h = self.semh[inc]

        def wrapped(eng, fn=fn, h=h, k=k):
            fn(eng).then_inc(h, k)
        self.q[e].append(wrapped)
        return tgt

    def dma(self, e, out_ap, in_ap, sem):
        return self.emit(
            e, lambda eng, o=out_ap, i=in_ap: eng.dma_start(out=o, in_=i),
            inc=sem, k=16)

    def run(self, e, eng):
        for fn in self.q[e]:
            fn(eng)


def _bc(ap, reps, inner):
    pat = [list(p) for p in ap.ap]
    assert len(pat) == 2, pat
    return bass_rust.AP(ap.tensor, ap.offset, [pat[0], [0, reps], pat[1]])


def _strided(ap, elem_off, dims):
    pat = [list(p) for p in ap.ap]
    return bass_rust.AP(ap.tensor, ap.offset + elem_off,
                        [pat[0]] + [list(d) for d in dims])


def build_nc(debug_taps=()):
    nc = bass.Bass("TRN2", target_bir_lowering=False)

    di = {}

    def inp(name, shape, dt):
        di[name] = nc.dram_tensor(name, list(shape), dt,
                                  kind="ExternalInput").ap()
        return di[name]

    xh = inp("xh", (6, 128, 32, 256), F32)
    wq = inp("wq", (32, 128, 32, 128), F32)
    wo = inp("wo", (32, 128, 32, 128), F32)
    wk = inp("wk", (128, 32, 128), F32)
    wv = inp("wv", (128, 32, 128), F32)
    uq = inp("uq", (32, 128, 32, 128), U16)
    n4 = inp("n4", (32, 4, 32, 128), F32)
    nt = inp("nt", (128, 4096), F32)
    fbias_d = inp("fbias", (128, 32), F32)
    cosT = inp("cosT", (128, 1536), F32)
    sinT = inp("sinT", (128, 1536), F32)
    alpha = inp("alpha", (128, 8), F32)
    alham = inp("alham", (128, 8), F32)
    maskm = inp("maskm", (1, 256), BF16)
    rsign = inp("rsign", (128, 128), BF16)
    idn = inp("idn", (128, 128), BF16)
    epad = inp("epad", (128, 256), BF16)
    e4 = inp("e4", (4, 128), BF16)
    ones1 = inp("ones1", (1, 128), F32)
    onesc = inp("onesc", (128, 1), BF16)
    onesr = inp("onesr", (1, 256), BF16)
    consts = inp("consts", (128, 8), F32)
    mask32 = inp("mask32", (128, 1), I32)
    inv2s = inp("inv2s", (128, 1), F32)

    out_d = nc.dram_tensor("out", [32, 128, 1024], F32,
                           kind="ExternalOutput").ap()
    x2sp = nc.dram_tensor("x2sp", [32, 128, 1024], F32, kind="Internal").ap()
    dbg_d = {}
    for tap, shape, dt in (
        ("dbg_h", [128, 32768], BF16), ("dbg_kr", [128, 1536], BF16),
        ("dbg_vext", [128, 1548], BF16), ("dbg_qr", [128, 4096], BF16),
        ("dbg_ao", [128, 32768], BF16), ("dbg_a", [128, 32768], BF16),
        ("dbg_h2", [128, 32768], BF16), ("dbg_f", [128, 32768], BF16),
    ):
        if tap in debug_taps:
            dbg_d[tap] = nc.dram_tensor(tap, shape, dt,
                                        kind="ExternalOutput").ap()

    ctx = contextlib.ExitStack()
    heap = ctx.enter_context(nc.sbuf_tensor("heap", [128, HEAP_BYTES], U8))
    P = ctx.enter_context(nc.psum_tensor("psum", [128, 4096], F32))
    PB = P.bitcast(BF16)

    sch = Sched()
    for s in ("sT", "sV", "sS", "dCN", "dW0", "dXB0", "dXB1", "dWS0", "dWS1",
              "dN4", "dNT", "dXR0", "dXR1", "dSP0", "dSP1", "dRL0", "dRL1", "dRL2", "dRL3",
              "dFR0", "dFR1", "dOUT0", "dOUT1", "dDBG"):
        sch.new_sem(s, ctx.enter_context(nc.semaphore(s)))
    for i_ in range(8):
        sch.new_sem("dT%d" % i_, ctx.enter_context(nc.semaphore("dT%d" % i_)))

    def vw(off, n, dt, p0=0, p1=128):
        sz = mybir.dt.size(dt)
        return heap[p0:p1, off:off + n * sz].bitcast(dt)

    def pv(elem_off, n, p0=0, p1=128):
        return P[p0:p1, elem_off:elem_off + n]

    O = _P.map
    W = sch.wait
    E = sch.emit
    DMA = sch.dma

    cos_v = vw(O["cos"], 1536, BF16)
    sin_v = vw(O["sin"], 1536, BF16)
    wk_v = vw(O["wk"], 4096, BF16)
    wv_v = vw(O["wv"], 4096, BF16)
    kr_v = vw(O["kr"], 1536, BF16)
    vext_v = vw(O["vext"], 1548, BF16)
    rsign_v = vw(O["rsign"], 128, BF16)
    idn_v = vw(O["idn"], 128, BF16)
    epad_v = vw(O["epad"], 256, BF16)
    e4_v = vw(O["e4"], 128, BF16, 0, 4)
    ones1_v = vw(O["ones1"], 128, F32, 0, 1)
    onesc_v = vw(O["onesc"], 1, BF16)
    alham_v = vw(O["alham"], 8, F32)
    alpha_v = vw(O["alpha"], 8, F32)
    maskm_v = vw(O["maskm"], 256, BF16, 0, 1)
    onesr_v = vw(O["onesr"], 256, BF16, 0, 1)
    fbias_v = vw(O["fbias"], 32, F32)
    consts_v = vw(O["consts"], 8, F32)
    mask32_v = vw(O["shiftv"], 1, I32)
    inv2s_v = vw(O["shiftv"] + 32, 1, F32)
    eps_ap = vw(O["consts"], 8, F32, 0, 1)[:, 0:1]
    zero_ap = consts_v[:, 1:2]
    stat = vw(O["stat"], 2048, F32, 0, 1)
    scr = vw(O["scr"], 16, F32)
    kb_v = vw(O["kb"], 256, BF16)
    blnd_v = vw(O["blnd"], 256, BF16)
    atA_v = vw(O["atA"], 1024, BF16)
    atB_v = vw(O["atB"], 1024, BF16)
    qb_v = vw(O["qb"], 1024, BF16)

    for name, v_ in (("rsign", rsign_v),
                     ("idn", idn_v), ("epad", epad_v), ("e4", e4_v),
                     ("ones1", ones1_v), ("onesc", onesc_v),
                     ("alham", alham_v), ("alpha", alpha_v),
                     ("maskm", maskm_v), ("onesr", onesr_v),
                     ("fbias", fbias_v),
                     ("consts", consts_v), ("mask32", mask32_v),
                     ("inv2s", inv2s_v)):
        DMA("y", v_, di[name], "dCN")
    c_consts = sch.cnt["dCN"]
    DMA("g", cos_v, di["cosT"], "dW0")
    DMA("g", sin_v, di["sinT"], "dW0")
    DMA("g", wk_v, wk, "dW0")
    DMA("g", wv_v, wv, "dW0")
    w0_all = sch.cnt["dW0"]
    for kc in range(12):
        E("v", lambda eng, a=vext_v[:, kc * 129 + 128: kc * 129 + 129]:
          eng.memset(a, 1.0), inc=("sV" if kc == 11 else None))

    xb_off = [R1, R1 + 16384]
    xsq_off = R1 + 32768
    htmp_off = R1 + 49152
    hown_off = R2
    hown = vw(hown_off, 32768, BF16)

    xb_t = [None] * 6
    norm_done = [0] * 6
    kv_done = [0] * 6
    rope_done = [0] * 6
    mvar_done = [0] * 6
    sumsq_done = [0] * 6

    for e_ in ("t", "v", "s"):
        W(e_, "dCN", c_consts)
    for tc in range(6):
        xb = vw(xb_off[tc % 2], 8192, BF16)
        xsq = vw(xsq_off, 8192, BF16)
        xbsem = "dXB0" if tc % 2 == 0 else "dXB1"
        if tc >= 2:
            W("g", "sV", norm_done[tc - 2])
        xb_t[tc] = DMA("g", xb, xh[tc], xbsem)
        W("s", xbsem, xb_t[tc])
        if tc >= 1:
            W("s", "sT", sumsq_done[tc - 1])
        E("s", lambda eng, o=xsq, i=xb: eng.activation(
            out=o, in_=i, func=AF.Square, bias=0.0, scale=1.0), inc="sS")
        sq_ready = sch.cnt["sS"]
        W("t", xbsem, xb_t[tc])
        if tc >= 1:
            W("t", "sV", mvar_done[tc - 1])
        for ic in range(NCH):
            E("t", lambda eng, o=pv(0, 256, 0, 1), l=onesc_v,
              r=xb[:, ic * 256:(ic + 1) * 256], st=(ic == 0), sp=(ic == 31):
              eng.matmul(o, l, r, start=st, stop=sp))
        W("t", "sS", sq_ready)
        for ic in range(NCH):
            E("t", lambda eng, o=pv(512, 256, 0, 1), l=onesc_v,
              r=xsq[:, ic * 256:(ic + 1) * 256], st=(ic == 0), sp=(ic == 31):
              eng.matmul(o, l, r, start=st, stop=sp),
              inc=("sT" if ic == 31 else None))
        stats_done = sch.cnt["sT"]
        sumsq_done[tc] = stats_done
        W("v", "sT", stats_done)
        E("v", lambda eng: eng.tensor_scalar(
            out=stat[:, 0:256], in0=pv(0, 256, 0, 1), scalar1=1.0 / 4096,
            scalar2=None, op0=OP.mult), inc="sV")
        W("s", "sV", sch.cnt["sV"])
        E("s", lambda eng: eng.activation(
            out=stat[:, 256:512], in_=stat[:, 0:256], func=AF.Square,
            bias=0.0, scale=1.0), inc="sS")
        W("v", "sS", sch.cnt["sS"])
        E("v", lambda eng: eng.scalar_tensor_tensor(
            out=stat[:, 512:768], in0=pv(512, 256, 0, 1), scalar=1.0 / 4096,
            in1=stat[:, 256:512], op0=OP.mult, op1=OP.subtract), inc="sV")
        W("s", "sV", sch.cnt["sV"])
        E("s", lambda eng: eng.activation(
            out=stat[:, 512:768], in_=stat[:, 512:768], func=AF.Sqrt,
            bias=eps_ap, scale=1.0), inc="sS")
        W("v", "sS", sch.cnt["sS"])
        E("v", lambda eng: eng.reciprocal(stat[:, 512:768], stat[:, 512:768]))
        E("v", lambda eng: eng.drain())
        E("v", lambda eng: eng.tensor_tensor(
            out=stat[:, 0:256], in0=stat[:, 0:256], in1=stat[:, 512:768],
            op=OP.mult), inc="sV")
        mvar_done[tc] = sch.cnt["sV"]
        bb = 1024 if tc % 2 == 0 else 1536
        W("t", "sV", mvar_done[tc])
        if tc >= 2:
            W("t", "sV", norm_done[tc - 2])
        E("t", lambda eng, o=pv(bb, 256), r=stat[:, 512:768]:
          eng.matmul(o, ones1_v, r, start=True, stop=True))
        E("t", lambda eng, o=pv(bb + 256, 256), r=stat[:, 0:256]:
          eng.matmul(o, ones1_v, r, start=True, stop=True), inc="sT")
        bcast_done = sch.cnt["sT"]
        W("v", "sT", bcast_done)
        if tc in (1, 2, 3, 4):
            hdst = _strided(hown, (tc - 1) * 256, [[1024, 32], [1, 256]])
        else:
            hch = vw(htmp_off, 8192, BF16)
            hdst = hch.rearrange("p (c t) -> p c t", c=32)
        xb3 = xb.rearrange("p (c t) -> p c t", c=32)
        E("v", lambda eng, o=hdst, i=xb3, b_=_bc(pv(bb, 256), 32, 256):
          eng.tensor_tensor(out=o, in0=i, in1=b_, op=OP.mult))
        E("v", lambda eng: eng.drain())
        E("v", lambda eng, o=hdst, b_=_bc(pv(bb + 256, 256), 32, 256):
          eng.tensor_tensor(out=o, in0=o, in1=b_, op=OP.subtract), inc="sV")
        norm_done[tc] = sch.cnt["sV"]

        def hsl(ic, lo=0, n=256, tc=tc):
            if tc in (1, 2, 3, 4):
                return vw(hown_off + ic * 2048 + (tc - 1) * 512 + lo * 2,
                          n, BF16)
            return vw(htmp_off + ic * 512 + lo * 2, n, BF16)

        W("t", "sV", norm_done[tc])
        if tc >= 1:
            W("t", "sV", rope_done[tc - 1])
            W("t", "sS", kv_done[tc - 1])
        W("t", "dW0", w0_all)
        for ic in range(NCH):
            E("t", lambda eng, o=pv(2048, 256),
              l=wk_v[:, ic * 128:(ic + 1) * 128], r=hsl(ic),
              st=(ic == 0), sp=(ic == 31):
              eng.matmul(o, l, r, start=st, stop=sp),
              inc=("sT" if ic == 31 else None))
        k_done = sch.cnt["sT"]
        W("s", "sT", k_done)
        E("s", lambda eng: eng.activation(
            out=kb_v, in_=pv(2048, 256), func=AF.Copy, bias=0.0, scale=1.0),
            inc="sS")
        W("t", "sS", sch.cnt["sS"])
        E("t", lambda eng: eng.matmul(
            pv(2560, 256), rsign_v, kb_v, start=True, stop=True), inc="sT")
        rk_done = sch.cnt["sT"]
        W("v", "sT", rk_done)
        cslc = cos_v[:, tc * 256:(tc + 1) * 256]
        sslc = sin_v[:, tc * 256:(tc + 1) * 256]
        krd = kr_v[:, tc * 256:(tc + 1) * 256]
        E("v", lambda eng, o=krd, c=cslc: eng.tensor_tensor(
            out=o, in0=pv(2048, 256), in1=c, op=OP.mult))
        E("v", lambda eng, s_=sslc: eng.tensor_tensor(
            out=kb_v, in0=pv(2560, 256), in1=s_, op=OP.mult))
        E("v", lambda eng: eng.drain())
        E("v", lambda eng, o=krd: eng.tensor_tensor(
            out=o, in0=o, in1=kb_v, op=OP.add), inc="sV")
        rope_done[tc] = sch.cnt["sV"]
        W("t", "dW0", w0_all)
        for qs in range(2):
            for ic in range(NCH):
                E("t", lambda eng, o=pv(3072 + qs * 128, 128),
                  l=hsl(ic, qs * 128, 128),
                  r=wv_v[:, ic * 128:(ic + 1) * 128],
                  st=(ic == 0), sp=(ic == 31):
                  eng.matmul(o, l, r, start=st, stop=sp),
                  inc=("sT" if (ic == 31 and qs == 1) else None))
        v_done = sch.cnt["sT"]
        W("s", "sT", v_done)
        for qs in range(2):
            kc = tc * 2 + qs
            E("s", lambda eng, o=vext_v[:, kc * 129:kc * 129 + 128],
              i=pv(3072 + qs * 128, 128):
              eng.activation(out=o, in_=i, func=AF.Copy, bias=0.0, scale=1.0),
              inc=("sS" if qs == 1 else None))
        kv_done[tc] = sch.cnt["sS"]

    if "dbg_h" in dbg_d:
        W("y", "sV", norm_done[4])
        th_ = DMA("y", dbg_d["dbg_h"], hown, "dT1")
        W("s", "dT1", th_)
    if "dbg_kr" in dbg_d:
        W("y", "sV", rope_done[5])
        DMA("y", dbg_d["dbg_kr"], kr_v, "dT5")
    if "dbg_vext" in dbg_d:
        W("y", "sS", kv_done[5])
        DMA("y", dbg_d["dbg_vext"], vext_v, "dT6")

    ao_off = R1
    ao = vw(ao_off, 32768, BF16)
    qr_off = [RQ, RQ + 8192]
    ws_off = [R3, R3 + 8192]
    ws_t = [None, None]
    ws_freed = [0, 0]
    qps_free = [[0, 0], [0, 0]]
    rqps_free = [0, 0]
    qb_free = [0, 0]
    sig_done = {}
    sps_free = [0, 0]
    avps_free_v = [0, 0]
    tps_free = [0, 0]
    tp_prev = 0
    unit_ctr = 0
    last_blend_v = 0
    last_blend_s = 0

    for hg in range(8):
        heads = list(range(hg * 4, hg * 4 + 4))
        qrv = vw(qr_off[hg % 2], 4096, BF16)
        for j, oc in enumerate(heads):
            slot = oc % 2
            wssem = "dWS0" if slot == 0 else "dWS1"
            W("g", "sT", ws_freed[slot])
            ws_t[slot] = DMA("g", vw(ws_off[slot], 4096, BF16), wq[oc], wssem)
            W("t", wssem, ws_t[slot])
            if hg == 0 and oc == heads[0]:
                W("t", "sV", rope_done[5])
            if hg > 0 and oc == heads[0]:
                W("t", "sV", avps_free_v[0])
                W("t", "sV", avps_free_v[1])
                W("t", "sV", tps_free[0])
                W("t", "sV", tps_free[1])
                W("t", "sS", sps_free[0])
                W("t", "sS", sps_free[1])
            for nh in range(2):
                qcol = (2 * slot + nh) * 512
                W("t", "sV", qps_free[slot][nh])
                for ic in range(NCH):
                    E("t", lambda eng, o=pv(qcol, 512),
                      l=vw(ws_off[slot] + ic * 256, 128, BF16),
                      r=vw(hown_off + ic * 2048 + nh * 1024, 512, BF16),
                      st=(ic == 0), sp=(ic == 31):
                      eng.matmul(o, l, r, start=st, stop=sp),
                      inc=("sT" if ic == 31 else None))
                q_done = sch.cnt["sT"]
                W("s", "sT", q_done)
                W("s", "sV", qb_free[nh])
                E("s", lambda eng, o=qb_v[:, nh * 512:(nh + 1) * 512],
                  i=pv(qcol, 512):
                  eng.activation(out=o, in_=i, func=AF.Copy, bias=0.0,
                                 scale=1.0), inc="sS")
                qb_done = sch.cnt["sS"]
                W("t", "sS", qb_done)
                W("t", "sV", rqps_free[nh])
                E("t", lambda eng, o=pv(2048 + nh * 512, 512),
                  r=qb_v[:, nh * 512:(nh + 1) * 512]:
                  eng.matmul(o, rsign_v, r, start=True, stop=True), inc="sT")
                rot_done = sch.cnt["sT"]
                W("v", "sT", rot_done)
                co = cos_v[:, 256 + nh * 512: 256 + (nh + 1) * 512]
                so = sin_v[:, 256 + nh * 512: 256 + (nh + 1) * 512]
                qd = qrv[:, j * 1024 + nh * 512: j * 1024 + (nh + 1) * 512]
                tu = qb_v[:, nh * 512:(nh + 1) * 512]
                E("v", lambda eng, o=tu, c=co, i=pv(qcol, 512):
                  eng.tensor_tensor(out=o, in0=i, in1=c, op=OP.mult))
                E("v", lambda eng, o=qd, s_=so, i=pv(2048 + nh * 512, 512):
                  eng.tensor_tensor(out=o, in0=i, in1=s_, op=OP.mult))
                E("v", lambda eng: eng.drain())
                E("v", lambda eng, o=qd, u=tu: eng.tensor_tensor(
                    out=o, in0=o, in1=u, op=OP.add), inc="sV")
                qps_free[slot][nh] = sch.cnt["sV"]
                rqps_free[nh] = sch.cnt["sV"]
                qb_free[nh] = sch.cnt["sV"]
            ws_freed[slot] = q_done
        rope_hg_done = sch.cnt["sV"]

        if hg == 0 and "dbg_qr" in dbg_d:
            W("y", "sV", rope_hg_done)
            tq_ = DMA("y", dbg_d["dbg_qr"], qrv, "dT0")
            W("v", "dT0", tq_)

        for b in range(4):
            for j, h_abs in enumerate(heads):
                for win in range(2):
                    atv = atA_v if win == 0 else atB_v
                    kb_chunk = 2 * (b + win)
                    for hf in range(2):
                        u = unit_ctr
                        unit_ctr += 1
                        sbase = 2048 + (u % 2) * 512
                        W("t", "sV", rope_hg_done)
                        W("t", "sS", sps_free[u % 2])
                        for k2 in range(2):
                            kc4 = hf * 2 + k2
                            is_mask = (b == 3 and win == 1 and kc4 >= 2)
                            last = (k2 == 1 and not is_mask)
                            E("t", lambda eng, o=pv(sbase + k2 * 256, 256),
                              l=kr_v[:, (kb_chunk + kc4) * 128:
                                     (kb_chunk + kc4 + 1) * 128],
                              r=qrv[:, j * 1024 + b * 256:
                                    j * 1024 + (b + 1) * 256],
                              sp=(not is_mask):
                              eng.matmul(o, l, r, start=True, stop=sp),
                              inc=("sT" if last else None))
                            if is_mask:
                                E("t", lambda eng,
                                  o=pv(sbase + k2 * 256, 256),
                                  l=maskm_v[:, (kc4 - 2) * 128:
                                            (kc4 - 1) * 128]:
                                  eng.matmul(o, l, onesr_v, start=False,
                                             stop=True),
                                  inc=("sT" if k2 == 1 else None))
                        sc_done = sch.cnt["sT"]
                        W("s", "sT", sc_done)
                        E("s", lambda eng, o=atv[:, hf * 512:(hf + 1) * 512],
                          i=pv(sbase, 512):
                          eng.activation(out=o, in_=i, func=AF.Sigmoid,
                                         bias=zero_ap, scale=SCALE),
                          inc="sS")
                        sig_done[u] = sch.cnt["sS"]
                        sps_free[u % 2] = sig_done[u]
                aset = h_abs % 2
                W("t", "sS", sig_done[unit_ctr - 1])
                W("t", "sV", avps_free_v[aset])
                for win in range(2):
                    atv = atA_v if win == 0 else atB_v
                    abase = (0 if win == 0 else 512) + aset * 3072
                    kb_chunk = 2 * (b + win)
                    for qs in range(2):
                        for kc4 in range(4):
                            last = (win == 1 and qs == 1 and kc4 == 3)
                            E("t", lambda eng, o=pv(abase + qs * 129, 129),
                              l=atv[:, kc4 * 256 + qs * 128:
                                    kc4 * 256 + (qs + 1) * 128],
                              r=vext_v[:, (kb_chunk + kc4) * 129:
                                       (kb_chunk + kc4 + 1) * 129],
                              st=(kc4 == 0), sp=(kc4 == 3):
                              eng.matmul(o, l, r, start=st, stop=sp),
                              inc=("sT" if last else None))
                av_done = sch.cnt["sT"]
                W("v", "sT", av_done)
                W("v", "sT", tp_prev)
                for win in range(2):
                    abase = (0 if win == 0 else 512) + aset * 3072
                    base_ap = P[:, abase + 128:abase + 129]
                    sums = bass_rust.AP(base_ap.tensor, base_ap.offset,
                                        [[4096, 128], [129, 2]])
                    E("v", lambda eng, o=scr[:, win * 2:win * 2 + 2], i=sums:
                      eng.reciprocal(o, i))
                E("v", lambda eng: eng.drain())
                E("v", lambda eng, b2=alham_v[:, 2 * b:2 * b + 2]:
                  eng.tensor_tensor(out=scr[:, 4:6], in0=scr[:, 0:2],
                                    in1=b2, op=OP.mult))
                E("v", lambda eng, b2=alpha_v[:, 2 * b:2 * b + 2]:
                  eng.tensor_tensor(out=scr[:, 6:8], in0=scr[:, 2:4],
                                    in1=b2, op=OP.mult), inc="sV")
                E("v", lambda eng: eng.drain())
                for qs in range(2):
                    E("v", lambda eng, o=kb_v[:, qs * 128:(qs + 1) * 128],
                      i=pv(aset * 3072 + qs * 129, 128),
                      sc1=scr[:, 4 + qs:5 + qs]:
                      eng.tensor_scalar(out=o, in0=i, scalar1=sc1,
                                        scalar2=None, op0=OP.mult))
                E("v", lambda eng: eng.drain())
                for qs in range(2):
                    E("v", lambda eng, o=blnd_v[:, qs * 128:(qs + 1) * 128],
                      i=pv(aset * 3072 + 512 + qs * 129, 128),
                      sc2=scr[:, 6 + qs:7 + qs],
                      t_=kb_v[:, qs * 128:(qs + 1) * 128]:
                      eng.scalar_tensor_tensor(
                          out=o, in0=i, scalar=sc2, in1=t_,
                          op0=OP.mult, op1=OP.add),
                      inc=("sV" if qs == 1 else None))
                last_blend_v = sch.cnt["sV"]
                avps_free_v[aset] = last_blend_v
                tslot = h_abs % 2
                W("t", "sV", last_blend_v)
                W("t", "sV", tps_free[tslot])
                tbase = 2048 + tslot * 256
                for qs in range(2):
                    E("t", lambda eng, o=PB[:, tbase + qs * 128:
                                            tbase + (qs + 1) * 128],
                      i=blnd_v[:, qs * 128:(qs + 1) * 128]:
                      eng.transpose(o, i, idn_v),
                      inc=("sT" if qs == 1 else None))
                tp_done = sch.cnt["sT"]
                tp_prev = tp_done
                W("v", "sT", tp_done)
                E("v", lambda eng, o=ao[:, h_abs * 1024 + b * 256:
                                        h_abs * 1024 + (b + 1) * 256],
                  i=PB[:, tbase:tbase + 256]:
                  eng.tensor_copy(o, i), inc="sV")
                tps_free[tslot] = sch.cnt["sV"]
    ao_done_v = sch.cnt["sV"]

    if "dbg_ao" in dbg_d:
        W("y", "sV", ao_done_v)
        ta_ = DMA("y", dbg_d["dbg_ao"], ao, "dT2")
        W("y", "dT2", ta_)

    abf_off = R2
    abf = vw(abf_off, 32768, BF16)
    asq = vw(RQ, 1024, BF16)
    evac_done = [0, 0]
    asq_free = 0
    for oc in range(NCH):
        slot = oc % 2
        wssem = "dWS0" if slot == 0 else "dWS1"
        W("g", "sT", ws_freed[slot])
        ws_t[slot] = DMA("g", vw(ws_off[slot], 4096, BF16), wo[oc], wssem)
        W("t", wssem, ws_t[slot])
        if oc == 0:
            W("t", "sV", ao_done_v)
        W("t", "sS", evac_done[slot])
        base = slot * 1024
        for nh in range(2):
            for ic in range(NCH):
                E("t", lambda eng, o=pv(base + nh * 512, 512),
                  l=vw(ws_off[slot] + ic * 256, 128, BF16),
                  r=ao[:, ic * 1024 + nh * 512: ic * 1024 + (nh + 1) * 512],
                  st=(ic == 0), sp=(ic == 31):
                  eng.matmul(o, l, r, start=st, stop=sp),
                  inc=("sT" if (nh == 1 and ic == 31) else None))
        ws_freed[slot] = sch.cnt["sT"]
        wo_done = sch.cnt["sT"]
        W("s", "sT", wo_done)
        W("s", "sT", asq_free)
        E("s", lambda eng, o=abf[:, oc * 1024:(oc + 1) * 1024],
          i=pv(base, 1024):
          eng.activation(out=o, in_=i, func=AF.Copy, bias=0.0, scale=1.0))
        E("s", lambda eng, i=pv(base, 1024):
          eng.activation(out=asq, in_=i, func=AF.Square, bias=0.0, scale=1.0),
          inc="sS")
        evac_done[slot] = sch.cnt["sS"]
        W("t", "sS", evac_done[slot])
        for th in range(2):
            E("t", lambda eng, o=pv(2048 + th * 512, 512, 0, 1),
              r=abf[:, oc * 1024 + th * 512: oc * 1024 + (th + 1) * 512],
              st=(oc == 0), sp=(oc == 31):
              eng.matmul(o, onesc_v, r, start=st, stop=sp))
        for th in range(2):
            E("t", lambda eng, o=pv(3072 + th * 512, 512, 0, 1),
              r=asq[:, th * 512:(th + 1) * 512], st=(oc == 0), sp=(oc == 31):
              eng.matmul(o, onesc_v, r, start=st, stop=sp),
              inc=("sT" if th == 1 else None))
        asq_free = sch.cnt["sT"]

    if "dbg_a" in dbg_d:
        W("y", "sS", max(evac_done))
        tb_ = DMA("y", dbg_d["dbg_a"], abf, "dT3")
        W("v", "dT3", tb_)

    def ln_rows():
        W("v", "sT", sch.cnt["sT"])
        E("v", lambda eng: eng.tensor_scalar(
            out=stat[:, 0:1024], in0=pv(2048, 1024, 0, 1),
            scalar1=1.0 / 4096, scalar2=None, op0=OP.mult), inc="sV")
        W("s", "sV", sch.cnt["sV"])
        E("s", lambda eng: eng.activation(
            out=stat[:, 1024:2048], in_=stat[:, 0:1024], func=AF.Square,
            bias=0.0, scale=1.0), inc="sS")
        W("v", "sS", sch.cnt["sS"])
        E("v", lambda eng: eng.scalar_tensor_tensor(
            out=stat[:, 1024:2048], in0=pv(3072, 1024, 0, 1),
            scalar=1.0 / 4096, in1=stat[:, 1024:2048],
            op0=OP.mult, op1=OP.subtract), inc="sV")
        W("s", "sV", sch.cnt["sV"])
        E("s", lambda eng: eng.activation(
            out=stat[:, 1024:2048], in_=stat[:, 1024:2048], func=AF.Sqrt,
            bias=eps_ap, scale=1.0), inc="sS")
        W("v", "sS", sch.cnt["sS"])
        E("v", lambda eng: eng.reciprocal(stat[:, 1024:2048],
                                          stat[:, 1024:2048]))
        E("v", lambda eng: eng.drain())
        E("v", lambda eng: eng.tensor_tensor(
            out=stat[:, 0:1024], in0=stat[:, 0:1024], in1=stat[:, 1024:2048],
            op=OP.mult), inc="sV")
        return sch.cnt["sV"]

    def bcast_rows(rstd_off, mr_off):
        W("t", "sV", sch.cnt["sV"])
        for k in range(2):
            E("t", lambda eng, o=pv(rstd_off + k * 512, 512),
              r=stat[:, 1024 + k * 512:1024 + (k + 1) * 512]:
              eng.matmul(o, ones1_v, r, start=True, stop=True))
        for k in range(2):
            E("t", lambda eng, o=pv(mr_off + k * 512, 512),
              r=stat[:, k * 512:(k + 1) * 512]:
              eng.matmul(o, ones1_v, r, start=True, stop=True),
              inc=("sT" if k == 1 else None))
        return sch.cnt["sT"]

    wo_all = max(ws_freed)
    asq_last = asq_free
    ln_rows()
    bcD = bcast_rows(0, 1024)

    xrl_off = [R1, R1 + 4096]
    x2o_off = [R1 + 8192, R1 + 12288]
    tmpd_off = R1 + 16384
    x2bf_off = R1 + 20480
    x2sq_off = R1 + 22528
    rl_off = [R1 + 24576 + i * 4096 for i in range(4)]
    xrl_t = [0, 0]
    x2_done = [0, 0]
    x2s_done = [0, 0]
    x2stats_free = 0
    spill_t = [None] * 32
    for oc in range(NCH):
        slot = oc % 2
        W("y", "sT", wo_all)
        if oc >= 2:
            W("y", "sV", x2_done[slot])
        xrsem = "dXR0" if slot == 0 else "dXR1"
        for tcq in range(4):
            xrl_t[slot] = DMA(
                "y", vw(xrl_off[slot] + tcq * 1024, 256, F32),
                xh[tcq + 1, :, oc, :], xrsem)
        W("v", xrsem, xrl_t[slot])
        W("v", "sT", bcD)
        if oc >= 2:
            W("v", "dSP0" if slot == 0 else "dSP1", spill_t[oc - 2])
            W("v", "sS", x2s_done[slot])
        tmpd = vw(tmpd_off, 1024, F32)
        x2o = vw(x2o_off[slot], 1024, F32)
        E("v", lambda eng: eng.drain())
        E("v", lambda eng, o=tmpd, i=abf[:, oc * 1024:(oc + 1) * 1024]:
          eng.tensor_tensor(out=o, in0=i, in1=pv(0, 1024), op=OP.mult))
        E("v", lambda eng: eng.drain())
        E("v", lambda eng, o=tmpd:
          eng.tensor_tensor(out=o, in0=o, in1=pv(1024, 1024),
                            op=OP.subtract))
        E("v", lambda eng: eng.drain())
        E("v", lambda eng, o=x2o, t_=tmpd, x_=vw(xrl_off[slot], 1024, F32):
          eng.tensor_tensor(out=o, in0=t_, in1=x_, op=OP.add), inc="sV")
        x2_done[slot] = sch.cnt["sV"]
        W("y", "sV", x2_done[slot])
        spill_t[oc] = DMA("y", x2sp[oc], x2o, "dSP0" if slot == 0 else "dSP1")
        W("s", "sV", x2_done[slot])
        W("s", "sT", x2stats_free)
        E("s", lambda eng, o=vw(x2bf_off, 1024, BF16), i=x2o:
          eng.activation(out=o, in_=i, func=AF.Copy, bias=0.0, scale=1.0))
        E("s", lambda eng, o=vw(x2sq_off, 1024, BF16), i=x2o:
          eng.activation(out=o, in_=i, func=AF.Square, bias=0.0, scale=1.0),
          inc="sS")
        x2s_done[slot] = sch.cnt["sS"]
        W("t", "sS", x2s_done[slot])
        for th in range(2):
            E("t", lambda eng, o=pv(2048 + th * 512, 512, 0, 1),
              r=vw(x2bf_off + th * 1024, 512, BF16), st=(oc == 0),
              sp=(oc == 31):
              eng.matmul(o, onesc_v, r, start=st, stop=sp))
        for th in range(2):
            E("t", lambda eng, o=pv(3072 + th * 512, 512, 0, 1),
              r=vw(x2sq_off + th * 1024, 512, BF16), st=(oc == 0),
              sp=(oc == 31):
              eng.matmul(o, onesc_v, r, start=st, stop=sp),
              inc=("sT" if th == 1 else None))
        x2stats_free = sch.cnt["sT"]

    ln_rows()
    bc2 = bcast_rows(2048, 3072)

    h2_off = R2
    h2 = vw(h2_off, 32768, BF16)
    h2_done = [0] * 32
    for oc in range(NCH):
        rlsem = "dRL%d" % (oc % 4)
        if oc >= 4:
            W("y", "sV", h2_done[oc - 4])
        W("y", "dSP0" if oc % 2 == 0 else "dSP1", spill_t[oc])
        rlt = DMA("y", vw(rl_off[oc % 4], 1024, F32), x2sp[oc], rlsem)
        W("v", rlsem, rlt)
        W("v", "sT", bc2)
        tmpd = vw(tmpd_off, 1024, F32)
        E("v", lambda eng: eng.drain())
        E("v", lambda eng, o=tmpd, x_=vw(rl_off[oc % 4], 1024, F32):
          eng.tensor_tensor(out=o, in0=x_, in1=pv(2048, 1024), op=OP.mult))
        E("v", lambda eng: eng.drain())
        E("v", lambda eng, o=h2[:, oc * 1024:(oc + 1) * 1024], t_=tmpd:
          eng.tensor_tensor(out=o, in0=t_, in1=pv(3072, 1024),
                            op=OP.subtract), inc="sV")
        h2_done[oc] = sch.cnt["sV"]

    if "dbg_h2" in dbg_d:
        W("y", "sV", h2_done[31])
        tc_ = DMA("y", dbg_d["dbg_h2"], h2, "dT4")
        W("v", "dT4", tc_)

    ntneg = vw(RQ, 4096, BF16)
    n4_off = RQ + 8192
    w1_off = [O["wk"], O["wv"]]
    hsb = vw(O["atA"], 1024, BF16)
    fsq_off = O["atB"]
    fbf_off = R1
    fbf = vw(fbf_off, 32768, BF16)

    W("g", "sV", ao_done_v)
    W("g", "sT", asq_last)
    nt_t = DMA("g", ntneg, nt, "dNT")
    W("v", "dNT", nt_t)
    E("v", lambda eng: eng.tensor_scalar(
        out=ntneg, in0=ntneg, scalar1=-1.0, scalar2=None, op0=OP.mult),
        inc="sV")
    ntneg_done = sch.cnt["sV"]

    for th in range(2):
        for ic in range(NCH):
            W("t", "sV", h2_done[ic])
            E("t", lambda eng, o=pv(th * 512, 512),
              l=epad_v[:, 124 - ic * 4:252 - ic * 4],
              r=h2[:, ic * 1024 + th * 512: ic * 1024 + (th + 1) * 512],
              st=(ic == 0), sp=(ic == 31):
              eng.matmul(o, l, r, start=st, stop=sp),
              inc=("sT" if (th == 1 and ic == 31) else None))
    hmm_done = sch.cnt["sT"]
    W("s", "sT", hmm_done)
    E("s", lambda eng: eng.activation(
        out=hsb, in_=pv(0, 1024), func=AF.Copy, bias=0.0, scale=3.5),
        inc="sS")
    h75_done = sch.cnt["sS"]

    us_t = [0, 0]
    unpack_done = [0, 0]
    n4_free = 0
    w1_done = {}
    fdone_oc = [0] * 32
    fevac_done = 0
    fsq_free = 0
    for oc in range(NCH):
        slot = oc % 2
        wssem = "dWS0" if slot == 0 else "dWS1"
        W("g", "sT", ws_freed[slot])
        W("g", "sV", w1_done.get((oc - 2, 7), 0))
        us_t[slot] = DMA("g", vw(ws_off[slot], 4096, U16), uq[oc], wssem)
        W("g", "sT", n4_free)
        n4t = DMA("g", vw(n4_off, 4096, BF16, 0, 4), n4[oc], "dN4")
        W("v", wssem, us_t[slot])
        if oc >= 2:
            W("v", "sT", fdone_oc[oc - 2])
        vq = vw(ws_off[slot], 4096, U16)
        vbf = vw(ws_off[slot], 4096, BF16)
        E("v", lambda eng, i=vq:
          eng.tensor_scalar(out=i, in0=i, scalar1=mask32_v, scalar2=None,
                            op0=OP.bitwise_and))
        E("v", lambda eng: eng.drain())
        E("v", lambda eng, i=vq, o=vbf:
          eng.tensor_scalar(out=o, in0=i, scalar1=inv2s_v, scalar2=None,
                            op0=OP.mult),
          inc="sV")
        E("v", lambda eng: eng.drain())
        unpack_done[slot] = sch.cnt["sV"]
        for icg in range(8):
            nb = 1024 + (icg % 2) * 512
            W("t", "dN4", n4t)
            W("t", "sV", w1_done.get((oc, icg - 2),
                                     w1_done.get((oc - 1, icg + 6), 0)))
            E("t", lambda eng, o=pv(nb, 512),
              r=vw(n4_off + icg * 1024, 512, BF16, 0, 4):
              eng.matmul(o, e4_v, r, start=True, stop=True), inc="sT")
            nexp_done = sch.cnt["sT"]
            if icg == 7:
                n4_free = nexp_done
            W("v", "sT", nexp_done)
            E("v", lambda eng, o=vw(w1_off[slot] + icg * 1024, 512, BF16),
              i=vbf[:, icg * 512:(icg + 1) * 512], n_=pv(nb, 512):
              eng.tensor_tensor(out=o, in0=i, in1=n_, op=OP.mult), inc="sV")
            w1_done[(oc, icg)] = sch.cnt["sV"]
        W("t", "sS", fevac_done)
        W("t", "sS", h75_done)
        W("t", "sV", ntneg_done)
        for th in range(2):
            for ic in range(NCH):
                W("t", "sV", w1_done[(oc, ic // 4)])
                E("t", lambda eng, o=pv(th * 512, 512),
                  l=vw(w1_off[slot] + ic * 256, 128, BF16),
                  r=h2[:, ic * 1024 + th * 512: ic * 1024 + (th + 1) * 512],
                  st=(ic == 0):
                  eng.matmul(o, l, r, start=st, stop=False))
            E("t", lambda eng, o=pv(th * 512, 512),
              l=ntneg[:, oc * 128:(oc + 1) * 128],
              r=hsb[:, th * 512:(th + 1) * 512]:
              eng.matmul(o, l, r, start=False, stop=True),
              inc=("sT" if th == 1 else None))
        fdone_oc[oc] = sch.cnt["sT"]
        W("s", "dSP0", sch.cnt["dSP0"])
        W("s", "dSP1", sch.cnt["dSP1"])
        W("s", "sT", fdone_oc[oc])
        W("s", "sT", fsq_free)
        E("s", lambda eng, o=fbf[:, oc * 1024:(oc + 1) * 1024],
          i=pv(0, 1024), b_=fbias_v[:, oc:oc + 1]:
          eng.activation(out=o, in_=i, func=AF.Identity, bias=b_,
                         scale=2.0 / 7))
        E("s", lambda eng, o=vw(fsq_off, 1024, BF16), i=pv(0, 1024),
          b_=fbias_v[:, oc:oc + 1]:
          eng.activation(out=o, in_=i, func=AF.Square, bias=b_,
                         scale=2.0 / 7), inc="sS")
        fevac_done = sch.cnt["sS"]
        W("t", "sS", fevac_done)
        for th in range(2):
            E("t", lambda eng, o=pv(2048 + th * 512, 512, 0, 1),
              r=fbf[:, oc * 1024 + th * 512: oc * 1024 + (th + 1) * 512],
              st=(oc == 0), sp=(oc == 31):
              eng.matmul(o, onesc_v, r, start=st, stop=sp))
        for th in range(2):
            E("t", lambda eng, o=pv(3072 + th * 512, 512, 0, 1),
              r=vw(fsq_off + th * 1024, 512, BF16), st=(oc == 0),
              sp=(oc == 31):
              eng.matmul(o, onesc_v, r, start=st, stop=sp),
              inc=("sT" if th == 1 else None))
        fsq_free = sch.cnt["sT"]
    f_all_done = fdone_oc[31]

    if "dbg_f" in dbg_d:
        W("y", "sS", fevac_done)
        DMA("y", dbg_d["dbg_f"], fbf, "dT7")

    ln_rows()
    bc3 = bcast_rows(0, 1024)

    xr2_off = [R3, R3 + 4096]
    ob_off = [R3 + 8192, R3 + 12288]
    tmpf_off = R2
    out_t = [0, 0]
    of_done = [0, 0]
    for oc in range(NCH):
        slot = oc % 2
        frsem = "dFR0" if slot == 0 else "dFR1"
        W("y", "sV", w1_done[(30 + slot, 7)])
        if oc >= 2:
            W("y", "sV", of_done[slot])
        W("y", "dSP0", sch.cnt["dSP0"])
        W("y", "dSP1", sch.cnt["dSP1"])
        rlt = DMA("y", vw(xr2_off[slot], 1024, F32), x2sp[oc], frsem)
        W("v", frsem, rlt)
        W("v", "sT", bc3)
        if oc >= 2:
            W("v", "dOUT0" if slot == 0 else "dOUT1", out_t[slot])
        tmpf = vw(tmpf_off, 1024, F32)
        E("v", lambda eng: eng.drain())
        E("v", lambda eng, o=tmpf, i=fbf[:, oc * 1024:(oc + 1) * 1024]:
          eng.tensor_tensor(out=o, in0=i, in1=pv(0, 1024), op=OP.mult))
        E("v", lambda eng: eng.drain())
        E("v", lambda eng, o=tmpf:
          eng.tensor_tensor(out=o, in0=o, in1=pv(1024, 1024),
                            op=OP.subtract))
        E("v", lambda eng: eng.drain())
        E("v", lambda eng, o=vw(ob_off[slot], 1024, F32), t_=tmpf,
          x_=vw(xr2_off[slot], 1024, F32):
          eng.tensor_tensor(out=o, in0=t_, in1=x_, op=OP.add), inc="sV")
        of_done[slot] = sch.cnt["sV"]
        W("y", "sV", of_done[slot])
        out_t[slot] = DMA("y", out_d[oc], vw(ob_off[slot], 1024, F32), "dOUT0" if slot == 0 else "dOUT1")

    final_out0 = sch.cnt["dOUT0"]
    final_out1 = sch.cnt["dOUT1"]
    final_dbg = 0
    E("y", lambda eng: eng.wait_ge(sch.semh["dOUT0"], final_out0))
    E("y", lambda eng: eng.wait_ge(sch.semh["dOUT1"], final_out1))
    for i_ in range(8):
        if sch.cnt["dT%d" % i_]:
            E("y", lambda eng, s_="dT%d" % i_, v_=sch.cnt["dT%d" % i_]:
              eng.wait_ge(sch.semh[s_], v_))

    with nc.Block() as blk:
        @blk.sync
        def _(eng):
            sch.run("y", eng)

        @blk.gpsimd
        def _(eng):
            sch.run("g", eng)

        @blk.tensor
        def _(eng):
            sch.run("t", eng)

        @blk.vector
        def _(eng):
            sch.run("v", eng)

        @blk.scalar
        def _(eng):
            sch.run("s", eng)

    ctx.close()
    return nc, list(dbg_d.keys())


def kernel(L0_attn_x, L0_attn_Wq, L0_attn_Wk, L0_attn_Wv, L0_attn_Wo,
           L0_ffn_weight_q3, L0_ffn_weight_norm, L0_ffn_bias):
    x = np.asarray(L0_attn_x, dtype=np.float32)
    shared = _prep_shared(np.asarray(L0_attn_Wq, np.float32),
                          np.asarray(L0_attn_Wk, np.float32),
                          np.asarray(L0_attn_Wv, np.float32),
                          np.asarray(L0_attn_Wo, np.float32),
                          np.asarray(L0_ffn_weight_q3),
                          np.asarray(L0_ffn_weight_norm, np.float32),
                          np.asarray(L0_ffn_bias, np.float32))
    nc, taps = build_nc(tuple(_dbg_taps))
    in_maps = []
    for c in range(NCORES):
        m = dict(shared)
        m.update(_prep_core(x, c))
        in_maps.append(m)
    res = run_bass_kernel_spmd(nc, in_maps, core_ids=list(range(NCORES)))
    out = np.empty((B, S, D), dtype=np.float32)
    for c in range(NCORES):
        b = c // 4
        T0 = 1024 * (c % 4)
        oc = res.results[c]["out"]
        out[b, T0:T0 + 1024, :] = np.ascontiguousarray(
            oc.transpose(2, 0, 1)).reshape(1024, 4096)
    if taps:
        kernel.dbg = [{t: res.results[c][t] for t in taps}
                      for c in range(NCORES)]
    return out


# version 04
# speedup vs baseline: 1.2208x; 1.2208x over previous
import math
import contextlib

import numpy as np
import ml_dtypes

import bass_rust
import concourse.bass as bass
import concourse.mybir as mybir
from concourse.bass_utils import run_bass_kernel_spmd

F32 = mybir.dt.float32
BF16 = mybir.dt.bfloat16
U16 = mybir.dt.uint16
I32 = mybir.dt.int32
U8 = mybir.dt.uint8
OP = mybir.AluOpType
AF = mybir.ActivationFunctionType
BF = ml_dtypes.bfloat16

B, S, D = 2, 4096, 4096
H, DH = 32, 128
SCALE = 1.0 / math.sqrt(DH)
NCORES = 8
TOK = 1024
HTOK = 1536
NCH = 32
EPS = 1e-5

HEAP_BYTES = 212832


def _al(x, a=32):
    return (x + a - 1) // a * a


class _Off:
    def __init__(self, base=0):
        self.cur = base
        self.map = {}

    def add(self, name, nbytes):
        self.map[name] = self.cur
        self.cur = _al(self.cur + nbytes)
        return self.map[name]


_P = _Off()
_P.add("cos", 1536 * 2)
_P.add("sin", 1536 * 2)
_P.add("wk", 4096 * 2)
_P.add("wv", 4096 * 2)
_P.add("kr", 1536 * 2)
_P.add("vext", 1548 * 2)
_P.add("rsign", 128 * 2)
_P.add("idn", 128 * 2)
_P.add("epad", 256 * 2)
_P.add("e4", 128 * 2)
_P.add("ones1", 128 * 4)
_P.add("onesc", 32)
_P.add("alham", 8 * 4)
_P.add("alpha", 8 * 4)
_P.add("maskm", 256 * 2)
_P.add("onesr", 256 * 2)
_P.add("fbias", 32 * 4)
_P.add("consts", 8 * 4)
_P.add("shiftv", 64)
_P.add("stat", 2048 * 4)
_P.add("scr", 16 * 4)
_P.add("kb", 256 * 2)
_P.add("blnd", 512 * 2)
_P.add("atA", 1024 * 2)
_P.add("atB", 1024 * 2)
_P.add("qb", 1024 * 2)
PERSIST_END = _P.cur

R3 = _al(PERSIST_END)
R3_SZ = 2 * 8192
R1 = _al(R3 + R3_SZ)
R1_SZ = 65536
R2 = _al(R1 + R1_SZ)
R2_SZ = 65536
RQ = _al(R2 + R2_SZ)
RQ_SZ = 16384
assert RQ + RQ_SZ <= HEAP_BYTES, (RQ + RQ_SZ, HEAP_BYTES)

_dbg_taps = []


def _rope_cache():
    inv_freq = 1.0 / 10000.0 ** (np.arange(0, DH, 2, dtype=np.float32) / DH)
    t = np.arange(S, dtype=np.float32)
    f = np.outer(t, inv_freq)
    cos = np.zeros((S, DH), dtype=np.float32)
    sin = np.zeros((S, DH), dtype=np.float32)
    cos[:, 0::2] = np.cos(f)
    cos[:, 1::2] = np.cos(f)
    sin[:, 0::2] = np.sin(f)
    sin[:, 1::2] = np.sin(f)
    return cos, sin


def _prep_shared(Wq, Wk, Wv, Wo, wq3, wnorm, fbias):
    d = {}
    d["wq"] = np.ascontiguousarray(
        Wq.reshape(32, 128, 32, 128).transpose(0, 3, 2, 1))
    d["wo"] = np.ascontiguousarray(
        Wo.reshape(32, 128, 32, 128).transpose(0, 3, 2, 1))
    d["wk"] = np.ascontiguousarray(Wk.reshape(128, 32, 128).transpose(2, 1, 0))
    d["wv"] = np.ascontiguousarray(Wv.reshape(128, 32, 128).transpose(2, 1, 0))

    pk = np.ascontiguousarray(wq3).view(np.uint8)
    pk13 = np.concatenate(
        [pk, np.zeros((pk.shape[0], 1), np.uint8)], axis=1)
    j = np.arange(32)
    bidx = (3 * j) // 8
    lo = pk13[:, bidx]
    hi = pk13[:, bidx + 1]
    b2 = np.stack([lo, hi], axis=-1)
    u16 = np.ascontiguousarray(b2).reshape(-1, 64).view(np.uint16)
    u16oi = u16.reshape(4096, 4096)
    d["uq"] = np.ascontiguousarray(
        u16oi.reshape(32, 128, 32, 128).transpose(0, 3, 2, 1))

    n_og = wnorm.reshape(4096, 128)
    d["n4"] = np.ascontiguousarray(
        n_og.reshape(32, 128, 32, 4).transpose(0, 3, 2, 1))
    d["nt"] = np.ascontiguousarray(n_og.T)
    d["fbias"] = np.ascontiguousarray(fbias.reshape(32, 128).T)

    e4 = np.zeros((4, 128), dtype=BF)
    for g in range(4):
        e4[g, g * 32:(g + 1) * 32] = 1
    d["e4"] = e4
    epad = np.zeros((128, 256), dtype=BF)
    p = np.arange(128)
    epad[p, 124 + p // 32] = 1
    d["epad"] = epad
    rsign = np.zeros((128, 128), dtype=np.float32)
    m = np.arange(64)
    rsign[m + 64, m] = -1.0
    rsign[m, m + 64] = 1.0
    d["rsign"] = rsign.astype(BF)
    d["idn"] = np.eye(128, dtype=BF)
    d["ones1"] = np.ones((1, 128), np.float32)
    d["onesc"] = np.ones((128, 1), dtype=BF)
    d["onesr"] = np.ones((1, 256), dtype=BF)
    consts = np.zeros((128, 8), np.float32)
    consts[:, 0] = EPS
    d["consts"] = consts
    s_ = (3 * (np.arange(128) % 32)) % 8
    d["mask32"] = (7 << s_).astype(np.int32).reshape(128, 1)
    d["inv2s"] = (0.5 ** s_).astype(np.float32).reshape(128, 1)
    return d


def _prep_core(x, core):
    b = core // 4
    T0 = 1024 * (core % 4)
    lo = T0 - 256
    xh = np.zeros((HTOK, D), dtype=np.float32)
    s0 = max(lo, 0)
    s1 = min(T0 + 1280, S)
    xh[s0 - lo:s1 - lo] = x[b, s0:s1]
    xh_arr = np.ascontiguousarray(
        xh.reshape(6, 256, 32, 128).transpose(0, 3, 2, 1))

    cos, sin = _rope_cache()
    pos = np.clip(np.arange(lo, lo + HTOK), 0, S - 1)
    cosT = np.ascontiguousarray(cos[pos].T)
    sinT = np.ascontiguousarray(sin[pos].T)

    alpha = np.zeros((128, 8), np.float32)
    alham = np.zeros((128, 8), np.float32)
    for bl in range(4):
        gb = T0 // 256 + bl
        if gb == 0:
            a = np.ones(256, np.float32)
        else:
            a = np.linspace(0.0, 1.0, 256).astype(np.float32)
        for qs in range(2):
            alpha[:, 2 * bl + qs] = a[qs * 128:(qs + 1) * 128]
            alham[:, 2 * bl + qs] = (1.0 - a)[qs * 128:(qs + 1) * 128]

    maskm = np.zeros((1, 256), dtype=BF)
    if core % 4 == 3:
        maskm[:] = np.float32(-30.0 / SCALE).astype(BF)
    return dict(xh=xh_arr, cosT=cosT, sinT=sinT, alpha=alpha, alham=alham,
                maskm=maskm)


class Sched:
    ENGINES = ("y", "g", "t", "v", "s")

    def __init__(self):
        self.q = {e: [] for e in self.ENGINES}
        self.cnt = {}
        self.hw = {}
        self.semh = {}

    def new_sem(self, name, handle):
        self.semh[name] = handle
        self.cnt[name] = 0
        for e in self.ENGINES:
            self.hw[(e, name)] = 0

    def wait(self, e, sem, val):
        if val is None or val <= 0:
            return
        if self.hw[(e, sem)] >= val:
            return
        self.hw[(e, sem)] = val
        h = self.semh[sem]
        self.q[e].append(lambda eng, h=h, v=val: eng.wait_ge(h, v))

    def emit(self, e, fn, inc=None, k=1):
        if inc is None:
            self.q[e].append(fn)
            return None
        self.cnt[inc] += k
        tgt = self.cnt[inc]
        h = self.semh[inc]

        def wrapped(eng, fn=fn, h=h, k=k):
            fn(eng).then_inc(h, k)
        self.q[e].append(wrapped)
        return tgt

    def dma(self, e, out_ap, in_ap, sem):
        return self.emit(
            e, lambda eng, o=out_ap, i=in_ap: eng.dma_start(out=o, in_=i),
            inc=sem, k=16)

    def run(self, e, eng):
        for fn in self.q[e]:
            fn(eng)


def _bc(ap, reps, inner):
    pat = [list(p) for p in ap.ap]
    assert len(pat) == 2, pat
    return bass_rust.AP(ap.tensor, ap.offset, [pat[0], [0, reps], pat[1]])


def _strided(ap, elem_off, dims):
    pat = [list(p) for p in ap.ap]
    return bass_rust.AP(ap.tensor, ap.offset + elem_off,
                        [pat[0]] + [list(d) for d in dims])


def build_nc(debug_taps=()):
    nc = bass.Bass("TRN2", target_bir_lowering=False)

    di = {}

    def inp(name, shape, dt):
        di[name] = nc.dram_tensor(name, list(shape), dt,
                                  kind="ExternalInput").ap()
        return di[name]

    xh = inp("xh", (6, 128, 32, 256), F32)
    wq = inp("wq", (32, 128, 32, 128), F32)
    wo = inp("wo", (32, 128, 32, 128), F32)
    wk = inp("wk", (128, 32, 128), F32)
    wv = inp("wv", (128, 32, 128), F32)
    uq = inp("uq", (32, 128, 32, 128), U16)
    n4 = inp("n4", (32, 4, 32, 128), F32)
    nt = inp("nt", (128, 4096), F32)
    fbias_d = inp("fbias", (128, 32), F32)
    cosT = inp("cosT", (128, 1536), F32)
    sinT = inp("sinT", (128, 1536), F32)
    alpha = inp("alpha", (128, 8), F32)
    alham = inp("alham", (128, 8), F32)
    maskm = inp("maskm", (1, 256), BF16)
    rsign = inp("rsign", (128, 128), BF16)
    idn = inp("idn", (128, 128), BF16)
    epad = inp("epad", (128, 256), BF16)
    e4 = inp("e4", (4, 128), BF16)
    ones1 = inp("ones1", (1, 128), F32)
    onesc = inp("onesc", (128, 1), BF16)
    onesr = inp("onesr", (1, 256), BF16)
    consts = inp("consts", (128, 8), F32)
    mask32 = inp("mask32", (128, 1), I32)
    inv2s = inp("inv2s", (128, 1), F32)

    out_d = nc.dram_tensor("out", [32, 128, 1024], F32,
                           kind="ExternalOutput").ap()
    x2sp = nc.dram_tensor("x2sp", [32, 128, 1024], F32, kind="Internal").ap()
    dbg_d = {}
    for tap, shape, dt in (
        ("dbg_h", [128, 32768], BF16), ("dbg_kr", [128, 1536], BF16),
        ("dbg_vext", [128, 1548], BF16), ("dbg_qr", [128, 4096], BF16),
        ("dbg_ao", [128, 32768], BF16), ("dbg_a", [128, 32768], BF16),
        ("dbg_h2", [128, 32768], BF16), ("dbg_f", [128, 32768], BF16),
    ):
        if tap in debug_taps:
            dbg_d[tap] = nc.dram_tensor(tap, shape, dt,
                                        kind="ExternalOutput").ap()

    ctx = contextlib.ExitStack()
    heap = ctx.enter_context(nc.sbuf_tensor("heap", [128, HEAP_BYTES], U8))
    P = ctx.enter_context(nc.psum_tensor("psum", [128, 4096], F32))
    PB = P.bitcast(BF16)

    sch = Sched()
    for s in ("sT", "sV", "sS", "dCN", "dW0", "dXB0", "dXB1", "dWS0", "dWS1",
              "dN4", "dNT", "dXR0", "dXR1", "dXR2", "dXR3", "dSP0", "dSP1", "dRL0", "dRL1", "dRL2", "dRL3",
              "dFR0", "dFR1", "dOUT0", "dOUT1", "dDBG"):
        sch.new_sem(s, ctx.enter_context(nc.semaphore(s)))
    for i_ in range(8):
        sch.new_sem("dT%d" % i_, ctx.enter_context(nc.semaphore("dT%d" % i_)))

    def vw(off, n, dt, p0=0, p1=128):
        sz = mybir.dt.size(dt)
        return heap[p0:p1, off:off + n * sz].bitcast(dt)

    def pv(elem_off, n, p0=0, p1=128):
        return P[p0:p1, elem_off:elem_off + n]

    O = _P.map
    W = sch.wait
    E = sch.emit
    DMA = sch.dma

    cos_v = vw(O["cos"], 1536, BF16)
    sin_v = vw(O["sin"], 1536, BF16)
    wk_v = vw(O["wk"], 4096, BF16)
    wv_v = vw(O["wv"], 4096, BF16)
    kr_v = vw(O["kr"], 1536, BF16)
    vext_v = vw(O["vext"], 1548, BF16)
    rsign_v = vw(O["rsign"], 128, BF16)
    idn_v = vw(O["idn"], 128, BF16)
    epad_v = vw(O["epad"], 256, BF16)
    e4_v = vw(O["e4"], 128, BF16, 0, 4)
    ones1_v = vw(O["ones1"], 128, F32, 0, 1)
    onesc_v = vw(O["onesc"], 1, BF16)
    alham_v = vw(O["alham"], 8, F32)
    alpha_v = vw(O["alpha"], 8, F32)
    maskm_v = vw(O["maskm"], 256, BF16, 0, 1)
    onesr_v = vw(O["onesr"], 256, BF16, 0, 1)
    fbias_v = vw(O["fbias"], 32, F32)
    consts_v = vw(O["consts"], 8, F32)
    mask32_v = vw(O["shiftv"], 1, I32)
    inv2s_v = vw(O["shiftv"] + 32, 1, F32)
    eps_ap = vw(O["consts"], 8, F32, 0, 1)[:, 0:1]
    zero_ap = consts_v[:, 1:2]
    stat = vw(O["stat"], 2048, F32, 0, 1)
    scr = vw(O["scr"], 16, F32)
    kb_v = vw(O["kb"], 256, BF16)
    blnd_v = vw(O["blnd"], 512, BF16)
    atA_v = vw(O["atA"], 1024, BF16)
    atB_v = vw(O["atB"], 1024, BF16)
    qb_v = vw(O["qb"], 1024, BF16)

    for name, v_ in (("rsign", rsign_v),
                     ("idn", idn_v), ("epad", epad_v), ("e4", e4_v),
                     ("ones1", ones1_v), ("onesc", onesc_v),
                     ("alham", alham_v), ("alpha", alpha_v),
                     ("maskm", maskm_v), ("onesr", onesr_v),
                     ("fbias", fbias_v),
                     ("consts", consts_v), ("mask32", mask32_v),
                     ("inv2s", inv2s_v)):
        DMA("y", v_, di[name], "dCN")
    c_consts = sch.cnt["dCN"]
    DMA("g", cos_v, di["cosT"], "dW0")
    DMA("g", sin_v, di["sinT"], "dW0")
    DMA("g", wk_v, wk, "dW0")
    DMA("g", wv_v, wv, "dW0")
    w0_all = sch.cnt["dW0"]
    for kc in range(12):
        E("v", lambda eng, a=vext_v[:, kc * 129 + 128: kc * 129 + 129]:
          eng.memset(a, 1.0), inc=("sV" if kc == 11 else None))

    xb_off = [R1, R1 + 16384]
    xsq_off = R1 + 32768
    htmp_off = R1 + 49152
    hown_off = R2
    hown = vw(hown_off, 32768, BF16)

    xb_t = [None] * 6
    norm_done = [0] * 6
    kv_done = [0] * 6
    rope_done = [0] * 6
    mvar_done = [0] * 6
    sumsq_done = [0] * 6

    for e_ in ("t", "v", "s"):
        W(e_, "dCN", c_consts)
    for tc in range(6):
        xb = vw(xb_off[tc % 2], 8192, BF16)
        xsq = vw(xsq_off, 8192, BF16)
        xbsem = "dXB0" if tc % 2 == 0 else "dXB1"
        if tc >= 2:
            W("g", "sV", norm_done[tc - 2])
        xb_t[tc] = DMA("g", xb, xh[tc], xbsem)
        W("s", xbsem, xb_t[tc])
        if tc >= 1:
            W("s", "sT", sumsq_done[tc - 1])
        E("s", lambda eng, o=xsq, i=xb: eng.activation(
            out=o, in_=i, func=AF.Square, bias=0.0, scale=1.0), inc="sS")
        sq_ready = sch.cnt["sS"]
        W("t", xbsem, xb_t[tc])
        if tc >= 1:
            W("t", "sV", mvar_done[tc - 1])
        for ic in range(NCH):
            E("t", lambda eng, o=pv(0, 256, 0, 1), l=onesc_v,
              r=xb[:, ic * 256:(ic + 1) * 256], st=(ic == 0), sp=(ic == 31):
              eng.matmul(o, l, r, start=st, stop=sp))
        W("t", "sS", sq_ready)
        for ic in range(NCH):
            E("t", lambda eng, o=pv(512, 256, 0, 1), l=onesc_v,
              r=xsq[:, ic * 256:(ic + 1) * 256], st=(ic == 0), sp=(ic == 31):
              eng.matmul(o, l, r, start=st, stop=sp),
              inc=("sT" if ic == 31 else None))
        stats_done = sch.cnt["sT"]
        sumsq_done[tc] = stats_done
        W("v", "sT", stats_done)
        E("v", lambda eng: eng.tensor_scalar(
            out=stat[:, 0:256], in0=pv(0, 256, 0, 1), scalar1=1.0 / 4096,
            scalar2=None, op0=OP.mult), inc="sV")
        W("s", "sV", sch.cnt["sV"])
        E("s", lambda eng: eng.activation(
            out=stat[:, 256:512], in_=stat[:, 0:256], func=AF.Square,
            bias=0.0, scale=1.0), inc="sS")
        W("v", "sS", sch.cnt["sS"])
        E("v", lambda eng: eng.scalar_tensor_tensor(
            out=stat[:, 512:768], in0=pv(512, 256, 0, 1), scalar=1.0 / 4096,
            in1=stat[:, 256:512], op0=OP.mult, op1=OP.subtract), inc="sV")
        W("s", "sV", sch.cnt["sV"])
        E("s", lambda eng: eng.activation(
            out=stat[:, 512:768], in_=stat[:, 512:768], func=AF.Sqrt,
            bias=eps_ap, scale=1.0), inc="sS")
        W("v", "sS", sch.cnt["sS"])
        E("v", lambda eng: eng.reciprocal(stat[:, 512:768], stat[:, 512:768]))
        E("v", lambda eng: eng.drain())
        E("v", lambda eng: eng.tensor_tensor(
            out=stat[:, 0:256], in0=stat[:, 0:256], in1=stat[:, 512:768],
            op=OP.mult), inc="sV")
        mvar_done[tc] = sch.cnt["sV"]
        bb = 1024 if tc % 2 == 0 else 1536
        W("t", "sV", mvar_done[tc])
        if tc >= 2:
            W("t", "sV", norm_done[tc - 2])
        E("t", lambda eng, o=pv(bb, 256), r=stat[:, 512:768]:
          eng.matmul(o, ones1_v, r, start=True, stop=True))
        E("t", lambda eng, o=pv(bb + 256, 256), r=stat[:, 0:256]:
          eng.matmul(o, ones1_v, r, start=True, stop=True), inc="sT")
        bcast_done = sch.cnt["sT"]
        W("v", "sT", bcast_done)
        if tc in (1, 2, 3, 4):
            hdst = _strided(hown, (tc - 1) * 256, [[1024, 32], [1, 256]])
        else:
            hch = vw(htmp_off, 8192, BF16)
            hdst = hch.rearrange("p (c t) -> p c t", c=32)
        xb3 = xb.rearrange("p (c t) -> p c t", c=32)
        E("v", lambda eng, o=hdst, i=xb3, b_=_bc(pv(bb, 256), 32, 256):
          eng.tensor_tensor(out=o, in0=i, in1=b_, op=OP.mult))
        E("v", lambda eng: eng.drain())
        E("v", lambda eng, o=hdst, b_=_bc(pv(bb + 256, 256), 32, 256):
          eng.tensor_tensor(out=o, in0=o, in1=b_, op=OP.subtract), inc="sV")
        norm_done[tc] = sch.cnt["sV"]

        def hsl(ic, lo=0, n=256, tc=tc):
            if tc in (1, 2, 3, 4):
                return vw(hown_off + ic * 2048 + (tc - 1) * 512 + lo * 2,
                          n, BF16)
            return vw(htmp_off + ic * 512 + lo * 2, n, BF16)

        W("t", "sV", norm_done[tc])
        if tc >= 1:
            W("t", "sV", rope_done[tc - 1])
            W("t", "sS", kv_done[tc - 1])
        W("t", "dW0", w0_all)
        for ic in range(NCH):
            E("t", lambda eng, o=pv(2048, 256),
              l=wk_v[:, ic * 128:(ic + 1) * 128], r=hsl(ic),
              st=(ic == 0), sp=(ic == 31):
              eng.matmul(o, l, r, start=st, stop=sp),
              inc=("sT" if ic == 31 else None))
        k_done = sch.cnt["sT"]
        W("s", "sT", k_done)
        E("s", lambda eng: eng.activation(
            out=kb_v, in_=pv(2048, 256), func=AF.Copy, bias=0.0, scale=1.0),
            inc="sS")
        W("t", "sS", sch.cnt["sS"])
        E("t", lambda eng: eng.matmul(
            pv(2560, 256), rsign_v, kb_v, start=True, stop=True), inc="sT")
        rk_done = sch.cnt["sT"]
        W("v", "sT", rk_done)
        cslc = cos_v[:, tc * 256:(tc + 1) * 256]
        sslc = sin_v[:, tc * 256:(tc + 1) * 256]
        krd = kr_v[:, tc * 256:(tc + 1) * 256]
        E("v", lambda eng, o=krd, c=cslc: eng.tensor_tensor(
            out=o, in0=pv(2048, 256), in1=c, op=OP.mult))
        E("v", lambda eng, s_=sslc: eng.tensor_tensor(
            out=kb_v, in0=pv(2560, 256), in1=s_, op=OP.mult))
        E("v", lambda eng: eng.drain())
        E("v", lambda eng, o=krd: eng.tensor_tensor(
            out=o, in0=o, in1=kb_v, op=OP.add), inc="sV")
        rope_done[tc] = sch.cnt["sV"]
        W("t", "dW0", w0_all)
        for qs in range(2):
            for ic in range(NCH):
                E("t", lambda eng, o=pv(3072 + qs * 128, 128),
                  l=hsl(ic, qs * 128, 128),
                  r=wv_v[:, ic * 128:(ic + 1) * 128],
                  st=(ic == 0), sp=(ic == 31):
                  eng.matmul(o, l, r, start=st, stop=sp),
                  inc=("sT" if (ic == 31 and qs == 1) else None))
        v_done = sch.cnt["sT"]
        W("s", "sT", v_done)
        for qs in range(2):
            kc = tc * 2 + qs
            E("s", lambda eng, o=vext_v[:, kc * 129:kc * 129 + 128],
              i=pv(3072 + qs * 128, 128):
              eng.activation(out=o, in_=i, func=AF.Copy, bias=0.0, scale=1.0),
              inc=("sS" if qs == 1 else None))
        kv_done[tc] = sch.cnt["sS"]

    if "dbg_h" in dbg_d:
        W("y", "sV", norm_done[4])
        th_ = DMA("y", dbg_d["dbg_h"], hown, "dT1")
        W("s", "dT1", th_)
    if "dbg_kr" in dbg_d:
        W("y", "sV", rope_done[5])
        DMA("y", dbg_d["dbg_kr"], kr_v, "dT5")
    if "dbg_vext" in dbg_d:
        W("y", "sS", kv_done[5])
        DMA("y", dbg_d["dbg_vext"], vext_v, "dT6")

    ao_off = R1
    ao = vw(ao_off, 32768, BF16)
    qr_off = [RQ, RQ + 8192]
    ws_off = [R3, R3 + 8192]
    ws_t = [None, None]
    ws_freed = [0, 0]
    qps_free = [[0, 0], [0, 0]]
    rqps_free = [0, 0]
    qb_free = [0, 0]
    sig_done = {}
    sps_free = [0, 0, 0]
    avps_free_v = [0, 0]
    tps_free = [0, 0]
    tp_prev = [0, 0]
    pend_tp = [None]
    unit_ctr = 0

    def flush_tp():
        if pend_tp[0] is None:
            return
        h_abs, b, blslot, blv = pend_tp[0]
        pend_tp[0] = None
        tslot = h_abs % 2
        W("t", "sV", blv)
        W("t", "sV", tps_free[tslot])
        tbase = 2048 + tslot * 256
        for qs in range(2):
            E("t", lambda eng, o=PB[:, tbase + qs * 128:
                                    tbase + (qs + 1) * 128],
              i=blnd_v[:, blslot * 256 + qs * 128:
                       blslot * 256 + (qs + 1) * 128]:
              eng.transpose(o, i, idn_v),
              inc=("sT" if qs == 1 else None))
        tp_prev[blslot] = sch.cnt["sT"]
        tp_done = sch.cnt["sT"]
        W("v", "sT", tp_done)
        E("v", lambda eng, o=ao[:, h_abs * 1024 + b * 256:
                                h_abs * 1024 + (b + 1) * 256],
          i=PB[:, tbase:tbase + 256]:
          eng.tensor_copy(o, i), inc="sV")
        tps_free[tslot] = sch.cnt["sV"]
    last_blend_v = 0
    last_blend_s = 0

    for hg in range(8):
        heads = list(range(hg * 4, hg * 4 + 4))
        qrv = vw(qr_off[hg % 2], 4096, BF16)
        for j, oc in enumerate(heads):
            slot = oc % 2
            wssem = "dWS0" if slot == 0 else "dWS1"
            W("g", "sT", ws_freed[slot])
            ws_t[slot] = DMA("g", vw(ws_off[slot], 4096, BF16), wq[oc], wssem)
            W("t", wssem, ws_t[slot])
            if hg == 0 and oc == heads[0]:
                W("t", "sV", rope_done[5])
            if hg > 0 and oc == heads[0]:
                W("t", "sV", avps_free_v[0])
                W("t", "sV", avps_free_v[1])
                W("t", "sV", tps_free[0])
                W("t", "sV", tps_free[1])
                W("t", "sS", sps_free[0])
                W("t", "sS", sps_free[1])
                W("t", "sS", sps_free[2])
            for nh in range(2):
                qcol = (2 * slot + nh) * 512
                W("t", "sV", qps_free[slot][nh])
                for ic in range(NCH):
                    E("t", lambda eng, o=pv(qcol, 512),
                      l=vw(ws_off[slot] + ic * 256, 128, BF16),
                      r=vw(hown_off + ic * 2048 + nh * 1024, 512, BF16),
                      st=(ic == 0), sp=(ic == 31):
                      eng.matmul(o, l, r, start=st, stop=sp),
                      inc=("sT" if ic == 31 else None))
                q_done = sch.cnt["sT"]
                W("s", "sT", q_done)
                W("s", "sV", qb_free[nh])
                E("s", lambda eng, o=qb_v[:, nh * 512:(nh + 1) * 512],
                  i=pv(qcol, 512):
                  eng.activation(out=o, in_=i, func=AF.Copy, bias=0.0,
                                 scale=1.0), inc="sS")
                qb_done = sch.cnt["sS"]
                W("t", "sS", qb_done)
                W("t", "sV", rqps_free[nh])
                E("t", lambda eng, o=pv(2048 + nh * 512, 512),
                  r=qb_v[:, nh * 512:(nh + 1) * 512]:
                  eng.matmul(o, rsign_v, r, start=True, stop=True), inc="sT")
                rot_done = sch.cnt["sT"]
                W("v", "sT", rot_done)
                co = cos_v[:, 256 + nh * 512: 256 + (nh + 1) * 512]
                so = sin_v[:, 256 + nh * 512: 256 + (nh + 1) * 512]
                qd = qrv[:, j * 1024 + nh * 512: j * 1024 + (nh + 1) * 512]
                tu = qb_v[:, nh * 512:(nh + 1) * 512]
                E("v", lambda eng, o=tu, c=co, i=pv(qcol, 512):
                  eng.tensor_tensor(out=o, in0=i, in1=c, op=OP.mult))
                E("v", lambda eng, o=qd, s_=so, i=pv(2048 + nh * 512, 512):
                  eng.tensor_tensor(out=o, in0=i, in1=s_, op=OP.mult))
                E("v", lambda eng: eng.drain())
                E("v", lambda eng, o=qd, u=tu: eng.tensor_tensor(
                    out=o, in0=o, in1=u, op=OP.add), inc="sV")
                qps_free[slot][nh] = sch.cnt["sV"]
                rqps_free[nh] = sch.cnt["sV"]
                qb_free[nh] = sch.cnt["sV"]
            ws_freed[slot] = q_done
        rope_hg_done = sch.cnt["sV"]

        if hg == 0 and "dbg_qr" in dbg_d:
            W("y", "sV", rope_hg_done)
            tq_ = DMA("y", dbg_d["dbg_qr"], qrv, "dT0")
            W("v", "dT0", tq_)

        for b in range(4):
            for j, h_abs in enumerate(heads):
                sig_win = [0, 0]
                for win in range(2):
                    atv = atA_v if win == 0 else atB_v
                    kb_chunk = 2 * (b + win)
                    for hf in range(2):
                        u = unit_ctr
                        unit_ctr += 1
                        sbase = 1536 + (u % 3) * 512
                        W("t", "sV", rope_hg_done)
                        W("t", "sS", sps_free[u % 3])
                        for k2 in range(2):
                            kc4 = hf * 2 + k2
                            is_mask = (b == 3 and win == 1 and kc4 >= 2)
                            last = (k2 == 1 and not is_mask)
                            E("t", lambda eng, o=pv(sbase + k2 * 256, 256),
                              l=kr_v[:, (kb_chunk + kc4) * 128:
                                     (kb_chunk + kc4 + 1) * 128],
                              r=qrv[:, j * 1024 + b * 256:
                                    j * 1024 + (b + 1) * 256],
                              sp=(not is_mask):
                              eng.matmul(o, l, r, start=True, stop=sp),
                              inc=("sT" if last else None))
                            if is_mask:
                                E("t", lambda eng,
                                  o=pv(sbase + k2 * 256, 256),
                                  l=maskm_v[:, (kc4 - 2) * 128:
                                            (kc4 - 1) * 128]:
                                  eng.matmul(o, l, onesr_v, start=False,
                                             stop=True),
                                  inc=("sT" if k2 == 1 else None))
                        sc_done = sch.cnt["sT"]
                        W("s", "sT", sc_done)
                        E("s", lambda eng, o=atv[:, hf * 512:(hf + 1) * 512],
                          i=pv(sbase, 512):
                          eng.activation(out=o, in_=i, func=AF.Sigmoid,
                                         bias=zero_ap, scale=SCALE),
                          inc="sS")
                        sig_done[u] = sch.cnt["sS"]
                        sps_free[u % 3] = sig_done[u]
                        sig_win[win] = sig_done[u]
                aset = h_abs % 2
                flush_tp()
                W("t", "sV", avps_free_v[aset])
                for win in range(2):
                    atv = atA_v if win == 0 else atB_v
                    abase = (0 if win == 0 else 512) + aset * 3072
                    kb_chunk = 2 * (b + win)
                    W("t", "sS", sig_win[win])
                    for qs in range(2):
                        for kc4 in range(4):
                            last = (win == 1 and qs == 1 and kc4 == 3)
                            E("t", lambda eng, o=pv(abase + qs * 129, 129),
                              l=atv[:, kc4 * 256 + qs * 128:
                                    kc4 * 256 + (qs + 1) * 128],
                              r=vext_v[:, (kb_chunk + kc4) * 129:
                                       (kb_chunk + kc4 + 1) * 129],
                              st=(kc4 == 0), sp=(kc4 == 3):
                              eng.matmul(o, l, r, start=st, stop=sp),
                              inc=("sT" if last else None))
                av_done = sch.cnt["sT"]
                blslot = (b * 4 + j) % 2
                W("v", "sT", av_done)
                W("v", "sT", tp_prev[blslot])
                for win in range(2):
                    abase = (0 if win == 0 else 512) + aset * 3072
                    base_ap = P[:, abase + 128:abase + 129]
                    sums = bass_rust.AP(base_ap.tensor, base_ap.offset,
                                        [[4096, 128], [129, 2]])
                    E("v", lambda eng, o=scr[:, win * 2:win * 2 + 2], i=sums:
                      eng.reciprocal(o, i))
                E("v", lambda eng: eng.drain())
                E("v", lambda eng, b2=alham_v[:, 2 * b:2 * b + 2]:
                  eng.tensor_tensor(out=scr[:, 4:6], in0=scr[:, 0:2],
                                    in1=b2, op=OP.mult))
                E("v", lambda eng, b2=alpha_v[:, 2 * b:2 * b + 2]:
                  eng.tensor_tensor(out=scr[:, 6:8], in0=scr[:, 2:4],
                                    in1=b2, op=OP.mult), inc="sV")
                E("v", lambda eng: eng.drain())
                for qs in range(2):
                    E("v", lambda eng, o=kb_v[:, qs * 128:(qs + 1) * 128],
                      i=pv(aset * 3072 + qs * 129, 128),
                      sc1=scr[:, 4 + qs:5 + qs]:
                      eng.tensor_scalar(out=o, in0=i, scalar1=sc1,
                                        scalar2=None, op0=OP.mult))
                E("v", lambda eng: eng.drain())
                for qs in range(2):
                    E("v", lambda eng, o=blnd_v[:, blslot * 256 + qs * 128:
                                            blslot * 256 + (qs + 1) * 128],
                      i=pv(aset * 3072 + 512 + qs * 129, 128),
                      sc2=scr[:, 6 + qs:7 + qs],
                      t_=kb_v[:, qs * 128:(qs + 1) * 128]:
                      eng.scalar_tensor_tensor(
                          out=o, in0=i, scalar=sc2, in1=t_,
                          op0=OP.mult, op1=OP.add),
                      inc=("sV" if qs == 1 else None))
                last_blend_v = sch.cnt["sV"]
                avps_free_v[aset] = last_blend_v
                pend_tp[0] = (h_abs, b, blslot, last_blend_v)
        flush_tp()
    ao_done_v = sch.cnt["sV"]

    if "dbg_ao" in dbg_d:
        W("y", "sV", ao_done_v)
        ta_ = DMA("y", dbg_d["dbg_ao"], ao, "dT2")
        W("y", "dT2", ta_)

    abf_off = R2
    abf = vw(abf_off, 32768, BF16)
    asq = vw(RQ, 1024, BF16)
    evac_done = [0, 0]
    asq_free = 0
    for oc in range(NCH):
        slot = oc % 2
        wssem = "dWS0" if slot == 0 else "dWS1"
        W("g", "sT", ws_freed[slot])
        ws_t[slot] = DMA("g", vw(ws_off[slot], 4096, BF16), wo[oc], wssem)
        W("t", wssem, ws_t[slot])
        if oc == 0:
            W("t", "sV", ao_done_v)
        W("t", "sS", evac_done[slot])
        base = slot * 1024
        for nh in range(2):
            for ic in range(NCH):
                E("t", lambda eng, o=pv(base + nh * 512, 512),
                  l=vw(ws_off[slot] + ic * 256, 128, BF16),
                  r=ao[:, ic * 1024 + nh * 512: ic * 1024 + (nh + 1) * 512],
                  st=(ic == 0), sp=(ic == 31):
                  eng.matmul(o, l, r, start=st, stop=sp),
                  inc=("sT" if (nh == 1 and ic == 31) else None))
        ws_freed[slot] = sch.cnt["sT"]
        wo_done = sch.cnt["sT"]
        W("s", "sT", wo_done)
        W("s", "sT", asq_free)
        E("s", lambda eng, o=abf[:, oc * 1024:(oc + 1) * 1024],
          i=pv(base, 1024):
          eng.activation(out=o, in_=i, func=AF.Copy, bias=0.0, scale=1.0))
        E("s", lambda eng, i=pv(base, 1024):
          eng.activation(out=asq, in_=i, func=AF.Square, bias=0.0, scale=1.0),
          inc="sS")
        evac_done[slot] = sch.cnt["sS"]
        W("t", "sS", evac_done[slot])
        for th in range(2):
            E("t", lambda eng, o=pv(2048 + th * 512, 512, 0, 1),
              r=abf[:, oc * 1024 + th * 512: oc * 1024 + (th + 1) * 512],
              st=(oc == 0), sp=(oc == 31):
              eng.matmul(o, onesc_v, r, start=st, stop=sp))
        for th in range(2):
            E("t", lambda eng, o=pv(3072 + th * 512, 512, 0, 1),
              r=asq[:, th * 512:(th + 1) * 512], st=(oc == 0), sp=(oc == 31):
              eng.matmul(o, onesc_v, r, start=st, stop=sp),
              inc=("sT" if th == 1 else None))
        asq_free = sch.cnt["sT"]

    if "dbg_a" in dbg_d:
        W("y", "sS", max(evac_done))
        tb_ = DMA("y", dbg_d["dbg_a"], abf, "dT3")
        W("v", "dT3", tb_)

    def ln_rows():
        W("v", "sT", sch.cnt["sT"])
        E("v", lambda eng: eng.tensor_scalar(
            out=stat[:, 0:1024], in0=pv(2048, 1024, 0, 1),
            scalar1=1.0 / 4096, scalar2=None, op0=OP.mult), inc="sV")
        W("s", "sV", sch.cnt["sV"])
        E("s", lambda eng: eng.activation(
            out=stat[:, 1024:2048], in_=stat[:, 0:1024], func=AF.Square,
            bias=0.0, scale=1.0), inc="sS")
        W("v", "sS", sch.cnt["sS"])
        E("v", lambda eng: eng.scalar_tensor_tensor(
            out=stat[:, 1024:2048], in0=pv(3072, 1024, 0, 1),
            scalar=1.0 / 4096, in1=stat[:, 1024:2048],
            op0=OP.mult, op1=OP.subtract), inc="sV")
        W("s", "sV", sch.cnt["sV"])
        E("s", lambda eng: eng.activation(
            out=stat[:, 1024:2048], in_=stat[:, 1024:2048], func=AF.Sqrt,
            bias=eps_ap, scale=1.0), inc="sS")
        W("v", "sS", sch.cnt["sS"])
        E("v", lambda eng: eng.reciprocal(stat[:, 1024:2048],
                                          stat[:, 1024:2048]))
        E("v", lambda eng: eng.drain())
        E("v", lambda eng: eng.tensor_tensor(
            out=stat[:, 0:1024], in0=stat[:, 0:1024], in1=stat[:, 1024:2048],
            op=OP.mult), inc="sV")
        return sch.cnt["sV"]

    def bcast_rows(rstd_off, mr_off):
        W("t", "sV", sch.cnt["sV"])
        for k in range(2):
            E("t", lambda eng, o=pv(rstd_off + k * 512, 512),
              r=stat[:, 1024 + k * 512:1024 + (k + 1) * 512]:
              eng.matmul(o, ones1_v, r, start=True, stop=True))
        for k in range(2):
            E("t", lambda eng, o=pv(mr_off + k * 512, 512),
              r=stat[:, k * 512:(k + 1) * 512]:
              eng.matmul(o, ones1_v, r, start=True, stop=True),
              inc=("sT" if k == 1 else None))
        return sch.cnt["sT"]

    wo_all = max(ws_freed)
    asq_last = asq_free
    ln_rows()
    bcD = bcast_rows(0, 1024)

    xrl_off = [R1 + i * 4096 for i in range(4)]
    x2o_off = [R1 + 16384, R1 + 20480]
    tmpd_off = R1 + 24576
    x2bf_off = R1 + 28672
    x2sq_off = R1 + 30720
    rl_off = [R1 + 32768 + i * 4096 for i in range(4)]
    xrl_t = [0] * 4
    x2_done = [0, 0]
    x2s_done = [0, 0]
    x2stats_free = 0
    spill_t = [None] * 32
    for oc in range(NCH):
        slot = oc % 2
        xslot = oc % 4
        W("y", "sT", wo_all)
        if oc >= 4:
            W("y", "sV", x2_done[oc % 2])
        xrsem = "dXR%d" % xslot
        for tcq in range(4):
            xrl_t[xslot] = DMA(
                "y", vw(xrl_off[xslot] + tcq * 1024, 256, F32),
                xh[tcq + 1, :, oc, :], xrsem)
        W("v", xrsem, xrl_t[xslot])
        W("v", "sT", bcD)
        if oc >= 2:
            W("v", "dSP0" if slot == 0 else "dSP1", spill_t[oc - 2])
            W("v", "sS", x2s_done[slot])
        tmpd = vw(tmpd_off, 1024, F32)
        x2o = vw(x2o_off[slot], 1024, F32)
        E("v", lambda eng: eng.drain())
        E("v", lambda eng, o=tmpd, i=abf[:, oc * 1024:(oc + 1) * 1024]:
          eng.tensor_tensor(out=o, in0=i, in1=pv(0, 1024), op=OP.mult))
        E("v", lambda eng: eng.drain())
        E("v", lambda eng, o=tmpd:
          eng.tensor_tensor(out=o, in0=o, in1=pv(1024, 1024),
                            op=OP.subtract))
        E("v", lambda eng: eng.drain())
        E("v", lambda eng, o=x2o, t_=tmpd, x_=vw(xrl_off[xslot], 1024, F32):
          eng.tensor_tensor(out=o, in0=t_, in1=x_, op=OP.add), inc="sV")
        x2_done[slot] = sch.cnt["sV"]
        W("g", "sV", x2_done[slot])
        spill_t[oc] = DMA("g", x2sp[oc], x2o, "dSP0" if slot == 0 else "dSP1")
        W("s", "sV", x2_done[slot])
        W("s", "sT", x2stats_free)
        E("s", lambda eng, o=vw(x2bf_off, 1024, BF16), i=x2o:
          eng.activation(out=o, in_=i, func=AF.Copy, bias=0.0, scale=1.0))
        E("s", lambda eng, o=vw(x2sq_off, 1024, BF16), i=x2o:
          eng.activation(out=o, in_=i, func=AF.Square, bias=0.0, scale=1.0),
          inc="sS")
        x2s_done[slot] = sch.cnt["sS"]
        W("t", "sS", x2s_done[slot])
        for th in range(2):
            E("t", lambda eng, o=pv(2048 + th * 512, 512, 0, 1),
              r=vw(x2bf_off + th * 1024, 512, BF16), st=(oc == 0),
              sp=(oc == 31):
              eng.matmul(o, onesc_v, r, start=st, stop=sp))
        for th in range(2):
            E("t", lambda eng, o=pv(3072 + th * 512, 512, 0, 1),
              r=vw(x2sq_off + th * 1024, 512, BF16), st=(oc == 0),
              sp=(oc == 31):
              eng.matmul(o, onesc_v, r, start=st, stop=sp),
              inc=("sT" if th == 1 else None))
        x2stats_free = sch.cnt["sT"]

    ln_rows()
    bc2 = bcast_rows(2048, 3072)

    h2_off = R2
    h2 = vw(h2_off, 32768, BF16)
    h2_done = [0] * 32
    for oc in range(NCH):
        rlsem = "dRL%d" % (oc % 4)
        if oc >= 4:
            W("y", "sV", h2_done[oc - 4])
        W("y", "dSP0" if oc % 2 == 0 else "dSP1", spill_t[oc])
        rlt = DMA("y", vw(rl_off[oc % 4], 1024, F32), x2sp[oc], rlsem)
        W("v", rlsem, rlt)
        W("v", "sT", bc2)
        tmpd = vw(tmpd_off, 1024, F32)
        E("v", lambda eng: eng.drain())
        E("v", lambda eng, o=tmpd, x_=vw(rl_off[oc % 4], 1024, F32):
          eng.tensor_tensor(out=o, in0=x_, in1=pv(2048, 1024), op=OP.mult))
        E("v", lambda eng: eng.drain())
        E("v", lambda eng, o=h2[:, oc * 1024:(oc + 1) * 1024], t_=tmpd:
          eng.tensor_tensor(out=o, in0=t_, in1=pv(3072, 1024),
                            op=OP.subtract), inc="sV")
        h2_done[oc] = sch.cnt["sV"]

    if "dbg_h2" in dbg_d:
        W("y", "sV", h2_done[31])
        tc_ = DMA("y", dbg_d["dbg_h2"], h2, "dT4")
        W("v", "dT4", tc_)

    ntneg = vw(RQ, 4096, BF16)
    n4_off = RQ + 8192
    w1_off = [O["wk"], O["wv"]]
    hsb = vw(O["atA"], 1024, BF16)
    fsq_off = O["atB"]
    fbf_off = R1
    fbf = vw(fbf_off, 32768, BF16)

    W("g", "sV", ao_done_v)
    W("g", "sT", asq_last)
    nt_t = DMA("g", ntneg, nt, "dNT")
    W("v", "dNT", nt_t)
    E("v", lambda eng: eng.tensor_scalar(
        out=ntneg, in0=ntneg, scalar1=-1.0, scalar2=None, op0=OP.mult),
        inc="sV")
    ntneg_done = sch.cnt["sV"]

    for th in range(2):
        for ic in range(NCH):
            W("t", "sV", h2_done[ic])
            E("t", lambda eng, o=pv(th * 512, 512),
              l=epad_v[:, 124 - ic * 4:252 - ic * 4],
              r=h2[:, ic * 1024 + th * 512: ic * 1024 + (th + 1) * 512],
              st=(ic == 0), sp=(ic == 31):
              eng.matmul(o, l, r, start=st, stop=sp),
              inc=("sT" if (th == 1 and ic == 31) else None))
    hmm_done = sch.cnt["sT"]
    W("s", "sT", hmm_done)
    E("s", lambda eng: eng.activation(
        out=hsb, in_=pv(0, 1024), func=AF.Copy, bias=0.0, scale=3.5),
        inc="sS")
    h75_done = sch.cnt["sS"]

    us_t = [0, 0]
    unpack_done = [0, 0]
    n4_free = 0
    w1_done = {}
    fdone_oc = [0] * 32
    fevac_done = 0
    fsq_free = 0
    for oc in range(NCH):
        slot = oc % 2
        wssem = "dWS0" if slot == 0 else "dWS1"
        W("g", "sT", ws_freed[slot])
        W("g", "sV", w1_done.get((oc - 2, 7), 0))
        us_t[slot] = DMA("g", vw(ws_off[slot], 4096, U16), uq[oc], wssem)
        W("g", "sT", n4_free)
        n4t = DMA("g", vw(n4_off, 4096, BF16, 0, 4), n4[oc], "dN4")
        W("v", wssem, us_t[slot])
        if oc >= 2:
            W("v", "sT", fdone_oc[oc - 2])
        vq = vw(ws_off[slot], 4096, U16)
        vbf = vw(ws_off[slot], 4096, BF16)
        E("v", lambda eng, i=vq:
          eng.tensor_scalar(out=i, in0=i, scalar1=mask32_v, scalar2=None,
                            op0=OP.bitwise_and))
        E("v", lambda eng: eng.drain())
        E("v", lambda eng, i=vq, o=vbf:
          eng.tensor_scalar(out=o, in0=i, scalar1=inv2s_v, scalar2=None,
                            op0=OP.mult),
          inc="sV")
        E("v", lambda eng: eng.drain())
        unpack_done[slot] = sch.cnt["sV"]
        for icg in range(8):
            nb = 1024 + (icg % 2) * 512
            W("t", "dN4", n4t)
            W("t", "sV", w1_done.get((oc, icg - 2),
                                     w1_done.get((oc - 1, icg + 6), 0)))
            E("t", lambda eng, o=pv(nb, 512),
              r=vw(n4_off + icg * 1024, 512, BF16, 0, 4):
              eng.matmul(o, e4_v, r, start=True, stop=True), inc="sT")
            nexp_done = sch.cnt["sT"]
            if icg == 7:
                n4_free = nexp_done
            W("v", "sT", nexp_done)
            E("v", lambda eng, o=vw(w1_off[slot] + icg * 1024, 512, BF16),
              i=vbf[:, icg * 512:(icg + 1) * 512], n_=pv(nb, 512):
              eng.tensor_tensor(out=o, in0=i, in1=n_, op=OP.mult), inc="sV")
            w1_done[(oc, icg)] = sch.cnt["sV"]
        W("t", "sS", fevac_done)
        W("t", "sS", h75_done)
        W("t", "sV", ntneg_done)
        for th in range(2):
            for ic in range(NCH):
                W("t", "sV", w1_done[(oc, ic // 4)])
                E("t", lambda eng, o=pv(th * 512, 512),
                  l=vw(w1_off[slot] + ic * 256, 128, BF16),
                  r=h2[:, ic * 1024 + th * 512: ic * 1024 + (th + 1) * 512],
                  st=(ic == 0):
                  eng.matmul(o, l, r, start=st, stop=False))
            E("t", lambda eng, o=pv(th * 512, 512),
              l=ntneg[:, oc * 128:(oc + 1) * 128],
              r=hsb[:, th * 512:(th + 1) * 512]:
              eng.matmul(o, l, r, start=False, stop=True),
              inc=("sT" if th == 1 else None))
        fdone_oc[oc] = sch.cnt["sT"]
        W("s", "dSP0", sch.cnt["dSP0"])
        W("s", "dSP1", sch.cnt["dSP1"])
        W("s", "sT", fdone_oc[oc])
        W("s", "sT", fsq_free)
        E("s", lambda eng, o=fbf[:, oc * 1024:(oc + 1) * 1024],
          i=pv(0, 1024), b_=fbias_v[:, oc:oc + 1]:
          eng.activation(out=o, in_=i, func=AF.Identity, bias=b_,
                         scale=2.0 / 7))
        E("s", lambda eng, o=vw(fsq_off, 1024, BF16), i=pv(0, 1024),
          b_=fbias_v[:, oc:oc + 1]:
          eng.activation(out=o, in_=i, func=AF.Square, bias=b_,
                         scale=2.0 / 7), inc="sS")
        fevac_done = sch.cnt["sS"]
        W("t", "sS", fevac_done)
        for th in range(2):
            E("t", lambda eng, o=pv(2048 + th * 512, 512, 0, 1),
              r=fbf[:, oc * 1024 + th * 512: oc * 1024 + (th + 1) * 512],
              st=(oc == 0), sp=(oc == 31):
              eng.matmul(o, onesc_v, r, start=st, stop=sp))
        for th in range(2):
            E("t", lambda eng, o=pv(3072 + th * 512, 512, 0, 1),
              r=vw(fsq_off + th * 1024, 512, BF16), st=(oc == 0),
              sp=(oc == 31):
              eng.matmul(o, onesc_v, r, start=st, stop=sp),
              inc=("sT" if th == 1 else None))
        fsq_free = sch.cnt["sT"]
    f_all_done = fdone_oc[31]

    if "dbg_f" in dbg_d:
        W("y", "sS", fevac_done)
        DMA("y", dbg_d["dbg_f"], fbf, "dT7")

    ln_rows()
    bc3 = bcast_rows(0, 1024)

    xr2_off = [R3, R3 + 4096]
    ob_off = [R3 + 8192, R3 + 12288]
    tmpf_off = R2
    out_t = [0, 0]
    of_done = [0, 0]
    for oc in range(NCH):
        slot = oc % 2
        frsem = "dFR0" if slot == 0 else "dFR1"
        W("y", "sV", w1_done[(30 + slot, 7)])
        if oc >= 2:
            W("y", "sV", of_done[slot])
        W("y", "dSP0", sch.cnt["dSP0"])
        W("y", "dSP1", sch.cnt["dSP1"])
        rlt = DMA("y", vw(xr2_off[slot], 1024, F32), x2sp[oc], frsem)
        W("v", frsem, rlt)
        W("v", "sT", bc3)
        if oc >= 2:
            W("v", "dOUT0" if slot == 0 else "dOUT1", out_t[slot])
        tmpf = vw(tmpf_off, 1024, F32)
        E("v", lambda eng: eng.drain())
        E("v", lambda eng, o=tmpf, i=fbf[:, oc * 1024:(oc + 1) * 1024]:
          eng.tensor_tensor(out=o, in0=i, in1=pv(0, 1024), op=OP.mult))
        E("v", lambda eng: eng.drain())
        E("v", lambda eng, o=tmpf:
          eng.tensor_tensor(out=o, in0=o, in1=pv(1024, 1024),
                            op=OP.subtract))
        E("v", lambda eng: eng.drain())
        E("v", lambda eng, o=vw(ob_off[slot], 1024, F32), t_=tmpf,
          x_=vw(xr2_off[slot], 1024, F32):
          eng.tensor_tensor(out=o, in0=t_, in1=x_, op=OP.add), inc="sV")
        of_done[slot] = sch.cnt["sV"]
        W("y", "sV", of_done[slot])
        out_t[slot] = DMA("y", out_d[oc], vw(ob_off[slot], 1024, F32), "dOUT0" if slot == 0 else "dOUT1")

    final_out0 = sch.cnt["dOUT0"]
    final_out1 = sch.cnt["dOUT1"]
    final_dbg = 0
    E("y", lambda eng: eng.wait_ge(sch.semh["dOUT0"], final_out0))
    E("y", lambda eng: eng.wait_ge(sch.semh["dOUT1"], final_out1))
    for i_ in range(8):
        if sch.cnt["dT%d" % i_]:
            E("y", lambda eng, s_="dT%d" % i_, v_=sch.cnt["dT%d" % i_]:
              eng.wait_ge(sch.semh[s_], v_))

    with nc.Block() as blk:
        @blk.sync
        def _(eng):
            sch.run("y", eng)

        @blk.gpsimd
        def _(eng):
            sch.run("g", eng)

        @blk.tensor
        def _(eng):
            sch.run("t", eng)

        @blk.vector
        def _(eng):
            sch.run("v", eng)

        @blk.scalar
        def _(eng):
            sch.run("s", eng)

    ctx.close()
    return nc, list(dbg_d.keys())


def kernel(L0_attn_x, L0_attn_Wq, L0_attn_Wk, L0_attn_Wv, L0_attn_Wo,
           L0_ffn_weight_q3, L0_ffn_weight_norm, L0_ffn_bias):
    x = np.asarray(L0_attn_x, dtype=np.float32)
    shared = _prep_shared(np.asarray(L0_attn_Wq, np.float32),
                          np.asarray(L0_attn_Wk, np.float32),
                          np.asarray(L0_attn_Wv, np.float32),
                          np.asarray(L0_attn_Wo, np.float32),
                          np.asarray(L0_ffn_weight_q3),
                          np.asarray(L0_ffn_weight_norm, np.float32),
                          np.asarray(L0_ffn_bias, np.float32))
    nc, taps = build_nc(tuple(_dbg_taps))
    in_maps = []
    for c in range(NCORES):
        m = dict(shared)
        m.update(_prep_core(x, c))
        in_maps.append(m)
    res = run_bass_kernel_spmd(nc, in_maps, core_ids=list(range(NCORES)))
    out = np.empty((B, S, D), dtype=np.float32)
    for c in range(NCORES):
        b = c // 4
        T0 = 1024 * (c % 4)
        oc = res.results[c]["out"]
        out[b, T0:T0 + 1024, :] = np.ascontiguousarray(
            oc.transpose(2, 0, 1)).reshape(1024, 4096)
    if taps:
        kernel.dbg = [{t: res.results[c][t] for t in taps}
                      for c in range(NCORES)]
    return out


# version 05
# speedup vs baseline: 1.2558x; 1.0287x over previous
import math
import contextlib

import numpy as np
import ml_dtypes

import bass_rust
import concourse.bass as bass
import concourse.mybir as mybir
from concourse.bass_utils import run_bass_kernel_spmd

F32 = mybir.dt.float32
BF16 = mybir.dt.bfloat16
U16 = mybir.dt.uint16
I32 = mybir.dt.int32
U8 = mybir.dt.uint8
OP = mybir.AluOpType
AF = mybir.ActivationFunctionType
BF = ml_dtypes.bfloat16

B, S, D = 2, 4096, 4096
H, DH = 32, 128
SCALE = 1.0 / math.sqrt(DH)
NCORES = 8
TOK = 1024
HTOK = 1536
NCH = 32
EPS = 1e-5

HEAP_BYTES = 212832


def _al(x, a=32):
    return (x + a - 1) // a * a


class _Off:
    def __init__(self, base=0):
        self.cur = base
        self.map = {}

    def add(self, name, nbytes):
        self.map[name] = self.cur
        self.cur = _al(self.cur + nbytes)
        return self.map[name]


_P = _Off()
_P.add("cos", 1536 * 2)
_P.add("sin", 1536 * 2)
_P.add("wk", 4096 * 2)
_P.add("wv", 4096 * 2)
_P.add("kr", 1536 * 2)
_P.add("vext", 1548 * 2)
_P.add("rsign", 128 * 2)
_P.add("idn", 128 * 2)
_P.add("epad", 256 * 2)
_P.add("e4", 128 * 2)
_P.add("ones1", 128 * 4)
_P.add("onesc", 32)
_P.add("alham", 8 * 4)
_P.add("alpha", 8 * 4)
_P.add("maskm", 256 * 2)
_P.add("onesr", 256 * 2)
_P.add("fbias", 32 * 4)
_P.add("consts", 8 * 4)
_P.add("shiftv", 64)
_P.add("stat", 2048 * 4)
_P.add("scr", 16 * 4)
_P.add("kb", 256 * 2)
_P.add("blnd", 512 * 2)
_P.add("atA", 1024 * 2)
_P.add("atB", 1024 * 2)
_P.add("qb", 1024 * 2)
PERSIST_END = _P.cur

R3 = _al(PERSIST_END)
R3_SZ = 2 * 8192
R1 = _al(R3 + R3_SZ)
R1_SZ = 65536
R2 = _al(R1 + R1_SZ)
R2_SZ = 65536
RQ = _al(R2 + R2_SZ)
RQ_SZ = 16384
assert RQ + RQ_SZ <= HEAP_BYTES, (RQ + RQ_SZ, HEAP_BYTES)

_dbg_taps = []


def _rope_cache():
    inv_freq = 1.0 / 10000.0 ** (np.arange(0, DH, 2, dtype=np.float32) / DH)
    t = np.arange(S, dtype=np.float32)
    f = np.outer(t, inv_freq)
    cos = np.zeros((S, DH), dtype=np.float32)
    sin = np.zeros((S, DH), dtype=np.float32)
    cos[:, 0::2] = np.cos(f)
    cos[:, 1::2] = np.cos(f)
    sin[:, 0::2] = np.sin(f)
    sin[:, 1::2] = np.sin(f)
    return cos, sin


def _prep_shared(Wq, Wk, Wv, Wo, wq3, wnorm, fbias):
    d = {}
    d["wq"] = np.ascontiguousarray(
        Wq.reshape(32, 128, 32, 128).transpose(0, 3, 2, 1))
    d["wo"] = np.ascontiguousarray(
        Wo.reshape(32, 128, 32, 128).transpose(0, 3, 2, 1))
    d["wk"] = np.ascontiguousarray(Wk.reshape(128, 32, 128).transpose(2, 1, 0))
    d["wv"] = np.ascontiguousarray(Wv.reshape(128, 32, 128).transpose(2, 1, 0))

    pk = np.ascontiguousarray(wq3).view(np.uint8)
    pk13 = np.concatenate(
        [pk, np.zeros((pk.shape[0], 1), np.uint8)], axis=1)
    j = np.arange(32)
    bidx = (3 * j) // 8
    lo = pk13[:, bidx]
    hi = pk13[:, bidx + 1]
    b2 = np.stack([lo, hi], axis=-1)
    u16 = np.ascontiguousarray(b2).reshape(-1, 64).view(np.uint16)
    u16oi = u16.reshape(4096, 4096)
    d["uq"] = np.ascontiguousarray(
        u16oi.reshape(32, 128, 32, 128).transpose(0, 3, 2, 1))

    n_og = wnorm.reshape(4096, 128)
    d["n4"] = np.ascontiguousarray(
        n_og.reshape(32, 128, 32, 4).transpose(0, 3, 2, 1))
    d["nt"] = np.ascontiguousarray(n_og.T)
    d["fbias"] = np.ascontiguousarray(fbias.reshape(32, 128).T)

    e4 = np.zeros((4, 128), dtype=BF)
    for g in range(4):
        e4[g, g * 32:(g + 1) * 32] = 1
    d["e4"] = e4
    epad = np.zeros((128, 256), dtype=BF)
    p = np.arange(128)
    epad[p, 124 + p // 32] = 1
    d["epad"] = epad
    rsign = np.zeros((128, 128), dtype=np.float32)
    m = np.arange(64)
    rsign[m + 64, m] = -1.0
    rsign[m, m + 64] = 1.0
    d["rsign"] = rsign.astype(BF)
    d["idn"] = np.eye(128, dtype=BF)
    d["ones1"] = np.ones((1, 128), np.float32)
    d["onesc"] = np.ones((128, 1), dtype=BF)
    d["onesr"] = np.ones((1, 256), dtype=BF)
    consts = np.zeros((128, 8), np.float32)
    consts[:, 0] = EPS
    d["consts"] = consts
    s_ = (3 * (np.arange(128) % 32)) % 8
    d["mask32"] = (7 << s_).astype(np.int32).reshape(128, 1)
    d["inv2s"] = (0.5 ** s_).astype(np.float32).reshape(128, 1)
    return d


def _prep_core(x, core):
    b = core // 4
    T0 = 1024 * (core % 4)
    lo = T0 - 256
    xh = np.zeros((HTOK, D), dtype=np.float32)
    s0 = max(lo, 0)
    s1 = min(T0 + 1280, S)
    xh[s0 - lo:s1 - lo] = x[b, s0:s1]
    xh_arr = np.ascontiguousarray(
        xh.reshape(6, 256, 32, 128).transpose(0, 3, 2, 1))

    cos, sin = _rope_cache()
    pos = np.clip(np.arange(lo, lo + HTOK), 0, S - 1)
    cosT = np.ascontiguousarray(cos[pos].T)
    sinT = np.ascontiguousarray(sin[pos].T)

    alpha = np.zeros((128, 8), np.float32)
    alham = np.zeros((128, 8), np.float32)
    for bl in range(4):
        gb = T0 // 256 + bl
        if gb == 0:
            a = np.ones(256, np.float32)
        else:
            a = np.linspace(0.0, 1.0, 256).astype(np.float32)
        for qs in range(2):
            alpha[:, 2 * bl + qs] = a[qs * 128:(qs + 1) * 128]
            alham[:, 2 * bl + qs] = (1.0 - a)[qs * 128:(qs + 1) * 128]

    maskm = np.zeros((1, 256), dtype=BF)
    if core % 4 == 3:
        maskm[:] = np.float32(-30.0 / SCALE).astype(BF)
    return dict(xh=xh_arr, cosT=cosT, sinT=sinT, alpha=alpha, alham=alham,
                maskm=maskm)


class Sched:
    ENGINES = ("y", "g", "t", "v", "s")

    def __init__(self):
        self.q = {e: [] for e in self.ENGINES}
        self.cnt = {}
        self.hw = {}
        self.semh = {}

    def new_sem(self, name, handle):
        self.semh[name] = handle
        self.cnt[name] = 0
        for e in self.ENGINES:
            self.hw[(e, name)] = 0

    def wait(self, e, sem, val):
        if val is None or val <= 0:
            return
        if self.hw[(e, sem)] >= val:
            return
        self.hw[(e, sem)] = val
        h = self.semh[sem]
        self.q[e].append(lambda eng, h=h, v=val: eng.wait_ge(h, v))

    def emit(self, e, fn, inc=None, k=1):
        if inc is None:
            self.q[e].append(fn)
            return None
        self.cnt[inc] += k
        tgt = self.cnt[inc]
        h = self.semh[inc]

        def wrapped(eng, fn=fn, h=h, k=k):
            fn(eng).then_inc(h, k)
        self.q[e].append(wrapped)
        return tgt

    def dma(self, e, out_ap, in_ap, sem):
        return self.emit(
            e, lambda eng, o=out_ap, i=in_ap: eng.dma_start(out=o, in_=i),
            inc=sem, k=16)

    def run(self, e, eng):
        for fn in self.q[e]:
            fn(eng)


def _bc(ap, reps, inner):
    pat = [list(p) for p in ap.ap]
    assert len(pat) == 2, pat
    return bass_rust.AP(ap.tensor, ap.offset, [pat[0], [0, reps], pat[1]])


def _strided(ap, elem_off, dims):
    pat = [list(p) for p in ap.ap]
    return bass_rust.AP(ap.tensor, ap.offset + elem_off,
                        [pat[0]] + [list(d) for d in dims])


def build_nc(debug_taps=()):
    nc = bass.Bass("TRN2", target_bir_lowering=False)

    di = {}

    def inp(name, shape, dt):
        di[name] = nc.dram_tensor(name, list(shape), dt,
                                  kind="ExternalInput").ap()
        return di[name]

    xh = inp("xh", (6, 128, 32, 256), F32)
    wq = inp("wq", (32, 128, 32, 128), F32)
    wo = inp("wo", (32, 128, 32, 128), F32)
    wk = inp("wk", (128, 32, 128), F32)
    wv = inp("wv", (128, 32, 128), F32)
    uq = inp("uq", (32, 128, 32, 128), U16)
    n4 = inp("n4", (32, 4, 32, 128), F32)
    nt = inp("nt", (128, 4096), F32)
    fbias_d = inp("fbias", (128, 32), F32)
    cosT = inp("cosT", (128, 1536), F32)
    sinT = inp("sinT", (128, 1536), F32)
    alpha = inp("alpha", (128, 8), F32)
    alham = inp("alham", (128, 8), F32)
    maskm = inp("maskm", (1, 256), BF16)
    rsign = inp("rsign", (128, 128), BF16)
    idn = inp("idn", (128, 128), BF16)
    epad = inp("epad", (128, 256), BF16)
    e4 = inp("e4", (4, 128), BF16)
    ones1 = inp("ones1", (1, 128), F32)
    onesc = inp("onesc", (128, 1), BF16)
    onesr = inp("onesr", (1, 256), BF16)
    consts = inp("consts", (128, 8), F32)
    mask32 = inp("mask32", (128, 1), I32)
    inv2s = inp("inv2s", (128, 1), F32)

    out_d = nc.dram_tensor("out", [32, 128, 1024], F32,
                           kind="ExternalOutput").ap()
    x2sp = nc.dram_tensor("x2sp", [32, 128, 1024], F32, kind="Internal").ap()
    dbg_d = {}
    for tap, shape, dt in (
        ("dbg_h", [128, 32768], BF16), ("dbg_kr", [128, 1536], BF16),
        ("dbg_vext", [128, 1548], BF16), ("dbg_qr", [128, 4096], BF16),
        ("dbg_ao", [128, 32768], BF16), ("dbg_a", [128, 32768], BF16),
        ("dbg_h2", [128, 32768], BF16), ("dbg_f", [128, 32768], BF16),
    ):
        if tap in debug_taps:
            dbg_d[tap] = nc.dram_tensor(tap, shape, dt,
                                        kind="ExternalOutput").ap()

    ctx = contextlib.ExitStack()
    heap = ctx.enter_context(nc.sbuf_tensor("heap", [128, HEAP_BYTES], U8))
    P = ctx.enter_context(nc.psum_tensor("psum", [128, 4096], F32))
    PB = P.bitcast(BF16)

    sch = Sched()
    for s in ("sT", "sV", "sS", "dCN", "dW0", "dXB0", "dXB1", "dWS0", "dWS1",
              "dN4", "dNT", "dXR0", "dXR1", "dXR2", "dXR3", "dSP0", "dSP1", "dSP2", "dSP3", "dRL0", "dRL1", "dRL2", "dRL3",
              "dFR0", "dFR1", "dFR2", "dFR3", "dOUT0", "dOUT1", "dOUT2", "dOUT3", "dDBG"):
        sch.new_sem(s, ctx.enter_context(nc.semaphore(s)))
    for i_ in range(8):
        sch.new_sem("dT%d" % i_, ctx.enter_context(nc.semaphore("dT%d" % i_)))

    def vw(off, n, dt, p0=0, p1=128):
        sz = mybir.dt.size(dt)
        return heap[p0:p1, off:off + n * sz].bitcast(dt)

    def pv(elem_off, n, p0=0, p1=128):
        return P[p0:p1, elem_off:elem_off + n]

    O = _P.map
    W = sch.wait
    E = sch.emit
    DMA = sch.dma

    cos_v = vw(O["cos"], 1536, BF16)
    sin_v = vw(O["sin"], 1536, BF16)
    wk_v = vw(O["wk"], 4096, BF16)
    wv_v = vw(O["wv"], 4096, BF16)
    kr_v = vw(O["kr"], 1536, BF16)
    vext_v = vw(O["vext"], 1548, BF16)
    rsign_v = vw(O["rsign"], 128, BF16)
    idn_v = vw(O["idn"], 128, BF16)
    epad_v = vw(O["epad"], 256, BF16)
    e4_v = vw(O["e4"], 128, BF16, 0, 4)
    ones1_v = vw(O["ones1"], 128, F32, 0, 1)
    onesc_v = vw(O["onesc"], 1, BF16)
    alham_v = vw(O["alham"], 8, F32)
    alpha_v = vw(O["alpha"], 8, F32)
    maskm_v = vw(O["maskm"], 256, BF16, 0, 1)
    onesr_v = vw(O["onesr"], 256, BF16, 0, 1)
    fbias_v = vw(O["fbias"], 32, F32)
    consts_v = vw(O["consts"], 8, F32)
    mask32_v = vw(O["shiftv"], 1, I32)
    inv2s_v = vw(O["shiftv"] + 32, 1, F32)
    eps_ap = vw(O["consts"], 8, F32, 0, 1)[:, 0:1]
    zero_ap = consts_v[:, 1:2]
    stat = vw(O["stat"], 2048, F32, 0, 1)
    scr = vw(O["scr"], 16, F32)
    kb_v = vw(O["kb"], 256, BF16)
    blnd_v = vw(O["blnd"], 512, BF16)
    atA_v = vw(O["atA"], 1024, BF16)
    atB_v = vw(O["atB"], 1024, BF16)
    qb_v = vw(O["qb"], 1024, BF16)

    for name, v_ in (("rsign", rsign_v),
                     ("idn", idn_v), ("epad", epad_v), ("e4", e4_v),
                     ("ones1", ones1_v), ("onesc", onesc_v),
                     ("alham", alham_v), ("alpha", alpha_v),
                     ("maskm", maskm_v), ("onesr", onesr_v),
                     ("fbias", fbias_v),
                     ("consts", consts_v), ("mask32", mask32_v),
                     ("inv2s", inv2s_v)):
        DMA("y", v_, di[name], "dCN")
    c_consts = sch.cnt["dCN"]
    DMA("g", cos_v, di["cosT"], "dW0")
    DMA("g", sin_v, di["sinT"], "dW0")
    DMA("g", wk_v, wk, "dW0")
    DMA("g", wv_v, wv, "dW0")
    w0_all = sch.cnt["dW0"]
    for kc in range(12):
        E("v", lambda eng, a=vext_v[:, kc * 129 + 128: kc * 129 + 129]:
          eng.memset(a, 1.0), inc=("sV" if kc == 11 else None))

    xb_off = [R1, R1 + 16384]
    xsq_off = R1 + 32768
    htmp_off = R1 + 49152
    hown_off = R2
    hown = vw(hown_off, 32768, BF16)

    xb_t = [None] * 6
    norm_done = [0] * 6
    kv_done = [0] * 6
    rope_done = [0] * 6
    mvar_done = [0] * 6
    sumsq_done = [0] * 6

    for e_ in ("t", "v", "s"):
        W(e_, "dCN", c_consts)
    for tc in range(6):
        xb = vw(xb_off[tc % 2], 8192, BF16)
        xsq = vw(xsq_off, 8192, BF16)
        xbsem = "dXB0" if tc % 2 == 0 else "dXB1"
        if tc >= 2:
            W("g", "sV", norm_done[tc - 2])
        xb_t[tc] = DMA("g", xb, xh[tc], xbsem)
        W("s", xbsem, xb_t[tc])
        if tc >= 1:
            W("s", "sT", sumsq_done[tc - 1])
        E("s", lambda eng, o=xsq, i=xb: eng.activation(
            out=o, in_=i, func=AF.Square, bias=0.0, scale=1.0), inc="sS")
        sq_ready = sch.cnt["sS"]
        W("t", xbsem, xb_t[tc])
        if tc >= 1:
            W("t", "sV", mvar_done[tc - 1])
        for ic in range(NCH):
            E("t", lambda eng, o=pv(0, 256, 0, 1), l=onesc_v,
              r=xb[:, ic * 256:(ic + 1) * 256], st=(ic == 0), sp=(ic == 31):
              eng.matmul(o, l, r, start=st, stop=sp))
        W("t", "sS", sq_ready)
        for ic in range(NCH):
            E("t", lambda eng, o=pv(512, 256, 0, 1), l=onesc_v,
              r=xsq[:, ic * 256:(ic + 1) * 256], st=(ic == 0), sp=(ic == 31):
              eng.matmul(o, l, r, start=st, stop=sp),
              inc=("sT" if ic == 31 else None))
        stats_done = sch.cnt["sT"]
        sumsq_done[tc] = stats_done
        W("v", "sT", stats_done)
        E("v", lambda eng: eng.tensor_scalar(
            out=stat[:, 0:256], in0=pv(0, 256, 0, 1), scalar1=1.0 / 4096,
            scalar2=None, op0=OP.mult), inc="sV")
        W("s", "sV", sch.cnt["sV"])
        E("s", lambda eng: eng.activation(
            out=stat[:, 256:512], in_=stat[:, 0:256], func=AF.Square,
            bias=0.0, scale=1.0), inc="sS")
        W("v", "sS", sch.cnt["sS"])
        E("v", lambda eng: eng.scalar_tensor_tensor(
            out=stat[:, 512:768], in0=pv(512, 256, 0, 1), scalar=1.0 / 4096,
            in1=stat[:, 256:512], op0=OP.mult, op1=OP.subtract), inc="sV")
        W("s", "sV", sch.cnt["sV"])
        E("s", lambda eng: eng.activation(
            out=stat[:, 512:768], in_=stat[:, 512:768], func=AF.Sqrt,
            bias=eps_ap, scale=1.0), inc="sS")
        W("v", "sS", sch.cnt["sS"])
        E("v", lambda eng: eng.reciprocal(stat[:, 512:768], stat[:, 512:768]))
        E("v", lambda eng: eng.drain())
        E("v", lambda eng: eng.tensor_tensor(
            out=stat[:, 0:256], in0=stat[:, 0:256], in1=stat[:, 512:768],
            op=OP.mult), inc="sV")
        mvar_done[tc] = sch.cnt["sV"]
        bb = 1024 if tc % 2 == 0 else 1536
        W("t", "sV", mvar_done[tc])
        if tc >= 2:
            W("t", "sV", norm_done[tc - 2])
        E("t", lambda eng, o=pv(bb, 256), r=stat[:, 512:768]:
          eng.matmul(o, ones1_v, r, start=True, stop=True))
        E("t", lambda eng, o=pv(bb + 256, 256), r=stat[:, 0:256]:
          eng.matmul(o, ones1_v, r, start=True, stop=True), inc="sT")
        bcast_done = sch.cnt["sT"]
        W("v", "sT", bcast_done)
        if tc in (1, 2, 3, 4):
            hdst = _strided(hown, (tc - 1) * 256, [[1024, 32], [1, 256]])
        else:
            hch = vw(htmp_off, 8192, BF16)
            hdst = hch.rearrange("p (c t) -> p c t", c=32)
        xb3 = xb.rearrange("p (c t) -> p c t", c=32)
        E("v", lambda eng, o=hdst, i=xb3, b_=_bc(pv(bb, 256), 32, 256):
          eng.tensor_tensor(out=o, in0=i, in1=b_, op=OP.mult))
        E("v", lambda eng: eng.drain())
        E("v", lambda eng, o=hdst, b_=_bc(pv(bb + 256, 256), 32, 256):
          eng.tensor_tensor(out=o, in0=o, in1=b_, op=OP.subtract), inc="sV")
        norm_done[tc] = sch.cnt["sV"]

        def hsl(ic, lo=0, n=256, tc=tc):
            if tc in (1, 2, 3, 4):
                return vw(hown_off + ic * 2048 + (tc - 1) * 512 + lo * 2,
                          n, BF16)
            return vw(htmp_off + ic * 512 + lo * 2, n, BF16)

        W("t", "sV", norm_done[tc])
        if tc >= 1:
            W("t", "sV", rope_done[tc - 1])
            W("t", "sS", kv_done[tc - 1])
        W("t", "dW0", w0_all)
        for ic in range(NCH):
            E("t", lambda eng, o=pv(2048, 256),
              l=wk_v[:, ic * 128:(ic + 1) * 128], r=hsl(ic),
              st=(ic == 0), sp=(ic == 31):
              eng.matmul(o, l, r, start=st, stop=sp),
              inc=("sT" if ic == 31 else None))
        k_done = sch.cnt["sT"]
        W("s", "sT", k_done)
        E("s", lambda eng: eng.activation(
            out=kb_v, in_=pv(2048, 256), func=AF.Copy, bias=0.0, scale=1.0),
            inc="sS")
        W("t", "sS", sch.cnt["sS"])
        E("t", lambda eng: eng.matmul(
            pv(2560, 256), rsign_v, kb_v, start=True, stop=True), inc="sT")
        rk_done = sch.cnt["sT"]
        W("v", "sT", rk_done)
        cslc = cos_v[:, tc * 256:(tc + 1) * 256]
        sslc = sin_v[:, tc * 256:(tc + 1) * 256]
        krd = kr_v[:, tc * 256:(tc + 1) * 256]
        E("v", lambda eng, o=krd, c=cslc: eng.tensor_tensor(
            out=o, in0=pv(2048, 256), in1=c, op=OP.mult))
        E("v", lambda eng, s_=sslc: eng.tensor_tensor(
            out=kb_v, in0=pv(2560, 256), in1=s_, op=OP.mult))
        E("v", lambda eng: eng.drain())
        E("v", lambda eng, o=krd: eng.tensor_tensor(
            out=o, in0=o, in1=kb_v, op=OP.add), inc="sV")
        rope_done[tc] = sch.cnt["sV"]
        W("t", "dW0", w0_all)
        for qs in range(2):
            for ic in range(NCH):
                E("t", lambda eng, o=pv(3072 + qs * 128, 128),
                  l=hsl(ic, qs * 128, 128),
                  r=wv_v[:, ic * 128:(ic + 1) * 128],
                  st=(ic == 0), sp=(ic == 31):
                  eng.matmul(o, l, r, start=st, stop=sp),
                  inc=("sT" if (ic == 31 and qs == 1) else None))
        v_done = sch.cnt["sT"]
        W("s", "sT", v_done)
        for qs in range(2):
            kc = tc * 2 + qs
            E("s", lambda eng, o=vext_v[:, kc * 129:kc * 129 + 128],
              i=pv(3072 + qs * 128, 128):
              eng.activation(out=o, in_=i, func=AF.Copy, bias=0.0, scale=1.0),
              inc=("sS" if qs == 1 else None))
        kv_done[tc] = sch.cnt["sS"]

    if "dbg_h" in dbg_d:
        W("y", "sV", norm_done[4])
        th_ = DMA("y", dbg_d["dbg_h"], hown, "dT1")
        W("s", "dT1", th_)
    if "dbg_kr" in dbg_d:
        W("y", "sV", rope_done[5])
        DMA("y", dbg_d["dbg_kr"], kr_v, "dT5")
    if "dbg_vext" in dbg_d:
        W("y", "sS", kv_done[5])
        DMA("y", dbg_d["dbg_vext"], vext_v, "dT6")

    ao_off = R1
    ao = vw(ao_off, 32768, BF16)
    qr_off = [RQ, RQ + 8192]
    ws_off = [R3, R3 + 8192]
    ws_t = [None, None]
    ws_freed = [0, 0]
    qps_free = [[0, 0], [0, 0]]
    rqps_free = [0, 0]
    qb_free = [0, 0]
    sig_done = {}
    sps_free = [0, 0, 0]
    avps_free_v = [0, 0]
    tps_free = [0, 0]
    tp_prev = [0, 0]
    pend_tp = [None]
    unit_ctr = 0

    def flush_tp():
        if pend_tp[0] is None:
            return
        h_abs, b, blslot, blv = pend_tp[0]
        pend_tp[0] = None
        tslot = h_abs % 2
        W("t", "sV", blv)
        W("t", "sV", tps_free[tslot])
        tbase = 2048 + tslot * 256
        for qs in range(2):
            E("t", lambda eng, o=PB[:, tbase + qs * 128:
                                    tbase + (qs + 1) * 128],
              i=blnd_v[:, blslot * 256 + qs * 128:
                       blslot * 256 + (qs + 1) * 128]:
              eng.transpose(o, i, idn_v),
              inc=("sT" if qs == 1 else None))
        tp_prev[blslot] = sch.cnt["sT"]
        tp_done = sch.cnt["sT"]
        W("v", "sT", tp_done)
        E("v", lambda eng, o=ao[:, h_abs * 1024 + b * 256:
                                h_abs * 1024 + (b + 1) * 256],
          i=PB[:, tbase:tbase + 256]:
          eng.tensor_copy(o, i), inc="sV")
        tps_free[tslot] = sch.cnt["sV"]
    last_blend_v = 0
    last_blend_s = 0

    for hg in range(8):
        heads = list(range(hg * 4, hg * 4 + 4))
        qrv = vw(qr_off[hg % 2], 4096, BF16)
        for j, oc in enumerate(heads):
            slot = oc % 2
            wssem = "dWS0" if slot == 0 else "dWS1"
            W("g", "sT", ws_freed[slot])
            ws_t[slot] = DMA("g", vw(ws_off[slot], 4096, BF16), wq[oc], wssem)
            W("t", wssem, ws_t[slot])
            if hg == 0 and oc == heads[0]:
                W("t", "sV", rope_done[5])
            if hg > 0 and oc == heads[0]:
                W("t", "sV", avps_free_v[0])
                W("t", "sV", avps_free_v[1])
                W("t", "sV", tps_free[0])
                W("t", "sV", tps_free[1])
                W("t", "sS", sps_free[0])
                W("t", "sS", sps_free[1])
                W("t", "sS", sps_free[2])
            for nh in range(2):
                qcol = (2 * slot + nh) * 512
                W("t", "sV", qps_free[slot][nh])
                for ic in range(NCH):
                    E("t", lambda eng, o=pv(qcol, 512),
                      l=vw(ws_off[slot] + ic * 256, 128, BF16),
                      r=vw(hown_off + ic * 2048 + nh * 1024, 512, BF16),
                      st=(ic == 0), sp=(ic == 31):
                      eng.matmul(o, l, r, start=st, stop=sp),
                      inc=("sT" if ic == 31 else None))
                q_done = sch.cnt["sT"]
                W("s", "sT", q_done)
                W("s", "sV", qb_free[nh])
                E("s", lambda eng, o=qb_v[:, nh * 512:(nh + 1) * 512],
                  i=pv(qcol, 512):
                  eng.activation(out=o, in_=i, func=AF.Copy, bias=0.0,
                                 scale=1.0), inc="sS")
                qb_done = sch.cnt["sS"]
                W("t", "sS", qb_done)
                W("t", "sV", rqps_free[nh])
                E("t", lambda eng, o=pv(2048 + nh * 512, 512),
                  r=qb_v[:, nh * 512:(nh + 1) * 512]:
                  eng.matmul(o, rsign_v, r, start=True, stop=True), inc="sT")
                rot_done = sch.cnt["sT"]
                W("v", "sT", rot_done)
                co = cos_v[:, 256 + nh * 512: 256 + (nh + 1) * 512]
                so = sin_v[:, 256 + nh * 512: 256 + (nh + 1) * 512]
                qd = qrv[:, j * 1024 + nh * 512: j * 1024 + (nh + 1) * 512]
                tu = qb_v[:, nh * 512:(nh + 1) * 512]
                E("v", lambda eng, o=tu, c=co, i=pv(qcol, 512):
                  eng.tensor_tensor(out=o, in0=i, in1=c, op=OP.mult))
                E("v", lambda eng, o=qd, s_=so, i=pv(2048 + nh * 512, 512):
                  eng.tensor_tensor(out=o, in0=i, in1=s_, op=OP.mult))
                E("v", lambda eng: eng.drain())
                E("v", lambda eng, o=qd, u=tu: eng.tensor_tensor(
                    out=o, in0=o, in1=u, op=OP.add), inc="sV")
                qps_free[slot][nh] = sch.cnt["sV"]
                rqps_free[nh] = sch.cnt["sV"]
                qb_free[nh] = sch.cnt["sV"]
            ws_freed[slot] = q_done
        rope_hg_done = sch.cnt["sV"]

        if hg == 0 and "dbg_qr" in dbg_d:
            W("y", "sV", rope_hg_done)
            tq_ = DMA("y", dbg_d["dbg_qr"], qrv, "dT0")
            W("v", "dT0", tq_)

        for b in range(4):
            for j, h_abs in enumerate(heads):
                sig_win = [0, 0]
                for win in range(2):
                    atv = atA_v if win == 0 else atB_v
                    kb_chunk = 2 * (b + win)
                    for hf in range(2):
                        u = unit_ctr
                        unit_ctr += 1
                        sbase = 1536 + (u % 3) * 512
                        W("t", "sV", rope_hg_done)
                        W("t", "sS", sps_free[u % 3])
                        for k2 in range(2):
                            kc4 = hf * 2 + k2
                            is_mask = (b == 3 and win == 1 and kc4 >= 2)
                            last = (k2 == 1 and not is_mask)
                            E("t", lambda eng, o=pv(sbase + k2 * 256, 256),
                              l=kr_v[:, (kb_chunk + kc4) * 128:
                                     (kb_chunk + kc4 + 1) * 128],
                              r=qrv[:, j * 1024 + b * 256:
                                    j * 1024 + (b + 1) * 256],
                              sp=(not is_mask):
                              eng.matmul(o, l, r, start=True, stop=sp),
                              inc=("sT" if last else None))
                            if is_mask:
                                E("t", lambda eng,
                                  o=pv(sbase + k2 * 256, 256),
                                  l=maskm_v[:, (kc4 - 2) * 128:
                                            (kc4 - 1) * 128]:
                                  eng.matmul(o, l, onesr_v, start=False,
                                             stop=True),
                                  inc=("sT" if k2 == 1 else None))
                        sc_done = sch.cnt["sT"]
                        W("s", "sT", sc_done)
                        E("s", lambda eng, o=atv[:, hf * 512:(hf + 1) * 512],
                          i=pv(sbase, 512):
                          eng.activation(out=o, in_=i, func=AF.Sigmoid,
                                         bias=zero_ap, scale=SCALE),
                          inc="sS")
                        sig_done[u] = sch.cnt["sS"]
                        sps_free[u % 3] = sig_done[u]
                        sig_win[win] = sig_done[u]
                aset = h_abs % 2
                flush_tp()
                W("t", "sV", avps_free_v[aset])
                for win in range(2):
                    atv = atA_v if win == 0 else atB_v
                    abase = (0 if win == 0 else 512) + aset * 3072
                    kb_chunk = 2 * (b + win)
                    W("t", "sS", sig_win[win])
                    for qs in range(2):
                        for kc4 in range(4):
                            last = (win == 1 and qs == 1 and kc4 == 3)
                            E("t", lambda eng, o=pv(abase + qs * 129, 129),
                              l=atv[:, kc4 * 256 + qs * 128:
                                    kc4 * 256 + (qs + 1) * 128],
                              r=vext_v[:, (kb_chunk + kc4) * 129:
                                       (kb_chunk + kc4 + 1) * 129],
                              st=(kc4 == 0), sp=(kc4 == 3):
                              eng.matmul(o, l, r, start=st, stop=sp),
                              inc=("sT" if last else None))
                av_done = sch.cnt["sT"]
                blslot = (b * 4 + j) % 2
                W("v", "sT", av_done)
                W("v", "sT", tp_prev[blslot])
                for win in range(2):
                    abase = (0 if win == 0 else 512) + aset * 3072
                    base_ap = P[:, abase + 128:abase + 129]
                    sums = bass_rust.AP(base_ap.tensor, base_ap.offset,
                                        [[4096, 128], [129, 2]])
                    E("v", lambda eng, o=scr[:, win * 2:win * 2 + 2], i=sums:
                      eng.reciprocal(o, i))
                E("v", lambda eng: eng.drain())
                E("v", lambda eng, b2=alham_v[:, 2 * b:2 * b + 2]:
                  eng.tensor_tensor(out=scr[:, 4:6], in0=scr[:, 0:2],
                                    in1=b2, op=OP.mult))
                E("v", lambda eng, b2=alpha_v[:, 2 * b:2 * b + 2]:
                  eng.tensor_tensor(out=scr[:, 6:8], in0=scr[:, 2:4],
                                    in1=b2, op=OP.mult), inc="sV")
                E("v", lambda eng: eng.drain())
                for qs in range(2):
                    E("v", lambda eng, o=kb_v[:, qs * 128:(qs + 1) * 128],
                      i=pv(aset * 3072 + qs * 129, 128),
                      sc1=scr[:, 4 + qs:5 + qs]:
                      eng.tensor_scalar(out=o, in0=i, scalar1=sc1,
                                        scalar2=None, op0=OP.mult))
                E("v", lambda eng: eng.drain())
                for qs in range(2):
                    E("v", lambda eng, o=blnd_v[:, blslot * 256 + qs * 128:
                                            blslot * 256 + (qs + 1) * 128],
                      i=pv(aset * 3072 + 512 + qs * 129, 128),
                      sc2=scr[:, 6 + qs:7 + qs],
                      t_=kb_v[:, qs * 128:(qs + 1) * 128]:
                      eng.scalar_tensor_tensor(
                          out=o, in0=i, scalar=sc2, in1=t_,
                          op0=OP.mult, op1=OP.add),
                      inc=("sV" if qs == 1 else None))
                last_blend_v = sch.cnt["sV"]
                avps_free_v[aset] = last_blend_v
                pend_tp[0] = (h_abs, b, blslot, last_blend_v)
        flush_tp()
    ao_done_v = sch.cnt["sV"]

    if "dbg_ao" in dbg_d:
        W("y", "sV", ao_done_v)
        ta_ = DMA("y", dbg_d["dbg_ao"], ao, "dT2")
        W("y", "dT2", ta_)

    abf_off = R2
    abf = vw(abf_off, 32768, BF16)
    asq = vw(RQ, 1024, BF16)
    evac_done = [0, 0]
    asq_free = 0
    for oc in range(NCH):
        slot = oc % 2
        wssem = "dWS0" if slot == 0 else "dWS1"
        W("g", "sT", ws_freed[slot])
        ws_t[slot] = DMA("g", vw(ws_off[slot], 4096, BF16), wo[oc], wssem)
        W("t", wssem, ws_t[slot])
        if oc == 0:
            W("t", "sV", ao_done_v)
        W("t", "sS", evac_done[slot])
        base = slot * 1024
        for nh in range(2):
            for ic in range(NCH):
                E("t", lambda eng, o=pv(base + nh * 512, 512),
                  l=vw(ws_off[slot] + ic * 256, 128, BF16),
                  r=ao[:, ic * 1024 + nh * 512: ic * 1024 + (nh + 1) * 512],
                  st=(ic == 0), sp=(ic == 31):
                  eng.matmul(o, l, r, start=st, stop=sp),
                  inc=("sT" if (nh == 1 and ic == 31) else None))
        ws_freed[slot] = sch.cnt["sT"]
        wo_done = sch.cnt["sT"]
        W("s", "sT", wo_done)
        W("s", "sT", asq_free)
        E("s", lambda eng, o=abf[:, oc * 1024:(oc + 1) * 1024],
          i=pv(base, 1024):
          eng.activation(out=o, in_=i, func=AF.Copy, bias=0.0, scale=1.0))
        E("s", lambda eng, i=pv(base, 1024):
          eng.activation(out=asq, in_=i, func=AF.Square, bias=0.0, scale=1.0),
          inc="sS")
        evac_done[slot] = sch.cnt["sS"]
        W("t", "sS", evac_done[slot])
        for th in range(2):
            E("t", lambda eng, o=pv(2048 + th * 512, 512, 0, 1),
              r=abf[:, oc * 1024 + th * 512: oc * 1024 + (th + 1) * 512],
              st=(oc == 0), sp=(oc == 31):
              eng.matmul(o, onesc_v, r, start=st, stop=sp))
        for th in range(2):
            E("t", lambda eng, o=pv(3072 + th * 512, 512, 0, 1),
              r=asq[:, th * 512:(th + 1) * 512], st=(oc == 0), sp=(oc == 31):
              eng.matmul(o, onesc_v, r, start=st, stop=sp),
              inc=("sT" if th == 1 else None))
        asq_free = sch.cnt["sT"]

    if "dbg_a" in dbg_d:
        W("y", "sS", max(evac_done))
        tb_ = DMA("y", dbg_d["dbg_a"], abf, "dT3")
        W("v", "dT3", tb_)

    def ln_rows():
        W("v", "sT", sch.cnt["sT"])
        E("v", lambda eng: eng.tensor_scalar(
            out=stat[:, 0:1024], in0=pv(2048, 1024, 0, 1),
            scalar1=1.0 / 4096, scalar2=None, op0=OP.mult), inc="sV")
        W("s", "sV", sch.cnt["sV"])
        E("s", lambda eng: eng.activation(
            out=stat[:, 1024:2048], in_=stat[:, 0:1024], func=AF.Square,
            bias=0.0, scale=1.0), inc="sS")
        W("v", "sS", sch.cnt["sS"])
        E("v", lambda eng: eng.scalar_tensor_tensor(
            out=stat[:, 1024:2048], in0=pv(3072, 1024, 0, 1),
            scalar=1.0 / 4096, in1=stat[:, 1024:2048],
            op0=OP.mult, op1=OP.subtract), inc="sV")
        W("s", "sV", sch.cnt["sV"])
        E("s", lambda eng: eng.activation(
            out=stat[:, 1024:2048], in_=stat[:, 1024:2048], func=AF.Sqrt,
            bias=eps_ap, scale=1.0), inc="sS")
        W("v", "sS", sch.cnt["sS"])
        E("v", lambda eng: eng.reciprocal(stat[:, 1024:2048],
                                          stat[:, 1024:2048]))
        E("v", lambda eng: eng.drain())
        E("v", lambda eng: eng.tensor_tensor(
            out=stat[:, 0:1024], in0=stat[:, 0:1024], in1=stat[:, 1024:2048],
            op=OP.mult), inc="sV")
        return sch.cnt["sV"]

    def bcast_rows(rstd_off, mr_off):
        W("t", "sV", sch.cnt["sV"])
        for k in range(2):
            E("t", lambda eng, o=pv(rstd_off + k * 512, 512),
              r=stat[:, 1024 + k * 512:1024 + (k + 1) * 512]:
              eng.matmul(o, ones1_v, r, start=True, stop=True))
        for k in range(2):
            E("t", lambda eng, o=pv(mr_off + k * 512, 512),
              r=stat[:, k * 512:(k + 1) * 512]:
              eng.matmul(o, ones1_v, r, start=True, stop=True),
              inc=("sT" if k == 1 else None))
        return sch.cnt["sT"]

    wo_all = max(ws_freed)
    asq_last = asq_free
    ln_rows()
    bcD = bcast_rows(0, 1024)

    xrl_off = [R1 + i * 4096 for i in range(4)]
    x2o_off = [R1 + 16384 + i * 4096 for i in range(4)]
    tmpd_off = R1 + 32768
    x2bf_off = R1 + 36864
    x2sq_off = R1 + 38912
    rl_off = [R1 + 40960 + i * 4096 for i in range(4)]
    rbD = vw(O["atA"], 1024, BF16)
    mbD = vw(O["atB"], 1024, BF16)
    W("v", "sT", bcD)
    E("v", lambda eng: eng.tensor_copy(rbD, pv(0, 1024)))
    E("v", lambda eng: eng.tensor_copy(mbD, pv(1024, 1024)))
    E("v", lambda eng: eng.drain(), inc="sV")
    bcD_bf = sch.cnt["sV"]
    xrl_t = [0] * 4
    x2_done = [0] * 32
    x2s_done = [0, 0]
    x2stats_free = 0
    spill_t = [None] * 32
    for oc in range(NCH):
        slot = oc % 2
        xslot = oc % 4
        W("y", "sT", wo_all)
        if oc >= 4:
            W("y", "sV", x2_done[oc - 4])
        xrsem = "dXR%d" % xslot
        for tcq in range(4):
            xrl_t[xslot] = DMA(
                "y", vw(xrl_off[xslot] + tcq * 1024, 256, F32),
                xh[tcq + 1, :, oc, :], xrsem)
        W("v", xrsem, xrl_t[xslot])
        W("v", "sV", bcD_bf)
        if oc >= 4:
            W("v", "dSP%d" % xslot, spill_t[oc - 4])
        if oc >= 2:
            W("v", "sS", x2s_done[slot])
        tmpd = vw(tmpd_off, 1024, BF16)
        x2o = vw(x2o_off[xslot], 1024, F32)
        E("v", lambda eng: eng.drain())
        E("v", lambda eng, o=tmpd, i=abf[:, oc * 1024:(oc + 1) * 1024]:
          eng.tensor_tensor(out=o, in0=i, in1=rbD, op=OP.mult))
        E("v", lambda eng: eng.drain())
        E("v", lambda eng, o=tmpd:
          eng.tensor_tensor(out=o, in0=o, in1=mbD, op=OP.subtract))
        E("v", lambda eng: eng.drain())
        E("v", lambda eng, o=x2o, t_=tmpd, x_=vw(xrl_off[xslot], 1024, F32):
          eng.tensor_tensor(out=o, in0=t_, in1=x_, op=OP.add), inc="sV")
        x2_done[oc] = sch.cnt["sV"]
        W("g", "sV", x2_done[oc])
        spill_t[oc] = DMA("g", x2sp[oc], x2o, "dSP%d" % xslot)
        W("s", "sV", x2_done[oc])
        W("s", "sT", x2stats_free)
        E("s", lambda eng, o=vw(x2bf_off, 1024, BF16), i=x2o:
          eng.activation(out=o, in_=i, func=AF.Copy, bias=0.0, scale=1.0))
        E("s", lambda eng, o=vw(x2sq_off, 1024, BF16), i=x2o:
          eng.activation(out=o, in_=i, func=AF.Square, bias=0.0, scale=1.0),
          inc="sS")
        x2s_done[slot] = sch.cnt["sS"]
        W("t", "sS", x2s_done[slot])
        for th in range(2):
            E("t", lambda eng, o=pv(2048 + th * 512, 512, 0, 1),
              r=vw(x2bf_off + th * 1024, 512, BF16), st=(oc == 0),
              sp=(oc == 31):
              eng.matmul(o, onesc_v, r, start=st, stop=sp))
        for th in range(2):
            E("t", lambda eng, o=pv(3072 + th * 512, 512, 0, 1),
              r=vw(x2sq_off + th * 1024, 512, BF16), st=(oc == 0),
              sp=(oc == 31):
              eng.matmul(o, onesc_v, r, start=st, stop=sp),
              inc=("sT" if th == 1 else None))
        x2stats_free = sch.cnt["sT"]

    ln_rows()
    bc2 = bcast_rows(2048, 3072)

    h2_off = R2
    h2 = vw(h2_off, 32768, BF16)
    h2_done = [0] * 32
    for oc in range(NCH):
        rlsem = "dRL%d" % (oc % 4)
        if oc >= 4:
            W("y", "sV", h2_done[oc - 4])
        W("y", "dSP%d" % (oc % 4), spill_t[oc])
        rlt = DMA("y", vw(rl_off[oc % 4], 1024, F32), x2sp[oc], rlsem)
        W("v", rlsem, rlt)
        W("v", "sT", bc2)
        tmpd = vw(tmpd_off, 1024, F32)
        E("v", lambda eng: eng.drain())
        E("v", lambda eng, o=tmpd, x_=vw(rl_off[oc % 4], 1024, F32):
          eng.tensor_tensor(out=o, in0=x_, in1=pv(2048, 1024), op=OP.mult))
        E("v", lambda eng: eng.drain())
        E("v", lambda eng, o=h2[:, oc * 1024:(oc + 1) * 1024], t_=tmpd:
          eng.tensor_tensor(out=o, in0=t_, in1=pv(3072, 1024),
                            op=OP.subtract), inc="sV")
        h2_done[oc] = sch.cnt["sV"]

    if "dbg_h2" in dbg_d:
        W("y", "sV", h2_done[31])
        tc_ = DMA("y", dbg_d["dbg_h2"], h2, "dT4")
        W("v", "dT4", tc_)

    ntneg = vw(RQ, 4096, BF16)
    n4_off = RQ + 8192
    w1_off = [O["wk"], O["wv"]]
    hsb = vw(O["atA"], 1024, BF16)
    fsq_off = O["atB"]
    fbf_off = R1
    fbf = vw(fbf_off, 32768, BF16)

    W("g", "sV", ao_done_v)
    W("g", "sT", asq_last)
    nt_t = DMA("g", ntneg, nt, "dNT")
    W("v", "dNT", nt_t)
    E("v", lambda eng: eng.tensor_scalar(
        out=ntneg, in0=ntneg, scalar1=-1.0, scalar2=None, op0=OP.mult),
        inc="sV")
    ntneg_done = sch.cnt["sV"]

    for th in range(2):
        for ic in range(NCH):
            W("t", "sV", h2_done[ic])
            E("t", lambda eng, o=pv(th * 512, 512),
              l=epad_v[:, 124 - ic * 4:252 - ic * 4],
              r=h2[:, ic * 1024 + th * 512: ic * 1024 + (th + 1) * 512],
              st=(ic == 0), sp=(ic == 31):
              eng.matmul(o, l, r, start=st, stop=sp),
              inc=("sT" if (th == 1 and ic == 31) else None))
    hmm_done = sch.cnt["sT"]
    W("s", "sT", hmm_done)
    E("s", lambda eng: eng.activation(
        out=hsb, in_=pv(0, 1024), func=AF.Copy, bias=0.0, scale=3.5),
        inc="sS")
    h75_done = sch.cnt["sS"]

    us_t = [0, 0]
    unpack_done = [0, 0]
    n4_free = 0
    w1_done = {}
    fdone_oc = [0] * 32
    fevac_done = 0
    fsq_free = 0
    for oc in range(NCH):
        slot = oc % 2
        wssem = "dWS0" if slot == 0 else "dWS1"
        W("g", "sT", ws_freed[slot])
        W("g", "sV", w1_done.get((oc - 2, 7), 0))
        us_t[slot] = DMA("g", vw(ws_off[slot], 4096, U16), uq[oc], wssem)
        W("g", "sT", n4_free)
        n4t = DMA("g", vw(n4_off, 4096, BF16, 0, 4), n4[oc], "dN4")
        W("v", wssem, us_t[slot])
        if oc >= 2:
            W("v", "sT", fdone_oc[oc - 2])
        vq = vw(ws_off[slot], 4096, U16)
        vbf = vw(ws_off[slot], 4096, BF16)
        E("v", lambda eng, i=vq:
          eng.tensor_scalar(out=i, in0=i, scalar1=mask32_v, scalar2=None,
                            op0=OP.bitwise_and))
        E("v", lambda eng: eng.drain())
        E("v", lambda eng, i=vq, o=vbf:
          eng.tensor_scalar(out=o, in0=i, scalar1=inv2s_v, scalar2=None,
                            op0=OP.mult),
          inc="sV")
        E("v", lambda eng: eng.drain())
        unpack_done[slot] = sch.cnt["sV"]
        for icg in range(8):
            nb = 1024 + (icg % 2) * 512
            W("t", "dN4", n4t)
            W("t", "sV", w1_done.get((oc, icg - 2),
                                     w1_done.get((oc - 1, icg + 6), 0)))
            E("t", lambda eng, o=pv(nb, 512),
              r=vw(n4_off + icg * 1024, 512, BF16, 0, 4):
              eng.matmul(o, e4_v, r, start=True, stop=True), inc="sT")
            nexp_done = sch.cnt["sT"]
            if icg == 7:
                n4_free = nexp_done
            W("v", "sT", nexp_done)
            E("v", lambda eng, o=vw(w1_off[slot] + icg * 1024, 512, BF16),
              i=vbf[:, icg * 512:(icg + 1) * 512], n_=pv(nb, 512):
              eng.tensor_tensor(out=o, in0=i, in1=n_, op=OP.mult), inc="sV")
            w1_done[(oc, icg)] = sch.cnt["sV"]
        W("t", "sS", fevac_h0 if oc > 0 else 0)
        W("t", "sS", h75_done)
        W("t", "sV", ntneg_done)
        for th in range(2):
            if th == 1:
                W("t", "sS", fevac_done)
            for ic in range(NCH):
                W("t", "sV", w1_done[(oc, ic // 4)])
                E("t", lambda eng, o=pv(th * 512, 512),
                  l=vw(w1_off[slot] + ic * 256, 128, BF16),
                  r=h2[:, ic * 1024 + th * 512: ic * 1024 + (th + 1) * 512],
                  st=(ic == 0):
                  eng.matmul(o, l, r, start=st, stop=False))
            E("t", lambda eng, o=pv(th * 512, 512),
              l=ntneg[:, oc * 128:(oc + 1) * 128],
              r=hsb[:, th * 512:(th + 1) * 512]:
              eng.matmul(o, l, r, start=False, stop=True),
              inc=("sT" if th == 1 else None))
        fdone_oc[oc] = sch.cnt["sT"]
        for i_ in range(4):
            W("s", "dSP%d" % i_, sch.cnt["dSP%d" % i_])
        W("s", "sT", fdone_oc[oc])
        W("s", "sT", fsq_free)
        for th in range(2):
            E("s", lambda eng, o=fbf[:, oc * 1024 + th * 512:
                                     oc * 1024 + (th + 1) * 512],
              i=pv(th * 512, 512), b_=fbias_v[:, oc:oc + 1]:
              eng.activation(out=o, in_=i, func=AF.Identity, bias=b_,
                             scale=2.0 / 7))
            E("s", lambda eng, o=vw(fsq_off + th * 1024, 512, BF16),
              i=pv(th * 512, 512), b_=fbias_v[:, oc:oc + 1]:
              eng.activation(out=o, in_=i, func=AF.Square, bias=b_,
                             scale=2.0 / 7), inc="sS")
            if th == 0:
                fevac_h0 = sch.cnt["sS"]
        fevac_done = sch.cnt["sS"]
        W("t", "sS", fevac_done)
        for th in range(2):
            E("t", lambda eng, o=pv(2048 + th * 512, 512, 0, 1),
              r=fbf[:, oc * 1024 + th * 512: oc * 1024 + (th + 1) * 512],
              st=(oc == 0), sp=(oc == 31):
              eng.matmul(o, onesc_v, r, start=st, stop=sp))
        for th in range(2):
            E("t", lambda eng, o=pv(3072 + th * 512, 512, 0, 1),
              r=vw(fsq_off + th * 1024, 512, BF16), st=(oc == 0),
              sp=(oc == 31):
              eng.matmul(o, onesc_v, r, start=st, stop=sp),
              inc=("sT" if th == 1 else None))
        fsq_free = sch.cnt["sT"]
    f_all_done = fdone_oc[31]

    if "dbg_f" in dbg_d:
        W("y", "sS", fevac_done)
        DMA("y", dbg_d["dbg_f"], fbf, "dT7")

    ln_rows()
    bc3 = bcast_rows(0, 1024)

    xr2_off = [R3 + i * 4096 for i in range(4)]
    ob_off = [R2 + i * 4096 for i in range(4)]
    tmpf_off = R2 + 16384
    rb3 = vw(RQ, 1024, BF16)
    mb3 = vw(RQ + 2048, 1024, BF16)
    W("v", "sT", bc3)
    E("v", lambda eng: eng.tensor_copy(rb3, pv(0, 1024)))
    E("v", lambda eng: eng.tensor_copy(mb3, pv(1024, 1024)))
    E("v", lambda eng: eng.drain(), inc="sV")
    bc3_bf = sch.cnt["sV"]
    out_t = [0] * 4
    of_done = [0] * 32
    for oc in range(NCH):
        xslot = oc % 4
        frsem = "dFR%d" % xslot
        W("y", "sV", w1_done[(31, 7)])
        if oc >= 4:
            W("y", "sV", of_done[oc - 4])
        for i_ in range(4):
            W("y", "dSP%d" % i_, sch.cnt["dSP%d" % i_])
        rlt = DMA("y", vw(xr2_off[xslot], 1024, F32), x2sp[oc], frsem)
        W("v", frsem, rlt)
        W("v", "sV", bc3_bf)
        W("v", "sT", f_all_done)
        if oc >= 4:
            W("v", "dOUT%d" % xslot, out_t[xslot])
        tmpf = vw(tmpf_off, 1024, BF16)
        E("v", lambda eng: eng.drain())
        E("v", lambda eng, o=tmpf, i=fbf[:, oc * 1024:(oc + 1) * 1024]:
          eng.tensor_tensor(out=o, in0=i, in1=rb3, op=OP.mult))
        E("v", lambda eng: eng.drain())
        E("v", lambda eng, o=tmpf:
          eng.tensor_tensor(out=o, in0=o, in1=mb3, op=OP.subtract))
        E("v", lambda eng: eng.drain())
        E("v", lambda eng, o=vw(ob_off[xslot], 1024, F32), t_=tmpf,
          x_=vw(xr2_off[xslot], 1024, F32):
          eng.tensor_tensor(out=o, in0=t_, in1=x_, op=OP.add), inc="sV")
        of_done[oc] = sch.cnt["sV"]
        W("y", "sV", of_done[oc])
        out_t[xslot] = DMA("y", out_d[oc], vw(ob_off[xslot], 1024, F32),
                           "dOUT%d" % xslot)

    final_dbg = 0
    for i_ in range(4):
        E("y", lambda eng, s_="dOUT%d" % i_, v_=sch.cnt["dOUT%d" % i_]:
          eng.wait_ge(sch.semh[s_], v_))
    for i_ in range(8):
        if sch.cnt["dT%d" % i_]:
            E("y", lambda eng, s_="dT%d" % i_, v_=sch.cnt["dT%d" % i_]:
              eng.wait_ge(sch.semh[s_], v_))

    with nc.Block() as blk:
        @blk.sync
        def _(eng):
            sch.run("y", eng)

        @blk.gpsimd
        def _(eng):
            sch.run("g", eng)

        @blk.tensor
        def _(eng):
            sch.run("t", eng)

        @blk.vector
        def _(eng):
            sch.run("v", eng)

        @blk.scalar
        def _(eng):
            sch.run("s", eng)

    ctx.close()
    return nc, list(dbg_d.keys())


def kernel(L0_attn_x, L0_attn_Wq, L0_attn_Wk, L0_attn_Wv, L0_attn_Wo,
           L0_ffn_weight_q3, L0_ffn_weight_norm, L0_ffn_bias):
    x = np.asarray(L0_attn_x, dtype=np.float32)
    shared = _prep_shared(np.asarray(L0_attn_Wq, np.float32),
                          np.asarray(L0_attn_Wk, np.float32),
                          np.asarray(L0_attn_Wv, np.float32),
                          np.asarray(L0_attn_Wo, np.float32),
                          np.asarray(L0_ffn_weight_q3),
                          np.asarray(L0_ffn_weight_norm, np.float32),
                          np.asarray(L0_ffn_bias, np.float32))
    nc, taps = build_nc(tuple(_dbg_taps))
    in_maps = []
    for c in range(NCORES):
        m = dict(shared)
        m.update(_prep_core(x, c))
        in_maps.append(m)
    res = run_bass_kernel_spmd(nc, in_maps, core_ids=list(range(NCORES)))
    out = np.empty((B, S, D), dtype=np.float32)
    for c in range(NCORES):
        b = c // 4
        T0 = 1024 * (c % 4)
        oc = res.results[c]["out"]
        out[b, T0:T0 + 1024, :] = np.ascontiguousarray(
            oc.transpose(2, 0, 1)).reshape(1024, 4096)
    if taps:
        kernel.dbg = [{t: res.results[c][t] for t in taps}
                      for c in range(NCORES)]
    return out


# version 12
# speedup vs baseline: 1.2712x; 1.0123x over previous
import math
import contextlib

import numpy as np
import ml_dtypes

import bass_rust
import concourse.bass as bass
import concourse.mybir as mybir
from concourse.bass_utils import run_bass_kernel_spmd

F32 = mybir.dt.float32
BF16 = mybir.dt.bfloat16
U16 = mybir.dt.uint16
I32 = mybir.dt.int32
U8 = mybir.dt.uint8
OP = mybir.AluOpType
AF = mybir.ActivationFunctionType
BF = ml_dtypes.bfloat16

B, S, D = 2, 4096, 4096
H, DH = 32, 128
SCALE = 1.0 / math.sqrt(DH)
NCORES = 8
TOK = 1024
HTOK = 1536
NCH = 32
EPS = 1e-5

HEAP_BYTES = 212832


def _al(x, a=32):
    return (x + a - 1) // a * a


class _Off:
    def __init__(self, base=0):
        self.cur = base
        self.map = {}

    def add(self, name, nbytes):
        self.map[name] = self.cur
        self.cur = _al(self.cur + nbytes)
        return self.map[name]


_P = _Off()
_P.add("cos", 1536 * 2)
_P.add("sin", 1536 * 2)
_P.add("wk", 4096 * 2)
_P.add("wv", 4096 * 2)
_P.add("kr", 1536 * 2)
_P.add("vext", 1548 * 2)
_P.add("rsign", 128 * 2)
_P.add("idn", 128 * 2)
_P.add("epad", 256 * 2)
_P.add("e4", 128 * 2)
_P.add("ones1", 128 * 4)
_P.add("onesc", 32)
_P.add("alham", 8 * 4)
_P.add("alpha", 8 * 4)
_P.add("maskm", 256 * 2)
_P.add("onesr", 256 * 2)
_P.add("fbias", 32 * 4)
_P.add("consts", 8 * 4)
_P.add("shiftv", 64)
_P.add("stat", 2048 * 4)
_P.add("scr", 16 * 4)
_P.add("kb", 256 * 2)
_P.add("blnd", 512 * 2)
_P.add("atA", 1024 * 2)
_P.add("atB", 1024 * 2)
_P.add("qb", 1024 * 2)
PERSIST_END = _P.cur

R3 = _al(PERSIST_END)
R3_SZ = 2 * 8192
R1 = _al(R3 + R3_SZ)
R1_SZ = 65536
R2 = _al(R1 + R1_SZ)
R2_SZ = 65536
RQ = _al(R2 + R2_SZ)
RQ_SZ = 16384
assert RQ + RQ_SZ <= HEAP_BYTES, (RQ + RQ_SZ, HEAP_BYTES)

_dbg_taps = []


def _rope_cache():
    inv_freq = 1.0 / 10000.0 ** (np.arange(0, DH, 2, dtype=np.float32) / DH)
    t = np.arange(S, dtype=np.float32)
    f = np.outer(t, inv_freq)
    cos = np.zeros((S, DH), dtype=np.float32)
    sin = np.zeros((S, DH), dtype=np.float32)
    cos[:, 0::2] = np.cos(f)
    cos[:, 1::2] = np.cos(f)
    sin[:, 0::2] = np.sin(f)
    sin[:, 1::2] = np.sin(f)
    return cos, sin


def _prep_shared(Wq, Wk, Wv, Wo, wq3, wnorm, fbias):
    d = {}
    d["wq"] = np.ascontiguousarray(
        Wq.reshape(32, 128, 32, 128).transpose(0, 3, 2, 1))
    d["wo"] = np.ascontiguousarray(
        Wo.reshape(32, 128, 32, 128).transpose(0, 3, 2, 1))
    d["wk"] = np.ascontiguousarray(Wk.reshape(128, 32, 128).transpose(2, 1, 0))
    d["wv"] = np.ascontiguousarray(Wv.reshape(128, 32, 128).transpose(2, 1, 0))

    pk = np.ascontiguousarray(wq3).view(np.uint8)
    pk13 = np.concatenate(
        [pk, np.zeros((pk.shape[0], 1), np.uint8)], axis=1)
    j = np.arange(32)
    bidx = (3 * j) // 8
    lo = pk13[:, bidx]
    hi = pk13[:, bidx + 1]
    b2 = np.stack([lo, hi], axis=-1)
    u16 = np.ascontiguousarray(b2).reshape(-1, 64).view(np.uint16)
    u16oi = u16.reshape(4096, 4096)
    d["uq"] = np.ascontiguousarray(
        u16oi.reshape(32, 128, 32, 128).transpose(0, 3, 2, 1))

    n_og = wnorm.reshape(4096, 128)
    d["n4"] = np.ascontiguousarray(
        n_og.reshape(32, 128, 32, 4).transpose(0, 3, 2, 1))
    d["nt"] = np.ascontiguousarray(n_og.T)
    d["fbias"] = np.ascontiguousarray(fbias.reshape(32, 128).T)

    e4 = np.zeros((4, 128), dtype=BF)
    for g in range(4):
        e4[g, g * 32:(g + 1) * 32] = 1
    d["e4"] = e4
    epad = np.zeros((128, 256), dtype=BF)
    p = np.arange(128)
    epad[p, 124 + p // 32] = 1
    d["epad"] = epad
    rsign = np.zeros((128, 128), dtype=np.float32)
    m = np.arange(64)
    rsign[m + 64, m] = -1.0
    rsign[m, m + 64] = 1.0
    d["rsign"] = rsign.astype(BF)
    d["idn"] = np.eye(128, dtype=BF)
    d["ones1"] = np.ones((1, 128), np.float32)
    d["onesc"] = np.ones((128, 1), dtype=BF)
    d["onesr"] = np.ones((1, 256), dtype=BF)
    consts = np.zeros((128, 8), np.float32)
    consts[:, 0] = EPS
    d["consts"] = consts
    s_ = (3 * (np.arange(128) % 32)) % 8
    d["mask32"] = (7 << s_).astype(np.int32).reshape(128, 1)
    d["inv2s"] = (0.5 ** s_).astype(np.float32).reshape(128, 1)
    return d


def _prep_core(x, core):
    b = core // 4
    T0 = 1024 * (core % 4)
    lo = T0 - 256
    xh = np.zeros((HTOK, D), dtype=np.float32)
    s0 = max(lo, 0)
    s1 = min(T0 + 1280, S)
    xh[s0 - lo:s1 - lo] = x[b, s0:s1]
    xh_arr = np.ascontiguousarray(
        xh.reshape(6, 256, 32, 128).transpose(0, 3, 2, 1))

    cos, sin = _rope_cache()
    pos = np.clip(np.arange(lo, lo + HTOK), 0, S - 1)
    cosT = np.ascontiguousarray(cos[pos].T)
    sinT = np.ascontiguousarray(sin[pos].T)

    xr = np.ascontiguousarray(
        x[b, T0:T0 + 1024].reshape(1024, 32, 128).transpose(1, 2, 0))

    alpha = np.zeros((128, 8), np.float32)
    alham = np.zeros((128, 8), np.float32)
    for bl in range(4):
        gb = T0 // 256 + bl
        if gb == 0:
            a = np.ones(256, np.float32)
        else:
            a = np.linspace(0.0, 1.0, 256).astype(np.float32)
        for qs in range(2):
            alpha[:, 2 * bl + qs] = a[qs * 128:(qs + 1) * 128]
            alham[:, 2 * bl + qs] = (1.0 - a)[qs * 128:(qs + 1) * 128]

    maskm = np.zeros((1, 256), dtype=BF)
    if core % 4 == 3:
        maskm[:] = np.float32(-30.0 / SCALE).astype(BF)
    return dict(xh=xh_arr, xr=xr, cosT=cosT, sinT=sinT, alpha=alpha,
                alham=alham, maskm=maskm)


class Sched:
    ENGINES = ("y", "g", "t", "v", "s")

    def __init__(self):
        self.q = {e: [] for e in self.ENGINES}
        self.cnt = {}
        self.hw = {}
        self.semh = {}

    def new_sem(self, name, handle):
        self.semh[name] = handle
        self.cnt[name] = 0
        for e in self.ENGINES:
            self.hw[(e, name)] = 0

    def wait(self, e, sem, val):
        if val is None or val <= 0:
            return
        if self.hw[(e, sem)] >= val:
            return
        self.hw[(e, sem)] = val
        h = self.semh[sem]
        self.q[e].append(lambda eng, h=h, v=val: eng.wait_ge(h, v))

    def emit(self, e, fn, inc=None, k=1):
        if inc is None:
            self.q[e].append(fn)
            return None
        self.cnt[inc] += k
        tgt = self.cnt[inc]
        h = self.semh[inc]

        def wrapped(eng, fn=fn, h=h, k=k):
            fn(eng).then_inc(h, k)
        self.q[e].append(wrapped)
        return tgt

    def dma(self, e, out_ap, in_ap, sem):
        return self.emit(
            e, lambda eng, o=out_ap, i=in_ap: eng.dma_start(out=o, in_=i),
            inc=sem, k=16)

    def run(self, e, eng):
        for fn in self.q[e]:
            fn(eng)


def _bc(ap, reps, inner):
    pat = [list(p) for p in ap.ap]
    assert len(pat) == 2, pat
    return bass_rust.AP(ap.tensor, ap.offset, [pat[0], [0, reps], pat[1]])


def _strided(ap, elem_off, dims):
    pat = [list(p) for p in ap.ap]
    return bass_rust.AP(ap.tensor, ap.offset + elem_off,
                        [pat[0]] + [list(d) for d in dims])


def build_nc(debug_taps=()):
    nc = bass.Bass("TRN2", target_bir_lowering=False)

    di = {}

    def inp(name, shape, dt):
        di[name] = nc.dram_tensor(name, list(shape), dt,
                                  kind="ExternalInput").ap()
        return di[name]

    xh = inp("xh", (6, 128, 32, 256), F32)
    xr = inp("xr", (32, 128, 1024), F32)
    wq = inp("wq", (32, 128, 32, 128), F32)
    wo = inp("wo", (32, 128, 32, 128), F32)
    wk = inp("wk", (128, 32, 128), F32)
    wv = inp("wv", (128, 32, 128), F32)
    uq = inp("uq", (32, 128, 32, 128), U16)
    n4 = inp("n4", (32, 4, 32, 128), F32)
    nt = inp("nt", (128, 4096), F32)
    fbias_d = inp("fbias", (128, 32), F32)
    cosT = inp("cosT", (128, 1536), F32)
    sinT = inp("sinT", (128, 1536), F32)
    alpha = inp("alpha", (128, 8), F32)
    alham = inp("alham", (128, 8), F32)
    maskm = inp("maskm", (1, 256), BF16)
    rsign = inp("rsign", (128, 128), BF16)
    idn = inp("idn", (128, 128), BF16)
    epad = inp("epad", (128, 256), BF16)
    e4 = inp("e4", (4, 128), BF16)
    ones1 = inp("ones1", (1, 128), F32)
    onesc = inp("onesc", (128, 1), BF16)
    onesr = inp("onesr", (1, 256), BF16)
    consts = inp("consts", (128, 8), F32)
    mask32 = inp("mask32", (128, 1), I32)
    inv2s = inp("inv2s", (128, 1), F32)

    out_d = nc.dram_tensor("out", [32, 128, 1024], F32,
                           kind="ExternalOutput").ap()
    x2sp = nc.dram_tensor("x2sp", [32, 128, 1024], F32, kind="Internal").ap()
    dbg_d = {}
    for tap, shape, dt in (
        ("dbg_h", [128, 32768], BF16), ("dbg_kr", [128, 1536], BF16),
        ("dbg_vext", [128, 1548], BF16), ("dbg_qr", [128, 4096], BF16),
        ("dbg_ao", [128, 32768], BF16), ("dbg_a", [128, 32768], BF16),
        ("dbg_h2", [128, 32768], BF16), ("dbg_f", [128, 32768], BF16),
    ):
        if tap in debug_taps:
            dbg_d[tap] = nc.dram_tensor(tap, shape, dt,
                                        kind="ExternalOutput").ap()

    ctx = contextlib.ExitStack()
    heap = ctx.enter_context(nc.sbuf_tensor("heap", [128, HEAP_BYTES], U8))
    P = ctx.enter_context(nc.psum_tensor("psum", [128, 4096], F32))
    PB = P.bitcast(BF16)

    sch = Sched()
    for s in ("sT", "sV", "sS", "sG", "dCN", "dW0", "dXB0", "dXB1",
              "dWS0", "dWS1", "dN4", "dNT", "dXR0", "dXR1", "dSP0", "dSP1",
              "dFR0", "dFR1", "dOUT0", "dOUT1", "dDBG"):
        sch.new_sem(s, ctx.enter_context(nc.semaphore(s)))
    for i_ in range(8):
        sch.new_sem("dT%d" % i_, ctx.enter_context(nc.semaphore("dT%d" % i_)))

    def vw(off, n, dt, p0=0, p1=128):
        sz = mybir.dt.size(dt)
        return heap[p0:p1, off:off + n * sz].bitcast(dt)

    def pv(elem_off, n, p0=0, p1=128):
        return P[p0:p1, elem_off:elem_off + n]

    O = _P.map
    W = sch.wait
    E = sch.emit
    DMA = sch.dma

    cos_v = vw(O["cos"], 1536, BF16)
    sin_v = vw(O["sin"], 1536, BF16)
    wk_v = vw(O["wk"], 4096, BF16)
    wv_v = vw(O["wv"], 4096, BF16)
    kr_v = vw(O["kr"], 1536, BF16)
    vext_v = vw(O["vext"], 1548, BF16)
    rsign_v = vw(O["rsign"], 128, BF16)
    idn_v = vw(O["idn"], 128, BF16)
    epad_v = vw(O["epad"], 256, BF16)
    e4_v = vw(O["e4"], 128, BF16, 0, 4)
    ones1_v = vw(O["ones1"], 128, F32, 0, 1)
    onesc_v = vw(O["onesc"], 1, BF16)
    alham_v = vw(O["alham"], 8, F32)
    alpha_v = vw(O["alpha"], 8, F32)
    maskm_v = vw(O["maskm"], 256, BF16, 0, 1)
    onesr_v = vw(O["onesr"], 256, BF16, 0, 1)
    fbias_v = vw(O["fbias"], 32, F32)
    consts_v = vw(O["consts"], 8, F32)
    mask32_v = vw(O["shiftv"], 1, I32)
    inv2s_v = vw(O["shiftv"] + 32, 1, F32)
    eps_ap = vw(O["consts"], 8, F32, 0, 1)[:, 0:1]
    zero_ap = consts_v[:, 1:2]
    stat = vw(O["stat"], 2048, F32, 0, 1)
    scr = vw(O["scr"], 16, F32)
    kb_v = vw(O["kb"], 256, BF16)
    blnd_v = vw(O["blnd"], 512, BF16)
    atA_v = vw(O["atA"], 1024, BF16)
    atB_v = vw(O["atB"], 1024, BF16)
    qb_v = vw(O["qb"], 1024, BF16)

    for name, v_ in (("rsign", rsign_v),
                     ("idn", idn_v), ("epad", epad_v), ("e4", e4_v),
                     ("ones1", ones1_v), ("onesc", onesc_v),
                     ("alham", alham_v), ("alpha", alpha_v),
                     ("maskm", maskm_v), ("onesr", onesr_v),
                     ("fbias", fbias_v),
                     ("consts", consts_v), ("mask32", mask32_v),
                     ("inv2s", inv2s_v)):
        DMA("y", v_, di[name], "dCN")
    c_consts = sch.cnt["dCN"]
    DMA("g", cos_v, di["cosT"], "dW0")
    DMA("g", sin_v, di["sinT"], "dW0")
    DMA("g", wk_v, wk, "dW0")
    DMA("g", wv_v, wv, "dW0")
    w0_all = sch.cnt["dW0"]
    for kc in range(12):
        E("v", lambda eng, a=vext_v[:, kc * 129 + 128: kc * 129 + 129]:
          eng.memset(a, 1.0), inc=("sV" if kc == 11 else None))

    xb_off = [R1, R1 + 16384]
    xsq_off = R1 + 32768
    htmp_off = R1 + 49152
    hown_off = R2
    hown = vw(hown_off, 32768, BF16)

    xb_t = [None] * 6
    norm_done = [0] * 6
    kv_done = [0] * 6
    rope_done = [0] * 6
    mvar_done = [0] * 6
    sumsq_done = [0] * 6

    for e_ in ("t", "v", "s"):
        W(e_, "dCN", c_consts)
    for tc in range(6):
        xb = vw(xb_off[tc % 2], 8192, BF16)
        xsq = vw(xsq_off, 8192, BF16)
        xbsem = "dXB0" if tc % 2 == 0 else "dXB1"
        if tc >= 2:
            W("g", "sV", norm_done[tc - 2])
        xb_t[tc] = DMA("g", xb, xh[tc], xbsem)
        W("s", xbsem, xb_t[tc])
        if tc >= 1:
            W("s", "sT", sumsq_done[tc - 1])
        E("s", lambda eng, o=xsq, i=xb: eng.activation(
            out=o, in_=i, func=AF.Square, bias=0.0, scale=1.0), inc="sS")
        sq_ready = sch.cnt["sS"]
        W("t", xbsem, xb_t[tc])
        if tc >= 1:
            W("t", "sV", mvar_done[tc - 1])
        for ic in range(NCH):
            E("t", lambda eng, o=pv(0, 256, 0, 1), l=onesc_v,
              r=xb[:, ic * 256:(ic + 1) * 256], st=(ic == 0), sp=(ic == 31):
              eng.matmul(o, l, r, start=st, stop=sp))
        W("t", "sS", sq_ready)
        for ic in range(NCH):
            E("t", lambda eng, o=pv(512, 256, 0, 1), l=onesc_v,
              r=xsq[:, ic * 256:(ic + 1) * 256], st=(ic == 0), sp=(ic == 31):
              eng.matmul(o, l, r, start=st, stop=sp),
              inc=("sT" if ic == 31 else None))
        stats_done = sch.cnt["sT"]
        sumsq_done[tc] = stats_done
        W("v", "sT", stats_done)
        E("v", lambda eng: eng.tensor_scalar(
            out=stat[:, 0:256], in0=pv(0, 256, 0, 1), scalar1=1.0 / 4096,
            scalar2=None, op0=OP.mult), inc="sV")
        W("s", "sV", sch.cnt["sV"])
        E("s", lambda eng: eng.activation(
            out=stat[:, 256:512], in_=stat[:, 0:256], func=AF.Square,
            bias=0.0, scale=1.0), inc="sS")
        W("v", "sS", sch.cnt["sS"])
        E("v", lambda eng: eng.scalar_tensor_tensor(
            out=stat[:, 512:768], in0=pv(512, 256, 0, 1), scalar=1.0 / 4096,
            in1=stat[:, 256:512], op0=OP.mult, op1=OP.subtract), inc="sV")
        W("s", "sV", sch.cnt["sV"])
        E("s", lambda eng: eng.activation(
            out=stat[:, 512:768], in_=stat[:, 512:768], func=AF.Sqrt,
            bias=eps_ap, scale=1.0), inc="sS")
        W("v", "sS", sch.cnt["sS"])
        E("v", lambda eng: eng.reciprocal(stat[:, 512:768], stat[:, 512:768]))
        E("v", lambda eng: eng.drain())
        E("v", lambda eng: eng.tensor_tensor(
            out=stat[:, 0:256], in0=stat[:, 0:256], in1=stat[:, 512:768],
            op=OP.mult), inc="sV")
        mvar_done[tc] = sch.cnt["sV"]
        bb = 1024 if tc % 2 == 0 else 1536
        W("t", "sV", mvar_done[tc])
        if tc >= 2:
            W("t", "sV", norm_done[tc - 2])
        E("t", lambda eng, o=pv(bb, 256), r=stat[:, 512:768]:
          eng.matmul(o, ones1_v, r, start=True, stop=True))
        E("t", lambda eng, o=pv(bb + 256, 256), r=stat[:, 0:256]:
          eng.matmul(o, ones1_v, r, start=True, stop=True), inc="sT")
        bcast_done = sch.cnt["sT"]
        W("v", "sT", bcast_done)
        if tc in (1, 2, 3, 4):
            hdst = _strided(hown, (tc - 1) * 256, [[1024, 32], [1, 256]])
        else:
            hch = vw(htmp_off, 8192, BF16)
            hdst = hch.rearrange("p (c t) -> p c t", c=32)
        xb3 = xb.rearrange("p (c t) -> p c t", c=32)
        E("v", lambda eng, o=hdst, i=xb3, b_=_bc(pv(bb, 256), 32, 256):
          eng.tensor_tensor(out=o, in0=i, in1=b_, op=OP.mult))
        E("v", lambda eng: eng.drain())
        E("v", lambda eng, o=hdst, b_=_bc(pv(bb + 256, 256), 32, 256):
          eng.tensor_tensor(out=o, in0=o, in1=b_, op=OP.subtract), inc="sV")
        norm_done[tc] = sch.cnt["sV"]

        def hsl(ic, lo=0, n=256, tc=tc):
            if tc in (1, 2, 3, 4):
                return vw(hown_off + ic * 2048 + (tc - 1) * 512 + lo * 2,
                          n, BF16)
            return vw(htmp_off + ic * 512 + lo * 2, n, BF16)

        W("t", "sV", norm_done[tc])
        if tc >= 1:
            W("t", "sV", rope_done[tc - 1])
            W("t", "sS", kv_done[tc - 1])
        W("t", "dW0", w0_all)
        for ic in range(NCH):
            E("t", lambda eng, o=pv(2048, 256),
              l=wk_v[:, ic * 128:(ic + 1) * 128], r=hsl(ic),
              st=(ic == 0), sp=(ic == 31):
              eng.matmul(o, l, r, start=st, stop=sp),
              inc=("sT" if ic == 31 else None))
        k_done = sch.cnt["sT"]
        W("s", "sT", k_done)
        E("s", lambda eng: eng.activation(
            out=kb_v, in_=pv(2048, 256), func=AF.Copy, bias=0.0, scale=1.0),
            inc="sS")
        W("t", "sS", sch.cnt["sS"])
        E("t", lambda eng: eng.matmul(
            pv(2560, 256), rsign_v, kb_v, start=True, stop=True), inc="sT")
        rk_done = sch.cnt["sT"]
        W("v", "sT", rk_done)
        cslc = cos_v[:, tc * 256:(tc + 1) * 256]
        sslc = sin_v[:, tc * 256:(tc + 1) * 256]
        krd = kr_v[:, tc * 256:(tc + 1) * 256]
        E("v", lambda eng, o=krd, c=cslc: eng.tensor_tensor(
            out=o, in0=pv(2048, 256), in1=c, op=OP.mult))
        E("v", lambda eng, s_=sslc: eng.tensor_tensor(
            out=kb_v, in0=pv(2560, 256), in1=s_, op=OP.mult))
        E("v", lambda eng: eng.drain())
        E("v", lambda eng, o=krd: eng.tensor_tensor(
            out=o, in0=o, in1=kb_v, op=OP.add), inc="sV")
        rope_done[tc] = sch.cnt["sV"]
        W("t", "dW0", w0_all)
        for qs in range(2):
            for ic in range(NCH):
                E("t", lambda eng, o=pv(3072 + qs * 128, 128),
                  l=hsl(ic, qs * 128, 128),
                  r=wv_v[:, ic * 128:(ic + 1) * 128],
                  st=(ic == 0), sp=(ic == 31):
                  eng.matmul(o, l, r, start=st, stop=sp),
                  inc=("sT" if (ic == 31 and qs == 1) else None))
        v_done = sch.cnt["sT"]
        W("s", "sT", v_done)
        for qs in range(2):
            kc = tc * 2 + qs
            E("s", lambda eng, o=vext_v[:, kc * 129:kc * 129 + 128],
              i=pv(3072 + qs * 128, 128):
              eng.activation(out=o, in_=i, func=AF.Copy, bias=0.0, scale=1.0),
              inc=("sS" if qs == 1 else None))
        kv_done[tc] = sch.cnt["sS"]

    if "dbg_h" in dbg_d:
        W("y", "sV", norm_done[4])
        th_ = DMA("y", dbg_d["dbg_h"], hown, "dT1")
        W("s", "dT1", th_)
    if "dbg_kr" in dbg_d:
        W("y", "sV", rope_done[5])
        DMA("y", dbg_d["dbg_kr"], kr_v, "dT5")
    if "dbg_vext" in dbg_d:
        W("y", "sS", kv_done[5])
        DMA("y", dbg_d["dbg_vext"], vext_v, "dT6")

    ao_off = R1
    ao = vw(ao_off, 32768, BF16)
    qr_off = [RQ, RQ + 8192]
    ws_off = [R3, R3 + 8192]
    ws_t = [None, None]
    ws_freed = [0, 0]
    qps_free = [[0, 0], [0, 0]]
    rqps_free = [0, 0]
    qb_free = [0, 0]
    sig_done = {}
    sps_free = [0, 0, 0]
    avps_free_v = [0, 0]
    tps_free = [0, 0]
    tp_prev = [0, 0]
    pend_tp = [None]
    unit_ctr = 0

    def flush_tp():
        if pend_tp[0] is None:
            return
        h_abs, b, blslot, blv = pend_tp[0]
        pend_tp[0] = None
        tslot = h_abs % 2
        W("t", "sV", blv)
        W("t", "sV", tps_free[tslot])
        tbase = 2048 + tslot * 256
        for qs in range(2):
            E("t", lambda eng, o=PB[:, tbase + qs * 128:
                                    tbase + (qs + 1) * 128],
              i=blnd_v[:, blslot * 256 + qs * 128:
                       blslot * 256 + (qs + 1) * 128]:
              eng.transpose(o, i, idn_v),
              inc=("sT" if qs == 1 else None))
        tp_prev[blslot] = sch.cnt["sT"]
        tp_done = sch.cnt["sT"]
        W("v", "sT", tp_done)
        E("v", lambda eng, o=ao[:, h_abs * 1024 + b * 256:
                                h_abs * 1024 + (b + 1) * 256],
          i=PB[:, tbase:tbase + 256]:
          eng.tensor_copy(o, i), inc="sV")
        tps_free[tslot] = sch.cnt["sV"]
    last_blend_v = 0
    last_blend_s = 0

    for hg in range(8):
        heads = list(range(hg * 4, hg * 4 + 4))
        qrv = vw(qr_off[hg % 2], 4096, BF16)
        for j, oc in enumerate(heads):
            slot = oc % 2
            wssem = "dWS0" if slot == 0 else "dWS1"
            W("g", "sT", ws_freed[slot])
            ws_t[slot] = DMA("g", vw(ws_off[slot], 4096, BF16), wq[oc], wssem)
            W("t", wssem, ws_t[slot])
            if hg == 0 and oc == heads[0]:
                W("t", "sV", rope_done[5])
            if hg > 0 and oc == heads[0]:
                W("t", "sV", avps_free_v[0])
                W("t", "sV", avps_free_v[1])
                W("t", "sV", tps_free[0])
                W("t", "sV", tps_free[1])
                W("t", "sS", sps_free[0])
                W("t", "sS", sps_free[1])
                W("t", "sS", sps_free[2])
            for nh in range(2):
                qcol = (2 * slot + nh) * 512
                W("t", "sV", qps_free[slot][nh])
                for ic in range(NCH):
                    E("t", lambda eng, o=pv(qcol, 512),
                      l=vw(ws_off[slot] + ic * 256, 128, BF16),
                      r=vw(hown_off + ic * 2048 + nh * 1024, 512, BF16),
                      st=(ic == 0), sp=(ic == 31):
                      eng.matmul(o, l, r, start=st, stop=sp),
                      inc=("sT" if ic == 31 else None))
                q_done = sch.cnt["sT"]
                W("s", "sT", q_done)
                W("s", "sV", qb_free[nh])
                E("s", lambda eng, o=qb_v[:, nh * 512:(nh + 1) * 512],
                  i=pv(qcol, 512):
                  eng.activation(out=o, in_=i, func=AF.Copy, bias=0.0,
                                 scale=1.0), inc="sS")
                qb_done = sch.cnt["sS"]
                W("t", "sS", qb_done)
                W("t", "sV", rqps_free[nh])
                E("t", lambda eng, o=pv(2048 + nh * 512, 512),
                  r=qb_v[:, nh * 512:(nh + 1) * 512]:
                  eng.matmul(o, rsign_v, r, start=True, stop=True), inc="sT")
                rot_done = sch.cnt["sT"]
                W("v", "sT", rot_done)
                co = cos_v[:, 256 + nh * 512: 256 + (nh + 1) * 512]
                so = sin_v[:, 256 + nh * 512: 256 + (nh + 1) * 512]
                qd = qrv[:, j * 1024 + nh * 512: j * 1024 + (nh + 1) * 512]
                tu = qb_v[:, nh * 512:(nh + 1) * 512]
                E("v", lambda eng, o=tu, c=co, i=pv(qcol, 512):
                  eng.tensor_tensor(out=o, in0=i, in1=c, op=OP.mult))
                E("v", lambda eng, o=qd, s_=so, i=pv(2048 + nh * 512, 512):
                  eng.tensor_tensor(out=o, in0=i, in1=s_, op=OP.mult))
                E("v", lambda eng: eng.drain())
                E("v", lambda eng, o=qd, u=tu: eng.tensor_tensor(
                    out=o, in0=o, in1=u, op=OP.add), inc="sV")
                qps_free[slot][nh] = sch.cnt["sV"]
                rqps_free[nh] = sch.cnt["sV"]
                qb_free[nh] = sch.cnt["sV"]
            ws_freed[slot] = q_done
        rope_hg_done = sch.cnt["sV"]

        if hg == 0 and "dbg_qr" in dbg_d:
            W("y", "sV", rope_hg_done)
            tq_ = DMA("y", dbg_d["dbg_qr"], qrv, "dT0")
            W("v", "dT0", tq_)

        for b in range(4):
            for j, h_abs in enumerate(heads):
                sig_win = [0, 0]
                for win in range(2):
                    atv = atA_v if win == 0 else atB_v
                    kb_chunk = 2 * (b + win)
                    for hf in range(2):
                        u = unit_ctr
                        unit_ctr += 1
                        sbase = 1536 + (u % 3) * 512
                        W("t", "sV", rope_hg_done)
                        W("t", "sS", sps_free[u % 3])
                        for k2 in range(2):
                            kc4 = hf * 2 + k2
                            is_mask = (b == 3 and win == 1 and kc4 >= 2)
                            last = (k2 == 1 and not is_mask)
                            E("t", lambda eng, o=pv(sbase + k2 * 256, 256),
                              l=kr_v[:, (kb_chunk + kc4) * 128:
                                     (kb_chunk + kc4 + 1) * 128],
                              r=qrv[:, j * 1024 + b * 256:
                                    j * 1024 + (b + 1) * 256],
                              sp=(not is_mask):
                              eng.matmul(o, l, r, start=True, stop=sp),
                              inc=("sT" if last else None))
                            if is_mask:
                                E("t", lambda eng,
                                  o=pv(sbase + k2 * 256, 256),
                                  l=maskm_v[:, (kc4 - 2) * 128:
                                            (kc4 - 1) * 128]:
                                  eng.matmul(o, l, onesr_v, start=False,
                                             stop=True),
                                  inc=("sT" if k2 == 1 else None))
                        sc_done = sch.cnt["sT"]
                        W("s", "sT", sc_done)
                        E("s", lambda eng, o=atv[:, hf * 512:(hf + 1) * 512],
                          i=pv(sbase, 512):
                          eng.activation(out=o, in_=i, func=AF.Sigmoid,
                                         bias=zero_ap, scale=SCALE),
                          inc="sS")
                        sig_done[u] = sch.cnt["sS"]
                        sps_free[u % 3] = sig_done[u]
                        sig_win[win] = sig_done[u]
                aset = h_abs % 2
                flush_tp()
                W("t", "sV", avps_free_v[aset])
                for win in range(2):
                    atv = atA_v if win == 0 else atB_v
                    abase = (0 if win == 0 else 512) + aset * 3072
                    kb_chunk = 2 * (b + win)
                    W("t", "sS", sig_win[win])
                    for qs in range(2):
                        for kc4 in range(4):
                            last = (win == 1 and qs == 1 and kc4 == 3)
                            E("t", lambda eng, o=pv(abase + qs * 129, 129),
                              l=atv[:, kc4 * 256 + qs * 128:
                                    kc4 * 256 + (qs + 1) * 128],
                              r=vext_v[:, (kb_chunk + kc4) * 129:
                                       (kb_chunk + kc4 + 1) * 129],
                              st=(kc4 == 0), sp=(kc4 == 3):
                              eng.matmul(o, l, r, start=st, stop=sp),
                              inc=("sT" if last else None))
                av_done = sch.cnt["sT"]
                blslot = (b * 4 + j) % 2
                W("v", "sT", av_done)
                W("v", "sT", tp_prev[blslot])
                for win in range(2):
                    abase = (0 if win == 0 else 512) + aset * 3072
                    base_ap = P[:, abase + 128:abase + 129]
                    sums = bass_rust.AP(base_ap.tensor, base_ap.offset,
                                        [[4096, 128], [129, 2]])
                    E("v", lambda eng, o=scr[:, win * 2:win * 2 + 2], i=sums:
                      eng.reciprocal(o, i))
                E("v", lambda eng: eng.drain())
                E("v", lambda eng, b2=alham_v[:, 2 * b:2 * b + 2]:
                  eng.tensor_tensor(out=scr[:, 4:6], in0=scr[:, 0:2],
                                    in1=b2, op=OP.mult))
                E("v", lambda eng, b2=alpha_v[:, 2 * b:2 * b + 2]:
                  eng.tensor_tensor(out=scr[:, 6:8], in0=scr[:, 2:4],
                                    in1=b2, op=OP.mult), inc="sV")
                E("v", lambda eng: eng.drain())
                for qs in range(2):
                    E("v", lambda eng, o=kb_v[:, qs * 128:(qs + 1) * 128],
                      i=pv(aset * 3072 + qs * 129, 128),
                      sc1=scr[:, 4 + qs:5 + qs]:
                      eng.tensor_scalar(out=o, in0=i, scalar1=sc1,
                                        scalar2=None, op0=OP.mult))
                E("v", lambda eng: eng.drain())
                for qs in range(2):
                    E("v", lambda eng, o=blnd_v[:, blslot * 256 + qs * 128:
                                            blslot * 256 + (qs + 1) * 128],
                      i=pv(aset * 3072 + 512 + qs * 129, 128),
                      sc2=scr[:, 6 + qs:7 + qs],
                      t_=kb_v[:, qs * 128:(qs + 1) * 128]:
                      eng.scalar_tensor_tensor(
                          out=o, in0=i, scalar=sc2, in1=t_,
                          op0=OP.mult, op1=OP.add),
                      inc=("sV" if qs == 1 else None))
                last_blend_v = sch.cnt["sV"]
                avps_free_v[aset] = last_blend_v
                pend_tp[0] = (h_abs, b, blslot, last_blend_v)
        flush_tp()
    ao_done_v = sch.cnt["sV"]

    if "dbg_ao" in dbg_d:
        W("y", "sV", ao_done_v)
        ta_ = DMA("y", dbg_d["dbg_ao"], ao, "dT2")
        W("y", "dT2", ta_)

    abf_off = R2
    abf = vw(abf_off, 32768, BF16)
    asq = vw(RQ, 1024, BF16)
    evac_done = [0, 0]
    asq_free = 0
    for oc in range(NCH):
        slot = oc % 2
        wssem = "dWS0" if slot == 0 else "dWS1"
        W("g", "sT", ws_freed[slot])
        ws_t[slot] = DMA("g", vw(ws_off[slot], 4096, BF16), wo[oc], wssem)
        W("t", wssem, ws_t[slot])
        if oc == 0:
            W("t", "sV", ao_done_v)
        W("t", "sS", evac_done[slot])
        base = slot * 1024
        for nh in range(2):
            for ic in range(NCH):
                E("t", lambda eng, o=pv(base + nh * 512, 512),
                  l=vw(ws_off[slot] + ic * 256, 128, BF16),
                  r=ao[:, ic * 1024 + nh * 512: ic * 1024 + (nh + 1) * 512],
                  st=(ic == 0), sp=(ic == 31):
                  eng.matmul(o, l, r, start=st, stop=sp),
                  inc=("sT" if (nh == 1 and ic == 31) else None))
        ws_freed[slot] = sch.cnt["sT"]
        wo_done = sch.cnt["sT"]
        W("s", "sT", wo_done)
        W("s", "sT", asq_free)
        E("s", lambda eng, o=abf[:, oc * 1024:(oc + 1) * 1024],
          i=pv(base, 1024):
          eng.activation(out=o, in_=i, func=AF.Copy, bias=0.0, scale=1.0))
        E("s", lambda eng, i=pv(base, 1024):
          eng.activation(out=asq, in_=i, func=AF.Square, bias=0.0, scale=1.0),
          inc="sS")
        evac_done[slot] = sch.cnt["sS"]
        W("t", "sS", evac_done[slot])
        for th in range(2):
            E("t", lambda eng, o=pv(2048 + th * 512, 512, 0, 1),
              r=abf[:, oc * 1024 + th * 512: oc * 1024 + (th + 1) * 512],
              st=(oc == 0), sp=(oc == 31):
              eng.matmul(o, onesc_v, r, start=st, stop=sp))
        for th in range(2):
            E("t", lambda eng, o=pv(3072 + th * 512, 512, 0, 1),
              r=asq[:, th * 512:(th + 1) * 512], st=(oc == 0), sp=(oc == 31):
              eng.matmul(o, onesc_v, r, start=st, stop=sp),
              inc=("sT" if th == 1 else None))
        asq_free = sch.cnt["sT"]

    if "dbg_a" in dbg_d:
        W("y", "sS", max(evac_done))
        tb_ = DMA("y", dbg_d["dbg_a"], abf, "dT3")
        W("v", "dT3", tb_)

    def ln_rows():
        W("v", "sT", sch.cnt["sT"])
        E("v", lambda eng: eng.tensor_scalar(
            out=stat[:, 0:1024], in0=pv(2048, 1024, 0, 1),
            scalar1=1.0 / 4096, scalar2=None, op0=OP.mult), inc="sV")
        W("s", "sV", sch.cnt["sV"])
        E("s", lambda eng: eng.activation(
            out=stat[:, 1024:2048], in_=stat[:, 0:1024], func=AF.Square,
            bias=0.0, scale=1.0), inc="sS")
        W("v", "sS", sch.cnt["sS"])
        E("v", lambda eng: eng.scalar_tensor_tensor(
            out=stat[:, 1024:2048], in0=pv(3072, 1024, 0, 1),
            scalar=1.0 / 4096, in1=stat[:, 1024:2048],
            op0=OP.mult, op1=OP.subtract), inc="sV")
        W("s", "sV", sch.cnt["sV"])
        E("s", lambda eng: eng.activation(
            out=stat[:, 1024:2048], in_=stat[:, 1024:2048], func=AF.Sqrt,
            bias=eps_ap, scale=1.0), inc="sS")
        W("v", "sS", sch.cnt["sS"])
        E("v", lambda eng: eng.reciprocal(stat[:, 1024:2048],
                                          stat[:, 1024:2048]))
        E("v", lambda eng: eng.drain())
        E("v", lambda eng: eng.tensor_tensor(
            out=stat[:, 0:1024], in0=stat[:, 0:1024], in1=stat[:, 1024:2048],
            op=OP.mult), inc="sV")
        return sch.cnt["sV"]

    def bcast_rows(rstd_off, mr_off):
        W("t", "sV", sch.cnt["sV"])
        for k in range(2):
            E("t", lambda eng, o=pv(rstd_off + k * 512, 512),
              r=stat[:, 1024 + k * 512:1024 + (k + 1) * 512]:
              eng.matmul(o, ones1_v, r, start=True, stop=True))
        for k in range(2):
            E("t", lambda eng, o=pv(mr_off + k * 512, 512),
              r=stat[:, k * 512:(k + 1) * 512]:
              eng.matmul(o, ones1_v, r, start=True, stop=True),
              inc=("sT" if k == 1 else None))
        return sch.cnt["sT"]

    wo_all = max(ws_freed)
    asq_last = asq_free
    ln_rows()
    bcD = bcast_rows(0, 1024)

    xrl4_off = [R1, R1 + 16384]
    x2o4_off = [R1 + 32768, R1 + 49152]
    tmp4_off = RQ
    x2sq4_off = RQ + 8192
    rbD = vw(O["atA"], 1024, BF16)
    mbD = vw(O["atB"], 1024, BF16)
    W("v", "sT", bcD)
    E("v", lambda eng: eng.tensor_copy(rbD, pv(0, 1024)))
    E("v", lambda eng: eng.tensor_copy(mbD, pv(1024, 1024)))
    E("v", lambda eng: eng.drain(), inc="sV")
    bcD_bf = sch.cnt["sV"]

    def grp_ap(dram, g):
        return bass_rust.AP(dram.tensor, g * 4 * 131072,
                            [[1024, 128], [131072, 4], [1, 1024]])

    def sb3(view):
        return _strided(view, 0, [[1024, 4], [1, 1024]])

    xrl_t = [0, 0]
    x2_done = [0] * 8
    scal_done = [0] * 8
    spill_t = [0, 0]
    spill_tg = [0] * 8
    stats_done_g = [0] * 8
    for g in range(8):
        buf = g % 2
        xrsem = "dXR%d" % buf
        if g < 2:
            W("y", "sT", wo_all)
        else:
            W("y", "sV", x2_done[g - 2])
        xrl_t[buf] = DMA("y", sb3(vw(xrl4_off[buf], 4096, F32)),
                         grp_ap(xr, g), xrsem)
        W("v", xrsem, xrl_t[buf])
        W("v", "sV", bcD_bf)
        if g == 0:
            W("v", "sT", asq_last)
        if g >= 2:
            W("v", "sS", scal_done[g - 2])
            W("v", "dSP%d" % buf, spill_t[buf])
        tmp4 = vw(tmp4_off, 4096, BF16)
        t3 = tmp4.rearrange("p (c t) -> p c t", c=4)
        ab3 = abf[:, g * 4096:(g + 1) * 4096].rearrange(
            "p (c t) -> p c t", c=4)
        x2o4 = vw(x2o4_off[buf], 4096, F32)
        E("v", lambda eng, o=t3, i=ab3, b_=_bc(rbD, 4, 1024):
          eng.tensor_tensor(out=o, in0=i, in1=b_, op=OP.mult))
        E("v", lambda eng: eng.drain())
        E("v", lambda eng, o=t3, b_=_bc(mbD, 4, 1024):
          eng.tensor_tensor(out=o, in0=o, in1=b_, op=OP.subtract))
        E("v", lambda eng: eng.drain())
        E("v", lambda eng, o=x2o4, t_=tmp4, x_=vw(xrl4_off[buf], 4096, F32):
          eng.tensor_tensor(out=o, in0=t_, in1=x_, op=OP.add), inc="sV")
        x2_done[g] = sch.cnt["sV"]
        W("s", "sV", x2_done[g])
        spill_t[buf] = DMA("s", grp_ap(x2sp, g), sb3(x2o4), "dSP%d" % buf)
        spill_tg[g] = spill_t[buf]
        if g >= 1:
            W("s", "sT", stats_done_g[g - 1])
        E("s", lambda eng, o=abf[:, g * 4096:(g + 1) * 4096], i=x2o4:
          eng.activation(out=o, in_=i, func=AF.Copy, bias=0.0, scale=1.0))
        E("s", lambda eng, o=vw(x2sq4_off, 4096, BF16), i=x2o4:
          eng.activation(out=o, in_=i, func=AF.Square, bias=0.0, scale=1.0),
          inc="sS")
        scal_done[g] = sch.cnt["sS"]
        W("t", "sS", scal_done[g])
        for oq in range(4):
            for th in range(2):
                E("t", lambda eng, o=pv(2048 + th * 512, 512, 0, 1),
                  r=abf[:, g * 4096 + oq * 1024 + th * 512:
                        g * 4096 + oq * 1024 + (th + 1) * 512],
                  st=(g == 0 and oq == 0), sp=(g == 7 and oq == 3):
                  eng.matmul(o, onesc_v, r, start=st, stop=sp))
        for oq in range(4):
            for th in range(2):
                E("t", lambda eng, o=pv(3072 + th * 512, 512, 0, 1),
                  r=vw(x2sq4_off + (oq * 1024 + th * 512) * 2, 512, BF16),
                  st=(g == 0 and oq == 0), sp=(g == 7 and oq == 3):
                  eng.matmul(o, onesc_v, r, start=st, stop=sp),
                  inc=("sT" if (oq == 3 and th == 1) else None))
        stats_done_g[g] = sch.cnt["sT"]

    ln_rows()
    bc2 = bcast_rows(2048, 3072)

    h2 = vw(R2, 32768, BF16)
    rb2 = rbD
    mb2 = mbD
    W("v", "sT", bc2)
    E("v", lambda eng: eng.tensor_copy(rb2, pv(2048, 1024)))
    E("v", lambda eng: eng.tensor_copy(mb2, pv(3072, 1024)))
    E("v", lambda eng: eng.drain(), inc="sV")
    bc2_bf = sch.cnt["sV"]

    ntneg = vw(RQ, 4096, BF16)
    n4_off = RQ + 8192
    us_t = [0, 0]
    for oc0 in (0, 1):
        W("g", "sT", ws_freed[oc0])
        us_t[oc0] = DMA("g", vw(ws_off[oc0], 4096, U16), uq[oc0],
                        "dWS%d" % oc0)
    W("g", "sV", x2_done[7])
    nt_t = DMA("g", ntneg, nt, "dNT")
    W("g", "sT", stats_done_g[7])
    n4t0 = DMA("g", vw(n4_off, 4096, BF16, 0, 4), n4[0], "dN4")

    h2g_done = [None] * 8
    for g in range(8):
        en = "g" if g % 3 == 2 else "v"
        sem = "sG" if en == "g" else "sV"
        h3 = h2[:, g * 4096:(g + 1) * 4096].rearrange("p (c t) -> p c t",
                                                      c=4)
        W(en, "sV", bc2_bf)
        E(en, lambda eng, o=h3, b_=_bc(rb2, 4, 1024):
          eng.tensor_tensor(out=o, in0=o, in1=b_, op=OP.mult))
        E(en, lambda eng: eng.drain())
        E(en, lambda eng, o=h3, b_=_bc(mb2, 4, 1024):
          eng.tensor_tensor(out=o, in0=o, in1=b_, op=OP.subtract), inc=sem)
        h2g_done[g] = (sem, sch.cnt[sem])

    if "dbg_h2" in dbg_d:
        W("y", "sV", h2g_done[7][1])
        W("y", "sG", sch.cnt["sG"])
        tc_ = DMA("y", dbg_d["dbg_h2"], h2, "dT4")
        W("v", "dT4", tc_)

    w1_off = [O["wk"], O["wv"]]
    hsb = vw(O["atA"], 1024, BF16)
    fsq_off = O["atB"]
    fbf_off = R1
    fbf = vw(fbf_off, 32768, BF16)

    W("v", "dNT", nt_t)
    E("v", lambda eng: eng.tensor_scalar(
        out=ntneg, in0=ntneg, scalar1=-1.0, scalar2=None, op0=OP.mult),
        inc="sV")
    ntneg_done = sch.cnt["sV"]

    for th in range(2):
        for ic in range(NCH):
            hs_, hv_ = h2g_done[ic // 4]
            W("t", hs_, hv_)
            E("t", lambda eng, o=pv(th * 512, 512),
              l=epad_v[:, 124 - ic * 4:252 - ic * 4],
              r=h2[:, ic * 1024 + th * 512: ic * 1024 + (th + 1) * 512],
              st=(ic == 0), sp=(ic == 31):
              eng.matmul(o, l, r, start=st, stop=sp),
              inc=("sT" if (th == 1 and ic == 31) else None))
    hmm_done = sch.cnt["sT"]
    W("s", "sT", hmm_done)
    E("s", lambda eng: eng.activation(
        out=hsb, in_=pv(0, 1024), func=AF.Copy, bias=0.0, scale=3.5),
        inc="sS")
    h75_done = sch.cnt["sS"]

    n4_free = 0
    w1v = {}
    fdone_oc = [0] * 32
    fevac_done = 0
    fsq_free = 0
    for oc in range(NCH):
        slot = oc % 2
        wssem = "dWS0" if slot == 0 else "dWS1"
        vq = vw(ws_off[slot], 4096, U16)
        vbf = vw(ws_off[slot], 4096, BF16)
        if oc >= 2:
            W("g", "sV", w1v[(oc - 2, 3)])
            us_t[slot] = DMA("g", vw(ws_off[slot], 4096, U16), uq[oc],
                             wssem)
        if oc >= 1:
            W("g", "sT", n4_free)
            n4t = DMA("g", vw(n4_off, 4096, BF16, 0, 4), n4[oc], "dN4")
        else:
            n4t = n4t0
        W("v", wssem, us_t[slot])
        if oc >= 2:
            W("v", "sT", fdone_oc[oc - 2])
        E("v", lambda eng, i=vq:
          eng.tensor_scalar(out=i, in0=i, scalar1=mask32_v, scalar2=None,
                            op0=OP.bitwise_and))
        E("v", lambda eng: eng.drain())
        E("v", lambda eng, i=vq, o=vbf:
          eng.tensor_scalar(out=o, in0=i, scalar1=inv2s_v, scalar2=None,
                            op0=OP.mult), inc="sV")
        E("v", lambda eng: eng.drain())

        def emit_main(icg4, oc=oc, slot=slot):
            W("t", "sV", w1v[(oc, icg4)])
            if icg4 == 0:
                if oc == 0:
                    W("t", "sS", h75_done)
                else:
                    W("t", "sS", fevac_done)
            for th in range(2):
                for ic in range(icg4 * 8, icg4 * 8 + 8):
                    E("t", lambda eng, o=pv(th * 512, 512),
                      l=vw(w1_off[slot] + ic * 256, 128, BF16),
                      r=h2[:, ic * 1024 + th * 512:
                           ic * 1024 + (th + 1) * 512],
                      st=(ic == 0):
                      eng.matmul(o, l, r, start=st, stop=False))

        for pr in range(4):
            W("t", "dN4", n4t)
            W("t", "sV", w1v.get((oc, pr - 1), w1v.get((oc - 1, 3), 0)))
            for k2 in range(2):
                E("t", lambda eng, o=pv(1024 + k2 * 512, 512),
                  r=vw(n4_off + (pr * 2 + k2) * 1024, 512, BF16, 0, 4):
                  eng.matmul(o, e4_v, r, start=True, stop=True),
                  inc=("sT" if k2 == 1 else None))
            nexp_done = sch.cnt["sT"]
            if pr == 3:
                n4_free = nexp_done
            W("v", "sT", nexp_done)
            E("v", lambda eng, o=vw(w1_off[slot] + pr * 2048, 1024, BF16),
              i=vbf[:, pr * 1024:(pr + 1) * 1024], n_=pv(1024, 1024):
              eng.tensor_tensor(out=o, in0=i, in1=n_, op=OP.mult),
              inc="sV")
            w1v[(oc, pr)] = sch.cnt["sV"]
            if pr >= 1:
                emit_main(pr - 1)
        emit_main(3)
        W("t", "sV", ntneg_done)
        for th in range(2):
            E("t", lambda eng, o=pv(th * 512, 512),
              l=ntneg[:, oc * 128:(oc + 1) * 128],
              r=hsb[:, th * 512:(th + 1) * 512]:
              eng.matmul(o, l, r, start=False, stop=True),
              inc=("sT" if th == 1 else None))
        fdone_oc[oc] = sch.cnt["sT"]
        if oc == 0:
            W("s", "sV", x2_done[7])
            W("s", "dSP0", spill_tg[6])
            W("s", "dSP1", spill_tg[7])
        W("s", "sT", fdone_oc[oc])
        W("s", "sT", fsq_free)
        for th in range(2):
            E("s", lambda eng, o=fbf[:, oc * 1024 + th * 512:
                                     oc * 1024 + (th + 1) * 512],
              i=pv(th * 512, 512), b_=fbias_v[:, oc:oc + 1]:
              eng.activation(out=o, in_=i, func=AF.Identity, bias=b_,
                             scale=2.0 / 7))
            E("s", lambda eng, o=vw(fsq_off + th * 1024, 512, BF16),
              i=pv(th * 512, 512), b_=fbias_v[:, oc:oc + 1]:
              eng.activation(out=o, in_=i, func=AF.Square, bias=b_,
                             scale=2.0 / 7), inc="sS")
            if th == 0:
                fevac_h0 = sch.cnt["sS"]
        fevac_done = sch.cnt["sS"]
        W("t", "sS", fevac_done)
        if oc == 0:
            W("t", "sV", bc2_bf)
        for th in range(2):
            E("t", lambda eng, o=pv(2048 + th * 512, 512, 0, 1),
              r=fbf[:, oc * 1024 + th * 512: oc * 1024 + (th + 1) * 512],
              st=(oc == 0), sp=(oc == 31):
              eng.matmul(o, onesc_v, r, start=st, stop=sp))
        for th in range(2):
            E("t", lambda eng, o=pv(3072 + th * 512, 512, 0, 1),
              r=vw(fsq_off + th * 1024, 512, BF16), st=(oc == 0),
              sp=(oc == 31):
              eng.matmul(o, onesc_v, r, start=st, stop=sp),
              inc=("sT" if th == 1 else None))
        fsq_free = sch.cnt["sT"]
    f_all_done = fdone_oc[31]

    if "dbg_f" in dbg_d:
        W("y", "sS", fevac_done)
        DMA("y", dbg_d["dbg_f"], fbf, "dT7")

    ln_rows()
    bc3 = bcast_rows(0, 1024)

    fb_off = [R3, O["wk"]]
    ob_off2 = [R2, R2 + 16384]
    tmpf4_off = R2 + 32768
    rb3 = vw(RQ, 1024, BF16)
    mb3 = vw(RQ + 2048, 1024, BF16)
    W("v", "sT", bc3)
    E("v", lambda eng: eng.tensor_copy(rb3, pv(0, 1024)))
    E("v", lambda eng: eng.tensor_copy(mb3, pv(1024, 1024)))
    E("v", lambda eng: eng.drain(), inc="sV")
    bc3_bf = sch.cnt["sV"]
    fb4 = [R3, R3 + 4096, R3 + 8192, R3 + 12288]
    ob4 = [R2 + i * 4096 for i in range(4)]
    tmpf_off = R2 + 16384
    frl_tg = [0] * 32
    out_tg = [0] * 32
    of_done = [0] * 32
    for oc in range(NCH):
        xs = oc % 4
        frsem = "dFR%d" % (oc % 2)
        if oc == 0:
            W("y", "sV", w1v[(31, 3)])
            W("y", "dSP0", spill_tg[6])
            W("y", "dSP1", spill_tg[7])
        if oc >= 4:
            W("y", "sV", of_done[oc - 4])
        frl_tg[oc] = DMA("y", vw(fb4[xs], 1024, F32), x2sp[oc], frsem)
        W("v", frsem, frl_tg[oc])
        W("v", "sV", bc3_bf)
        if oc == 0:
            W("v", "sT", f_all_done)
        if oc >= 4:
            W("v", "dOUT%d" % (oc % 2), out_tg[oc - 4])
        tmpf = vw(tmpf_off, 1024, BF16)
        E("v", lambda eng: eng.drain())
        E("v", lambda eng, o=tmpf, i=fbf[:, oc * 1024:(oc + 1) * 1024]:
          eng.tensor_tensor(out=o, in0=i, in1=rb3, op=OP.mult))
        E("v", lambda eng: eng.drain())
        E("v", lambda eng, o=tmpf:
          eng.tensor_tensor(out=o, in0=o, in1=mb3, op=OP.subtract))
        E("v", lambda eng: eng.drain())
        E("v", lambda eng, o=vw(ob4[xs], 1024, F32), t_=tmpf,
          x_=vw(fb4[xs], 1024, F32):
          eng.tensor_tensor(out=o, in0=t_, in1=x_, op=OP.add), inc="sV")
        of_done[oc] = sch.cnt["sV"]
        qe = "y" if oc % 2 == 0 else "g"
        W(qe, "sV", of_done[oc])
        out_tg[oc] = DMA(qe, out_d[oc], vw(ob4[xs], 1024, F32),
                         "dOUT%d" % (oc % 2))

    final_dbg = 0
    for i_ in range(2):
        E("y", lambda eng, s_="dOUT%d" % i_, v_=sch.cnt["dOUT%d" % i_]:
          eng.wait_ge(sch.semh[s_], v_))
    for i_ in range(8):
        if sch.cnt["dT%d" % i_]:
            E("y", lambda eng, s_="dT%d" % i_, v_=sch.cnt["dT%d" % i_]:
              eng.wait_ge(sch.semh[s_], v_))

    with nc.Block() as blk:
        @blk.sync
        def _(eng):
            sch.run("y", eng)

        @blk.gpsimd
        def _(eng):
            sch.run("g", eng)

        @blk.tensor
        def _(eng):
            sch.run("t", eng)

        @blk.vector
        def _(eng):
            sch.run("v", eng)

        @blk.scalar
        def _(eng):
            sch.run("s", eng)

    ctx.close()
    return nc, list(dbg_d.keys())


def kernel(L0_attn_x, L0_attn_Wq, L0_attn_Wk, L0_attn_Wv, L0_attn_Wo,
           L0_ffn_weight_q3, L0_ffn_weight_norm, L0_ffn_bias):
    x = np.asarray(L0_attn_x, dtype=np.float32)
    shared = _prep_shared(np.asarray(L0_attn_Wq, np.float32),
                          np.asarray(L0_attn_Wk, np.float32),
                          np.asarray(L0_attn_Wv, np.float32),
                          np.asarray(L0_attn_Wo, np.float32),
                          np.asarray(L0_ffn_weight_q3),
                          np.asarray(L0_ffn_weight_norm, np.float32),
                          np.asarray(L0_ffn_bias, np.float32))
    nc, taps = build_nc(tuple(_dbg_taps))
    in_maps = []
    for c in range(NCORES):
        m = dict(shared)
        m.update(_prep_core(x, c))
        in_maps.append(m)
    res = run_bass_kernel_spmd(nc, in_maps, core_ids=list(range(NCORES)))
    out = np.empty((B, S, D), dtype=np.float32)
    for c in range(NCORES):
        b = c // 4
        T0 = 1024 * (c % 4)
        oc = res.results[c]["out"]
        out[b, T0:T0 + 1024, :] = np.ascontiguousarray(
            oc.transpose(2, 0, 1)).reshape(1024, 4096)
    if taps:
        kernel.dbg = [{t: res.results[c][t] for t in taps}
                      for c in range(NCORES)]
    return out


# version 13 (final)
# speedup vs baseline: 1.3137x; 1.0334x over previous
import math
import contextlib

import numpy as np
import ml_dtypes

import bass_rust
import concourse.bass as bass
import concourse.mybir as mybir
from concourse.bass_utils import run_bass_kernel_spmd

F32 = mybir.dt.float32
BF16 = mybir.dt.bfloat16
U16 = mybir.dt.uint16
I32 = mybir.dt.int32
U8 = mybir.dt.uint8
OP = mybir.AluOpType
AF = mybir.ActivationFunctionType
BF = ml_dtypes.bfloat16

B, S, D = 2, 4096, 4096
H, DH = 32, 128
SCALE = 1.0 / math.sqrt(DH)
NCORES = 8
TOK = 1024
HTOK = 1536
NCH = 32
EPS = 1e-5

HEAP_BYTES = 212832


def _al(x, a=32):
    return (x + a - 1) // a * a


class _Off:
    def __init__(self, base=0):
        self.cur = base
        self.map = {}

    def add(self, name, nbytes):
        self.map[name] = self.cur
        self.cur = _al(self.cur + nbytes)
        return self.map[name]


_P = _Off()
_P.add("cos", 1536 * 2)
_P.add("sin", 1536 * 2)
_P.add("wk", 4096 * 2)
_P.add("wv", 4096 * 2)
_P.add("kr", 1536 * 2)
_P.add("vext", 1548 * 2)
_P.add("rsign", 128 * 2)
_P.add("idn", 128 * 2)
_P.add("epad", 256 * 2)
_P.add("e4", 128 * 2)
_P.add("ones1", 128 * 4)
_P.add("onesc", 32)
_P.add("alham", 8 * 4)
_P.add("alpha", 8 * 4)
_P.add("maskm", 256 * 2)
_P.add("onesr", 256 * 2)
_P.add("fbias", 32 * 4)
_P.add("consts", 8 * 4)
_P.add("shiftv", 64)
_P.add("stat", 2048 * 4)
_P.add("scr", 16 * 4)
_P.add("kb", 256 * 2)
_P.add("blnd", 512 * 2)
_P.add("atA", 1024 * 2)
_P.add("atB", 1024 * 2)
_P.add("qb", 1024 * 2)
PERSIST_END = _P.cur

R3 = _al(PERSIST_END)
R3_SZ = 2 * 8192
R1 = _al(R3 + R3_SZ)
R1_SZ = 65536
R2 = _al(R1 + R1_SZ)
R2_SZ = 65536
RQ = _al(R2 + R2_SZ)
RQ_SZ = 16384
assert RQ + RQ_SZ <= HEAP_BYTES, (RQ + RQ_SZ, HEAP_BYTES)

_dbg_taps = []


def _rope_cache():
    inv_freq = 1.0 / 10000.0 ** (np.arange(0, DH, 2, dtype=np.float32) / DH)
    t = np.arange(S, dtype=np.float32)
    f = np.outer(t, inv_freq)
    cos = np.zeros((S, DH), dtype=np.float32)
    sin = np.zeros((S, DH), dtype=np.float32)
    cos[:, 0::2] = np.cos(f)
    cos[:, 1::2] = np.cos(f)
    sin[:, 0::2] = np.sin(f)
    sin[:, 1::2] = np.sin(f)
    return cos, sin


def _prep_shared(Wq, Wk, Wv, Wo, wq3, wnorm, fbias):
    d = {}
    d["wq"] = np.ascontiguousarray(
        Wq.reshape(32, 128, 32, 128).transpose(0, 3, 2, 1))
    d["wo"] = np.ascontiguousarray(
        Wo.reshape(32, 128, 32, 128).transpose(0, 3, 2, 1))
    d["wk"] = np.ascontiguousarray(Wk.reshape(128, 32, 128).transpose(2, 1, 0))
    d["wv"] = np.ascontiguousarray(Wv.reshape(128, 32, 128).transpose(2, 1, 0))

    pk = np.ascontiguousarray(wq3).view(np.uint8)
    pk13 = np.concatenate(
        [pk, np.zeros((pk.shape[0], 1), np.uint8)], axis=1)
    j = np.arange(32)
    bidx = (3 * j) // 8
    lo = pk13[:, bidx]
    hi = pk13[:, bidx + 1]
    b2 = np.stack([lo, hi], axis=-1)
    u16 = np.ascontiguousarray(b2).reshape(-1, 64).view(np.uint16)
    u16oi = u16.reshape(4096, 4096)
    d["uq"] = np.ascontiguousarray(
        u16oi.reshape(32, 128, 32, 128).transpose(0, 3, 2, 1))

    n_og = wnorm.reshape(4096, 128)
    d["n4"] = np.ascontiguousarray(
        n_og.reshape(32, 128, 32, 4).transpose(0, 3, 2, 1))
    d["nt"] = np.ascontiguousarray(n_og.T)
    d["fbias"] = np.ascontiguousarray(fbias.reshape(32, 128).T)

    e4 = np.zeros((4, 128), dtype=BF)
    for g in range(4):
        e4[g, g * 32:(g + 1) * 32] = 1
    d["e4"] = e4
    epad = np.zeros((128, 256), dtype=BF)
    p = np.arange(128)
    epad[p, 124 + p // 32] = 1
    d["epad"] = epad
    rsign = np.zeros((128, 128), dtype=np.float32)
    m = np.arange(64)
    rsign[m + 64, m] = -1.0
    rsign[m, m + 64] = 1.0
    d["rsign"] = rsign.astype(BF)
    d["idn"] = np.eye(128, dtype=BF)
    d["ones1"] = np.ones((1, 128), np.float32)
    d["onesc"] = np.ones((128, 1), dtype=BF)
    d["onesr"] = np.ones((1, 256), dtype=BF)
    consts = np.zeros((128, 8), np.float32)
    consts[:, 0] = EPS
    d["consts"] = consts
    s_ = (3 * (np.arange(128) % 32)) % 8
    d["mask32"] = (7 << s_).astype(np.int32).reshape(128, 1)
    d["inv2s"] = (0.5 ** s_).astype(np.float32).reshape(128, 1)
    return d


def _prep_core(x, core):
    b = core // 4
    T0 = 1024 * (core % 4)
    lo = T0 - 256
    xh = np.zeros((HTOK, D), dtype=np.float32)
    s0 = max(lo, 0)
    s1 = min(T0 + 1280, S)
    xh[s0 - lo:s1 - lo] = x[b, s0:s1]
    xh_arr = np.ascontiguousarray(
        xh.reshape(6, 256, 32, 128).transpose(0, 3, 2, 1))

    cos, sin = _rope_cache()
    pos = np.clip(np.arange(lo, lo + HTOK), 0, S - 1)
    cosT = np.ascontiguousarray(cos[pos].T)
    sinT = np.ascontiguousarray(sin[pos].T)

    xr = np.ascontiguousarray(
        x[b, T0:T0 + 1024].reshape(1024, 32, 128).transpose(1, 2, 0))

    alpha = np.zeros((128, 8), np.float32)
    alham = np.zeros((128, 8), np.float32)
    for bl in range(4):
        gb = T0 // 256 + bl
        if gb == 0:
            a = np.ones(256, np.float32)
        else:
            a = np.linspace(0.0, 1.0, 256).astype(np.float32)
        for qs in range(2):
            alpha[:, 2 * bl + qs] = a[qs * 128:(qs + 1) * 128]
            alham[:, 2 * bl + qs] = (1.0 - a)[qs * 128:(qs + 1) * 128]

    maskm = np.zeros((1, 256), dtype=BF)
    if core % 4 == 3:
        maskm[:] = np.float32(-30.0 / SCALE).astype(BF)
    return dict(xh=xh_arr, xr=xr, cosT=cosT, sinT=sinT, alpha=alpha,
                alham=alham, maskm=maskm)


class Sched:
    ENGINES = ("y", "g", "t", "v", "s")

    def __init__(self):
        self.q = {e: [] for e in self.ENGINES}
        self.cnt = {}
        self.hw = {}
        self.semh = {}

    def new_sem(self, name, handle):
        self.semh[name] = handle
        self.cnt[name] = 0
        for e in self.ENGINES:
            self.hw[(e, name)] = 0

    def wait(self, e, sem, val):
        if val is None or val <= 0:
            return
        if self.hw[(e, sem)] >= val:
            return
        self.hw[(e, sem)] = val
        h = self.semh[sem]
        self.q[e].append(lambda eng, h=h, v=val: eng.wait_ge(h, v))

    def emit(self, e, fn, inc=None, k=1):
        if inc is None:
            self.q[e].append(fn)
            return None
        self.cnt[inc] += k
        tgt = self.cnt[inc]
        h = self.semh[inc]

        def wrapped(eng, fn=fn, h=h, k=k):
            fn(eng).then_inc(h, k)
        self.q[e].append(wrapped)
        return tgt

    def dma(self, e, out_ap, in_ap, sem):
        return self.emit(
            e, lambda eng, o=out_ap, i=in_ap: eng.dma_start(out=o, in_=i),
            inc=sem, k=16)

    def run(self, e, eng):
        for fn in self.q[e]:
            fn(eng)


def _bc(ap, reps, inner):
    pat = [list(p) for p in ap.ap]
    assert len(pat) == 2, pat
    return bass_rust.AP(ap.tensor, ap.offset, [pat[0], [0, reps], pat[1]])


def _strided(ap, elem_off, dims):
    pat = [list(p) for p in ap.ap]
    return bass_rust.AP(ap.tensor, ap.offset + elem_off,
                        [pat[0]] + [list(d) for d in dims])


def build_nc(debug_taps=()):
    nc = bass.Bass("TRN2", target_bir_lowering=False)

    di = {}

    def inp(name, shape, dt):
        di[name] = nc.dram_tensor(name, list(shape), dt,
                                  kind="ExternalInput").ap()
        return di[name]

    xh = inp("xh", (6, 128, 32, 256), F32)
    xr = inp("xr", (32, 128, 1024), F32)
    wq = inp("wq", (32, 128, 32, 128), F32)
    wo = inp("wo", (32, 128, 32, 128), F32)
    wk = inp("wk", (128, 32, 128), F32)
    wv = inp("wv", (128, 32, 128), F32)
    uq = inp("uq", (32, 128, 32, 128), U16)
    n4 = inp("n4", (32, 4, 32, 128), F32)
    nt = inp("nt", (128, 4096), F32)
    fbias_d = inp("fbias", (128, 32), F32)
    cosT = inp("cosT", (128, 1536), F32)
    sinT = inp("sinT", (128, 1536), F32)
    alpha = inp("alpha", (128, 8), F32)
    alham = inp("alham", (128, 8), F32)
    maskm = inp("maskm", (1, 256), BF16)
    rsign = inp("rsign", (128, 128), BF16)
    idn = inp("idn", (128, 128), BF16)
    epad = inp("epad", (128, 256), BF16)
    e4 = inp("e4", (4, 128), BF16)
    ones1 = inp("ones1", (1, 128), F32)
    onesc = inp("onesc", (128, 1), BF16)
    onesr = inp("onesr", (1, 256), BF16)
    consts = inp("consts", (128, 8), F32)
    mask32 = inp("mask32", (128, 1), I32)
    inv2s = inp("inv2s", (128, 1), F32)

    out_d = nc.dram_tensor("out", [32, 128, 1024], F32,
                           kind="ExternalOutput").ap()
    x2sp = nc.dram_tensor("x2sp", [32, 128, 1024], F32, kind="Internal").ap()
    dbg_d = {}
    for tap, shape, dt in (
        ("dbg_h", [128, 32768], BF16), ("dbg_kr", [128, 1536], BF16),
        ("dbg_vext", [128, 1548], BF16), ("dbg_qr", [128, 4096], BF16),
        ("dbg_ao", [128, 32768], BF16), ("dbg_a", [128, 32768], BF16),
        ("dbg_h2", [128, 32768], BF16), ("dbg_f", [128, 32768], BF16),
    ):
        if tap in debug_taps:
            dbg_d[tap] = nc.dram_tensor(tap, shape, dt,
                                        kind="ExternalOutput").ap()

    ctx = contextlib.ExitStack()
    heap = ctx.enter_context(nc.sbuf_tensor("heap", [128, HEAP_BYTES], U8))
    P = ctx.enter_context(nc.psum_tensor("psum", [128, 4096], F32))
    PB = P.bitcast(BF16)

    sch = Sched()
    for s in ("sT", "sV", "sS", "sG", "dCN", "dW0", "dXB0", "dXB1",
              "dWS0", "dWS1", "dN4", "dNT", "dXR0", "dXR1", "dSP0", "dSP1",
              "dFR0", "dFR1", "dOUT0", "dOUT1", "dDBG"):
        sch.new_sem(s, ctx.enter_context(nc.semaphore(s)))
    for i_ in range(8):
        sch.new_sem("dT%d" % i_, ctx.enter_context(nc.semaphore("dT%d" % i_)))

    def vw(off, n, dt, p0=0, p1=128):
        sz = mybir.dt.size(dt)
        return heap[p0:p1, off:off + n * sz].bitcast(dt)

    def pv(elem_off, n, p0=0, p1=128):
        return P[p0:p1, elem_off:elem_off + n]

    O = _P.map
    W = sch.wait
    E = sch.emit
    DMA = sch.dma

    cos_v = vw(O["cos"], 1536, BF16)
    sin_v = vw(O["sin"], 1536, BF16)
    wk_v = vw(O["wk"], 4096, BF16)
    wv_v = vw(O["wv"], 4096, BF16)
    kr_v = vw(O["kr"], 1536, BF16)
    vext_v = vw(O["vext"], 1548, BF16)
    rsign_v = vw(O["rsign"], 128, BF16)
    idn_v = vw(O["idn"], 128, BF16)
    epad_v = vw(O["epad"], 256, BF16)
    e4_v = vw(O["e4"], 128, BF16, 0, 4)
    ones1_v = vw(O["ones1"], 128, F32, 0, 1)
    onesc_v = vw(O["onesc"], 1, BF16)
    alham_v = vw(O["alham"], 8, F32)
    alpha_v = vw(O["alpha"], 8, F32)
    maskm_v = vw(O["maskm"], 256, BF16, 0, 1)
    onesr_v = vw(O["onesr"], 256, BF16, 0, 1)
    fbias_v = vw(O["fbias"], 32, F32)
    consts_v = vw(O["consts"], 8, F32)
    mask32_v = vw(O["shiftv"], 1, I32)
    inv2s_v = vw(O["shiftv"] + 32, 1, F32)
    eps_ap = vw(O["consts"], 8, F32, 0, 1)[:, 0:1]
    zero_ap = consts_v[:, 1:2]
    stat = vw(O["stat"], 2048, F32, 0, 1)
    scr = vw(O["scr"], 16, F32)
    kb_v = vw(O["kb"], 256, BF16)
    blnd_v = vw(O["blnd"], 512, BF16)
    atA_v = vw(O["atA"], 1024, BF16)
    atB_v = vw(O["atB"], 1024, BF16)
    qb_v = vw(O["qb"], 1024, BF16)

    for name, v_ in (("rsign", rsign_v),
                     ("idn", idn_v), ("epad", epad_v), ("e4", e4_v),
                     ("ones1", ones1_v), ("onesc", onesc_v),
                     ("alham", alham_v), ("alpha", alpha_v),
                     ("maskm", maskm_v), ("onesr", onesr_v),
                     ("fbias", fbias_v),
                     ("consts", consts_v), ("mask32", mask32_v),
                     ("inv2s", inv2s_v)):
        DMA("y", v_, di[name], "dCN")
    c_consts = sch.cnt["dCN"]
    DMA("g", cos_v, di["cosT"], "dW0")
    DMA("g", sin_v, di["sinT"], "dW0")
    DMA("g", wk_v, wk, "dW0")
    DMA("g", wv_v, wv, "dW0")
    w0_all = sch.cnt["dW0"]
    for kc in range(12):
        E("v", lambda eng, a=vext_v[:, kc * 129 + 128: kc * 129 + 129]:
          eng.memset(a, 1.0), inc=("sV" if kc == 11 else None))

    xb_off = [R1, R1 + 16384]
    xsq_off = R1 + 32768
    htmp_off = R1 + 49152
    hown_off = R2
    hown = vw(hown_off, 32768, BF16)

    xb_t = [None] * 6
    norm_done = [0] * 6
    kv_done = [0] * 6
    rope_done = [0] * 6
    mvar_done = [0] * 6
    sumsq_done = [0] * 6

    for e_ in ("t", "v", "s"):
        W(e_, "dCN", c_consts)
    for tc in range(6):
        xb = vw(xb_off[tc % 2], 8192, BF16)
        xsq = vw(xsq_off, 8192, BF16)
        xbsem = "dXB0" if tc % 2 == 0 else "dXB1"
        if tc >= 2:
            W("g", "sV", norm_done[tc - 2])
        xb_t[tc] = DMA("g", xb, xh[tc], xbsem)
        W("s", xbsem, xb_t[tc])
        if tc >= 1:
            W("s", "sT", sumsq_done[tc - 1])
        E("s", lambda eng, o=xsq, i=xb: eng.activation(
            out=o, in_=i, func=AF.Square, bias=0.0, scale=1.0), inc="sS")
        sq_ready = sch.cnt["sS"]
        W("t", xbsem, xb_t[tc])
        if tc >= 1:
            W("t", "sV", mvar_done[tc - 1])
        for ic in range(NCH):
            E("t", lambda eng, o=pv(0, 256, 0, 1), l=onesc_v,
              r=xb[:, ic * 256:(ic + 1) * 256], st=(ic == 0), sp=(ic == 31):
              eng.matmul(o, l, r, start=st, stop=sp))
        W("t", "sS", sq_ready)
        for ic in range(NCH):
            E("t", lambda eng, o=pv(512, 256, 0, 1), l=onesc_v,
              r=xsq[:, ic * 256:(ic + 1) * 256], st=(ic == 0), sp=(ic == 31):
              eng.matmul(o, l, r, start=st, stop=sp),
              inc=("sT" if ic == 31 else None))
        stats_done = sch.cnt["sT"]
        sumsq_done[tc] = stats_done
        W("v", "sT", stats_done)
        E("v", lambda eng: eng.tensor_scalar(
            out=stat[:, 0:256], in0=pv(0, 256, 0, 1), scalar1=1.0 / 4096,
            scalar2=None, op0=OP.mult), inc="sV")
        W("s", "sV", sch.cnt["sV"])
        E("s", lambda eng: eng.activation(
            out=stat[:, 256:512], in_=stat[:, 0:256], func=AF.Square,
            bias=0.0, scale=1.0), inc="sS")
        W("v", "sS", sch.cnt["sS"])
        E("v", lambda eng: eng.scalar_tensor_tensor(
            out=stat[:, 512:768], in0=pv(512, 256, 0, 1), scalar=1.0 / 4096,
            in1=stat[:, 256:512], op0=OP.mult, op1=OP.subtract), inc="sV")
        W("s", "sV", sch.cnt["sV"])
        E("s", lambda eng: eng.activation(
            out=stat[:, 512:768], in_=stat[:, 512:768], func=AF.Sqrt,
            bias=eps_ap, scale=1.0), inc="sS")
        W("v", "sS", sch.cnt["sS"])
        E("v", lambda eng: eng.reciprocal(stat[:, 512:768], stat[:, 512:768]))
        E("v", lambda eng: eng.drain())
        E("v", lambda eng: eng.tensor_tensor(
            out=stat[:, 0:256], in0=stat[:, 0:256], in1=stat[:, 512:768],
            op=OP.mult), inc="sV")
        mvar_done[tc] = sch.cnt["sV"]
        bb = 1024 if tc % 2 == 0 else 1536
        W("t", "sV", mvar_done[tc])
        if tc >= 2:
            W("t", "sV", norm_done[tc - 2])
        E("t", lambda eng, o=pv(bb, 256), r=stat[:, 512:768]:
          eng.matmul(o, ones1_v, r, start=True, stop=True))
        E("t", lambda eng, o=pv(bb + 256, 256), r=stat[:, 0:256]:
          eng.matmul(o, ones1_v, r, start=True, stop=True), inc="sT")
        bcast_done = sch.cnt["sT"]
        W("v", "sT", bcast_done)
        if tc in (1, 2, 3, 4):
            hdst = _strided(hown, (tc - 1) * 256, [[1024, 32], [1, 256]])
        else:
            hch = vw(htmp_off, 8192, BF16)
            hdst = hch.rearrange("p (c t) -> p c t", c=32)
        xb3 = xb.rearrange("p (c t) -> p c t", c=32)
        E("v", lambda eng, o=hdst, i=xb3, b_=_bc(pv(bb, 256), 32, 256):
          eng.tensor_tensor(out=o, in0=i, in1=b_, op=OP.mult))
        E("v", lambda eng: eng.drain())
        E("v", lambda eng, o=hdst, b_=_bc(pv(bb + 256, 256), 32, 256):
          eng.tensor_tensor(out=o, in0=o, in1=b_, op=OP.subtract), inc="sV")
        norm_done[tc] = sch.cnt["sV"]

        def hsl(ic, lo=0, n=256, tc=tc):
            if tc in (1, 2, 3, 4):
                return vw(hown_off + ic * 2048 + (tc - 1) * 512 + lo * 2,
                          n, BF16)
            return vw(htmp_off + ic * 512 + lo * 2, n, BF16)

        W("t", "sV", norm_done[tc])
        if tc >= 1:
            W("t", "sV", rope_done[tc - 1])
            W("t", "sS", kv_done[tc - 1])
        W("t", "dW0", w0_all)
        for ic in range(NCH):
            E("t", lambda eng, o=pv(2048, 256),
              l=wk_v[:, ic * 128:(ic + 1) * 128], r=hsl(ic),
              st=(ic == 0), sp=(ic == 31):
              eng.matmul(o, l, r, start=st, stop=sp),
              inc=("sT" if ic == 31 else None))
        k_done = sch.cnt["sT"]
        W("s", "sT", k_done)
        E("s", lambda eng: eng.activation(
            out=kb_v, in_=pv(2048, 256), func=AF.Copy, bias=0.0, scale=1.0),
            inc="sS")
        W("t", "sS", sch.cnt["sS"])
        E("t", lambda eng: eng.matmul(
            pv(2560, 256), rsign_v, kb_v, start=True, stop=True), inc="sT")
        rk_done = sch.cnt["sT"]
        W("v", "sT", rk_done)
        cslc = cos_v[:, tc * 256:(tc + 1) * 256]
        sslc = sin_v[:, tc * 256:(tc + 1) * 256]
        krd = kr_v[:, tc * 256:(tc + 1) * 256]
        E("v", lambda eng, o=krd, c=cslc: eng.tensor_tensor(
            out=o, in0=pv(2048, 256), in1=c, op=OP.mult))
        E("v", lambda eng, s_=sslc: eng.tensor_tensor(
            out=kb_v, in0=pv(2560, 256), in1=s_, op=OP.mult))
        E("v", lambda eng: eng.drain())
        E("v", lambda eng, o=krd: eng.tensor_tensor(
            out=o, in0=o, in1=kb_v, op=OP.add), inc="sV")
        rope_done[tc] = sch.cnt["sV"]
        W("t", "dW0", w0_all)
        for qs in range(2):
            for ic in range(NCH):
                E("t", lambda eng, o=pv(3072 + qs * 128, 128),
                  l=hsl(ic, qs * 128, 128),
                  r=wv_v[:, ic * 128:(ic + 1) * 128],
                  st=(ic == 0), sp=(ic == 31):
                  eng.matmul(o, l, r, start=st, stop=sp),
                  inc=("sT" if (ic == 31 and qs == 1) else None))
        v_done = sch.cnt["sT"]
        W("s", "sT", v_done)
        for qs in range(2):
            kc = tc * 2 + qs
            E("s", lambda eng, o=vext_v[:, kc * 129:kc * 129 + 128],
              i=pv(3072 + qs * 128, 128):
              eng.activation(out=o, in_=i, func=AF.Copy, bias=0.0, scale=1.0),
              inc=("sS" if qs == 1 else None))
        kv_done[tc] = sch.cnt["sS"]

    if "dbg_h" in dbg_d:
        W("y", "sV", norm_done[4])
        th_ = DMA("y", dbg_d["dbg_h"], hown, "dT1")
        W("s", "dT1", th_)
    if "dbg_kr" in dbg_d:
        W("y", "sV", rope_done[5])
        DMA("y", dbg_d["dbg_kr"], kr_v, "dT5")
    if "dbg_vext" in dbg_d:
        W("y", "sS", kv_done[5])
        DMA("y", dbg_d["dbg_vext"], vext_v, "dT6")

    ao_off = R1
    ao = vw(ao_off, 32768, BF16)
    qr_off = [RQ, RQ + 8192]
    ws_off = [R3, R3 + 8192]
    ws_t = [None, None]
    ws_freed = [0, 0]
    qps_free = [[0, 0], [0, 0]]
    rqps_free = [0, 0]
    qb_free = [0, 0]
    sig_done = {}
    sps_free = [0, 0, 0]
    avps_free_v = [0, 0]
    tps_free = [0, 0]
    tp_prev = [0, 0]
    pend_tp = [None]
    unit_ctr = 0

    def flush_tp():
        if pend_tp[0] is None:
            return
        h_abs, b, blslot, blv = pend_tp[0]
        pend_tp[0] = None
        tslot = h_abs % 2
        W("t", "sV", blv)
        W("t", "sV", tps_free[tslot])
        tbase = 2048 + tslot * 256
        for qs in range(2):
            E("t", lambda eng, o=PB[:, tbase + qs * 128:
                                    tbase + (qs + 1) * 128],
              i=blnd_v[:, blslot * 256 + qs * 128:
                       blslot * 256 + (qs + 1) * 128]:
              eng.transpose(o, i, idn_v),
              inc=("sT" if qs == 1 else None))
        tp_prev[blslot] = sch.cnt["sT"]
        tp_done = sch.cnt["sT"]
        W("v", "sT", tp_done)
        E("v", lambda eng, o=ao[:, h_abs * 1024 + b * 256:
                                h_abs * 1024 + (b + 1) * 256],
          i=PB[:, tbase:tbase + 256]:
          eng.tensor_copy(o, i), inc="sV")
        tps_free[tslot] = sch.cnt["sV"]
    last_blend_v = 0
    last_blend_s = 0

    for hg in range(8):
        heads = list(range(hg * 4, hg * 4 + 4))
        qrv = vw(qr_off[hg % 2], 4096, BF16)
        for j, oc in enumerate(heads):
            slot = oc % 2
            wssem = "dWS0" if slot == 0 else "dWS1"
            W("g", "sT", ws_freed[slot])
            ws_t[slot] = DMA("g", vw(ws_off[slot], 4096, BF16), wq[oc], wssem)
            W("t", wssem, ws_t[slot])
            if hg == 0 and oc == heads[0]:
                W("t", "sV", rope_done[5])
            if hg > 0 and oc == heads[0]:
                W("t", "sV", avps_free_v[0])
                W("t", "sV", avps_free_v[1])
                W("t", "sV", tps_free[0])
                W("t", "sV", tps_free[1])
                W("t", "sS", sps_free[0])
                W("t", "sS", sps_free[1])
                W("t", "sS", sps_free[2])
            for nh in range(2):
                qcol = (2 * slot + nh) * 512
                W("t", "sV", qps_free[slot][nh])
                for ic in range(NCH):
                    E("t", lambda eng, o=pv(qcol, 512),
                      l=vw(ws_off[slot] + ic * 256, 128, BF16),
                      r=vw(hown_off + ic * 2048 + nh * 1024, 512, BF16),
                      st=(ic == 0), sp=(ic == 31):
                      eng.matmul(o, l, r, start=st, stop=sp),
                      inc=("sT" if ic == 31 else None))
                q_done = sch.cnt["sT"]
                W("s", "sT", q_done)
                W("s", "sV", qb_free[nh])
                E("s", lambda eng, o=qb_v[:, nh * 512:(nh + 1) * 512],
                  i=pv(qcol, 512):
                  eng.activation(out=o, in_=i, func=AF.Copy, bias=0.0,
                                 scale=1.0), inc="sS")
                qb_done = sch.cnt["sS"]
                W("t", "sS", qb_done)
                W("t", "sV", rqps_free[nh])
                E("t", lambda eng, o=pv(2048 + nh * 512, 512),
                  r=qb_v[:, nh * 512:(nh + 1) * 512]:
                  eng.matmul(o, rsign_v, r, start=True, stop=True), inc="sT")
                rot_done = sch.cnt["sT"]
                W("v", "sT", rot_done)
                co = cos_v[:, 256 + nh * 512: 256 + (nh + 1) * 512]
                so = sin_v[:, 256 + nh * 512: 256 + (nh + 1) * 512]
                qd = qrv[:, j * 1024 + nh * 512: j * 1024 + (nh + 1) * 512]
                tu = qb_v[:, nh * 512:(nh + 1) * 512]
                E("v", lambda eng, o=tu, c=co, i=pv(qcol, 512):
                  eng.tensor_tensor(out=o, in0=i, in1=c, op=OP.mult))
                E("v", lambda eng, o=qd, s_=so, i=pv(2048 + nh * 512, 512):
                  eng.tensor_tensor(out=o, in0=i, in1=s_, op=OP.mult))
                E("v", lambda eng: eng.drain())
                E("v", lambda eng, o=qd, u=tu: eng.tensor_tensor(
                    out=o, in0=o, in1=u, op=OP.add), inc="sV")
                qps_free[slot][nh] = sch.cnt["sV"]
                rqps_free[nh] = sch.cnt["sV"]
                qb_free[nh] = sch.cnt["sV"]
            ws_freed[slot] = q_done
        rope_hg_done = sch.cnt["sV"]

        if hg == 0 and "dbg_qr" in dbg_d:
            W("y", "sV", rope_hg_done)
            tq_ = DMA("y", dbg_d["dbg_qr"], qrv, "dT0")
            W("v", "dT0", tq_)

        for b in range(4):
            for j, h_abs in enumerate(heads):
                sig_win = [0, 0]
                for win in range(2):
                    atv = atA_v if win == 0 else atB_v
                    kb_chunk = 2 * (b + win)
                    for hf in range(2):
                        u = unit_ctr
                        unit_ctr += 1
                        sbase = 1536 + (u % 3) * 512
                        W("t", "sV", rope_hg_done)
                        W("t", "sS", sps_free[u % 3])
                        for k2 in range(2):
                            kc4 = hf * 2 + k2
                            is_mask = (b == 3 and win == 1 and kc4 >= 2)
                            last = (k2 == 1 and not is_mask)
                            E("t", lambda eng, o=pv(sbase + k2 * 256, 256),
                              l=kr_v[:, (kb_chunk + kc4) * 128:
                                     (kb_chunk + kc4 + 1) * 128],
                              r=qrv[:, j * 1024 + b * 256:
                                    j * 1024 + (b + 1) * 256],
                              sp=(not is_mask):
                              eng.matmul(o, l, r, start=True, stop=sp),
                              inc=("sT" if last else None))
                            if is_mask:
                                E("t", lambda eng,
                                  o=pv(sbase + k2 * 256, 256),
                                  l=maskm_v[:, (kc4 - 2) * 128:
                                            (kc4 - 1) * 128]:
                                  eng.matmul(o, l, onesr_v, start=False,
                                             stop=True),
                                  inc=("sT" if k2 == 1 else None))
                        sc_done = sch.cnt["sT"]
                        W("s", "sT", sc_done)
                        E("s", lambda eng, o=atv[:, hf * 512:(hf + 1) * 512],
                          i=pv(sbase, 512):
                          eng.activation(out=o, in_=i, func=AF.Sigmoid,
                                         bias=zero_ap, scale=SCALE),
                          inc="sS")
                        sig_done[u] = sch.cnt["sS"]
                        sps_free[u % 3] = sig_done[u]
                        sig_win[win] = sig_done[u]
                aset = h_abs % 2
                flush_tp()
                W("t", "sV", avps_free_v[aset])
                for win in range(2):
                    atv = atA_v if win == 0 else atB_v
                    abase = (0 if win == 0 else 512) + aset * 3072
                    kb_chunk = 2 * (b + win)
                    W("t", "sS", sig_win[win])
                    for qs in range(2):
                        for kc4 in range(4):
                            last = (win == 1 and qs == 1 and kc4 == 3)
                            E("t", lambda eng, o=pv(abase + qs * 129, 129),
                              l=atv[:, kc4 * 256 + qs * 128:
                                    kc4 * 256 + (qs + 1) * 128],
                              r=vext_v[:, (kb_chunk + kc4) * 129:
                                       (kb_chunk + kc4 + 1) * 129],
                              st=(kc4 == 0), sp=(kc4 == 3):
                              eng.matmul(o, l, r, start=st, stop=sp),
                              inc=("sT" if last else None))
                av_done = sch.cnt["sT"]
                blslot = (b * 4 + j) % 2
                W("v", "sT", av_done)
                W("v", "sT", tp_prev[blslot])
                for win in range(2):
                    abase = (0 if win == 0 else 512) + aset * 3072
                    base_ap = P[:, abase + 128:abase + 129]
                    sums = bass_rust.AP(base_ap.tensor, base_ap.offset,
                                        [[4096, 128], [129, 2]])
                    E("v", lambda eng, o=scr[:, win * 2:win * 2 + 2], i=sums:
                      eng.reciprocal(o, i))
                E("v", lambda eng: eng.drain())
                E("v", lambda eng, b2=alham_v[:, 2 * b:2 * b + 2]:
                  eng.tensor_tensor(out=scr[:, 4:6], in0=scr[:, 0:2],
                                    in1=b2, op=OP.mult))
                E("v", lambda eng, b2=alpha_v[:, 2 * b:2 * b + 2]:
                  eng.tensor_tensor(out=scr[:, 6:8], in0=scr[:, 2:4],
                                    in1=b2, op=OP.mult), inc="sV")
                E("v", lambda eng: eng.drain())
                for qs in range(2):
                    E("v", lambda eng, o=kb_v[:, qs * 128:(qs + 1) * 128],
                      i=pv(aset * 3072 + qs * 129, 128),
                      sc1=scr[:, 4 + qs:5 + qs]:
                      eng.tensor_scalar(out=o, in0=i, scalar1=sc1,
                                        scalar2=None, op0=OP.mult))
                E("v", lambda eng: eng.drain())
                for qs in range(2):
                    E("v", lambda eng, o=blnd_v[:, blslot * 256 + qs * 128:
                                            blslot * 256 + (qs + 1) * 128],
                      i=pv(aset * 3072 + 512 + qs * 129, 128),
                      sc2=scr[:, 6 + qs:7 + qs],
                      t_=kb_v[:, qs * 128:(qs + 1) * 128]:
                      eng.scalar_tensor_tensor(
                          out=o, in0=i, scalar=sc2, in1=t_,
                          op0=OP.mult, op1=OP.add),
                      inc=("sV" if qs == 1 else None))
                last_blend_v = sch.cnt["sV"]
                avps_free_v[aset] = last_blend_v
                pend_tp[0] = (h_abs, b, blslot, last_blend_v)
        flush_tp()
    ao_done_v = sch.cnt["sV"]

    if "dbg_ao" in dbg_d:
        W("y", "sV", ao_done_v)
        ta_ = DMA("y", dbg_d["dbg_ao"], ao, "dT2")
        W("y", "dT2", ta_)

    abf_off = R2
    abf = vw(abf_off, 32768, BF16)
    asq = vw(RQ, 1024, BF16)
    evac_done = [0, 0]
    asq_free = 0
    for oc in range(NCH):
        slot = oc % 2
        wssem = "dWS0" if slot == 0 else "dWS1"
        W("g", "sT", ws_freed[slot])
        ws_t[slot] = DMA("g", vw(ws_off[slot], 4096, BF16), wo[oc], wssem)
        W("t", wssem, ws_t[slot])
        if oc == 0:
            W("t", "sV", ao_done_v)
        W("t", "sS", evac_done[slot])
        base = slot * 1024
        for nh in range(2):
            for ic in range(NCH):
                E("t", lambda eng, o=pv(base + nh * 512, 512),
                  l=vw(ws_off[slot] + ic * 256, 128, BF16),
                  r=ao[:, ic * 1024 + nh * 512: ic * 1024 + (nh + 1) * 512],
                  st=(ic == 0), sp=(ic == 31):
                  eng.matmul(o, l, r, start=st, stop=sp),
                  inc=("sT" if (nh == 1 and ic == 31) else None))
        ws_freed[slot] = sch.cnt["sT"]
        wo_done = sch.cnt["sT"]
        W("s", "sT", wo_done)
        W("s", "sT", asq_free)
        E("s", lambda eng, o=abf[:, oc * 1024:(oc + 1) * 1024],
          i=pv(base, 1024):
          eng.activation(out=o, in_=i, func=AF.Copy, bias=0.0, scale=1.0))
        E("s", lambda eng, i=pv(base, 1024):
          eng.activation(out=asq, in_=i, func=AF.Square, bias=0.0, scale=1.0),
          inc="sS")
        evac_done[slot] = sch.cnt["sS"]
        W("t", "sS", evac_done[slot])
        for th in range(2):
            E("t", lambda eng, o=pv(2048 + th * 512, 512, 0, 1),
              r=abf[:, oc * 1024 + th * 512: oc * 1024 + (th + 1) * 512],
              st=(oc == 0), sp=(oc == 31):
              eng.matmul(o, onesc_v, r, start=st, stop=sp))
        for th in range(2):
            E("t", lambda eng, o=pv(3072 + th * 512, 512, 0, 1),
              r=asq[:, th * 512:(th + 1) * 512], st=(oc == 0), sp=(oc == 31):
              eng.matmul(o, onesc_v, r, start=st, stop=sp),
              inc=("sT" if th == 1 else None))
        asq_free = sch.cnt["sT"]

    if "dbg_a" in dbg_d:
        W("y", "sS", max(evac_done))
        tb_ = DMA("y", dbg_d["dbg_a"], abf, "dT3")
        W("v", "dT3", tb_)

    def ln_rows():
        W("v", "sT", sch.cnt["sT"])
        E("v", lambda eng: eng.tensor_scalar(
            out=stat[:, 0:1024], in0=pv(2048, 1024, 0, 1),
            scalar1=1.0 / 4096, scalar2=None, op0=OP.mult), inc="sV")
        W("s", "sV", sch.cnt["sV"])
        E("s", lambda eng: eng.activation(
            out=stat[:, 1024:2048], in_=stat[:, 0:1024], func=AF.Square,
            bias=0.0, scale=1.0), inc="sS")
        W("v", "sS", sch.cnt["sS"])
        E("v", lambda eng: eng.scalar_tensor_tensor(
            out=stat[:, 1024:2048], in0=pv(3072, 1024, 0, 1),
            scalar=1.0 / 4096, in1=stat[:, 1024:2048],
            op0=OP.mult, op1=OP.subtract), inc="sV")
        W("s", "sV", sch.cnt["sV"])
        E("s", lambda eng: eng.activation(
            out=stat[:, 1024:2048], in_=stat[:, 1024:2048], func=AF.Sqrt,
            bias=eps_ap, scale=1.0), inc="sS")
        W("v", "sS", sch.cnt["sS"])
        E("v", lambda eng: eng.reciprocal(stat[:, 1024:2048],
                                          stat[:, 1024:2048]))
        E("v", lambda eng: eng.drain())
        E("v", lambda eng: eng.tensor_tensor(
            out=stat[:, 0:1024], in0=stat[:, 0:1024], in1=stat[:, 1024:2048],
            op=OP.mult), inc="sV")
        return sch.cnt["sV"]

    def bcast_rows(rstd_off, mr_off):
        W("t", "sV", sch.cnt["sV"])
        for k in range(2):
            E("t", lambda eng, o=pv(rstd_off + k * 512, 512),
              r=stat[:, 1024 + k * 512:1024 + (k + 1) * 512]:
              eng.matmul(o, ones1_v, r, start=True, stop=True))
        for k in range(2):
            E("t", lambda eng, o=pv(mr_off + k * 512, 512),
              r=stat[:, k * 512:(k + 1) * 512]:
              eng.matmul(o, ones1_v, r, start=True, stop=True),
              inc=("sT" if k == 1 else None))
        return sch.cnt["sT"]

    wo_all = max(ws_freed)
    asq_last = asq_free
    ln_rows()
    bcD = bcast_rows(0, 1024)

    xrl4_off = [R1, R1 + 16384]
    x2o4_off = [R1 + 32768, R1 + 49152]
    tmp4_off = RQ
    x2sq4_off = RQ + 8192
    rbD = vw(O["atA"], 1024, BF16)
    mbD = vw(O["atB"], 1024, BF16)
    W("v", "sT", bcD)
    E("v", lambda eng: eng.tensor_copy(rbD, pv(0, 1024)))
    E("v", lambda eng: eng.tensor_copy(mbD, pv(1024, 1024)))
    E("v", lambda eng: eng.drain(), inc="sV")
    bcD_bf = sch.cnt["sV"]

    def grp_ap(dram, g):
        return bass_rust.AP(dram.tensor, g * 4 * 131072,
                            [[1024, 128], [131072, 4], [1, 1024]])

    def sb3(view):
        return _strided(view, 0, [[1024, 4], [1, 1024]])

    xrl_t = [0, 0]
    x2_done = [0] * 8
    scal_done = [0] * 8
    spill_t = [0, 0]
    spill_tg = [0] * 8
    stats_done_g = [0] * 8
    for g in range(8):
        buf = g % 2
        xrsem = "dXR%d" % buf
        if g < 2:
            W("y", "sT", wo_all)
        else:
            W("y", "sV", x2_done[g - 2])
        xrl_t[buf] = DMA("y", sb3(vw(xrl4_off[buf], 4096, F32)),
                         grp_ap(xr, g), xrsem)
        W("v", xrsem, xrl_t[buf])
        W("v", "sV", bcD_bf)
        if g == 0:
            W("v", "sT", asq_last)
        if g >= 2:
            W("v", "sS", scal_done[g - 2])
            W("v", "dSP%d" % buf, spill_t[buf])
        tmp4 = vw(tmp4_off, 4096, BF16)
        t3 = tmp4.rearrange("p (c t) -> p c t", c=4)
        ab3 = abf[:, g * 4096:(g + 1) * 4096].rearrange(
            "p (c t) -> p c t", c=4)
        x2o4 = vw(x2o4_off[buf], 4096, F32)
        E("v", lambda eng, o=t3, i=ab3, b_=_bc(rbD, 4, 1024):
          eng.tensor_tensor(out=o, in0=i, in1=b_, op=OP.mult))
        E("v", lambda eng: eng.drain())
        E("v", lambda eng, o=t3, b_=_bc(mbD, 4, 1024):
          eng.tensor_tensor(out=o, in0=o, in1=b_, op=OP.subtract))
        E("v", lambda eng: eng.drain())
        E("v", lambda eng, o=x2o4, t_=tmp4, x_=vw(xrl4_off[buf], 4096, F32):
          eng.tensor_tensor(out=o, in0=t_, in1=x_, op=OP.add), inc="sV")
        x2_done[g] = sch.cnt["sV"]
        W("s", "sV", x2_done[g])
        spill_t[buf] = DMA("s", grp_ap(x2sp, g), sb3(x2o4), "dSP%d" % buf)
        spill_tg[g] = spill_t[buf]
        if g >= 1:
            W("s", "sT", stats_done_g[g - 1])
        E("s", lambda eng, o=abf[:, g * 4096:(g + 1) * 4096], i=x2o4:
          eng.activation(out=o, in_=i, func=AF.Copy, bias=0.0, scale=1.0))
        E("s", lambda eng, o=vw(x2sq4_off, 4096, BF16), i=x2o4:
          eng.activation(out=o, in_=i, func=AF.Square, bias=0.0, scale=1.0),
          inc="sS")
        scal_done[g] = sch.cnt["sS"]
        W("t", "sS", scal_done[g])
        for oq in range(4):
            for th in range(2):
                E("t", lambda eng, o=pv(2048 + th * 512, 512, 0, 1),
                  r=abf[:, g * 4096 + oq * 1024 + th * 512:
                        g * 4096 + oq * 1024 + (th + 1) * 512],
                  st=(g == 0 and oq == 0), sp=(g == 7 and oq == 3):
                  eng.matmul(o, onesc_v, r, start=st, stop=sp))
        for oq in range(4):
            for th in range(2):
                E("t", lambda eng, o=pv(3072 + th * 512, 512, 0, 1),
                  r=vw(x2sq4_off + (oq * 1024 + th * 512) * 2, 512, BF16),
                  st=(g == 0 and oq == 0), sp=(g == 7 and oq == 3):
                  eng.matmul(o, onesc_v, r, start=st, stop=sp),
                  inc=("sT" if (oq == 3 and th == 1) else None))
        stats_done_g[g] = sch.cnt["sT"]

    ln_rows()
    bc2 = bcast_rows(2048, 3072)

    h2 = vw(R2, 32768, BF16)
    rb2 = rbD
    mb2 = mbD
    W("v", "sT", bc2)
    E("v", lambda eng: eng.tensor_copy(rb2, pv(2048, 1024)))
    E("v", lambda eng: eng.tensor_copy(mb2, pv(3072, 1024)))
    E("v", lambda eng: eng.drain(), inc="sV")
    bc2_bf = sch.cnt["sV"]

    ntneg = vw(RQ, 4096, BF16)
    n4_off = RQ + 8192
    us_t = [0, 0]
    for oc0 in (0, 1):
        W("g", "sT", ws_freed[oc0])
        us_t[oc0] = DMA("g", vw(ws_off[oc0], 4096, U16), uq[oc0],
                        "dWS%d" % oc0)
    W("g", "sV", x2_done[7])
    nt_t = DMA("g", ntneg, nt, "dNT")
    W("g", "sT", stats_done_g[7])
    n4t0 = DMA("g", vw(n4_off, 4096, BF16, 0, 4), n4[0], "dN4")

    h2g_done = [None] * 8
    for g in range(8):
        en = "g" if g % 3 == 2 else "v"
        sem = "sG" if en == "g" else "sV"
        h3 = h2[:, g * 4096:(g + 1) * 4096].rearrange("p (c t) -> p c t",
                                                      c=4)
        W(en, "sV", bc2_bf)
        E(en, lambda eng, o=h3, b_=_bc(rb2, 4, 1024):
          eng.tensor_tensor(out=o, in0=o, in1=b_, op=OP.mult))
        E(en, lambda eng: eng.drain())
        E(en, lambda eng, o=h3, b_=_bc(mb2, 4, 1024):
          eng.tensor_tensor(out=o, in0=o, in1=b_, op=OP.subtract), inc=sem)
        h2g_done[g] = (sem, sch.cnt[sem])

    if "dbg_h2" in dbg_d:
        W("y", "sV", h2g_done[7][1])
        W("y", "sG", sch.cnt["sG"])
        tc_ = DMA("y", dbg_d["dbg_h2"], h2, "dT4")
        W("v", "dT4", tc_)

    w1_off = [O["wk"], O["wv"]]
    hsb = vw(O["atA"], 1024, BF16)
    fsq_off = O["atB"]
    fbf_off = R1
    fbf = vw(fbf_off, 32768, BF16)

    W("v", "dNT", nt_t)
    E("v", lambda eng: eng.tensor_scalar(
        out=ntneg, in0=ntneg, scalar1=-1.0, scalar2=None, op0=OP.mult),
        inc="sV")
    ntneg_done = sch.cnt["sV"]

    for th in range(2):
        for ic in range(NCH):
            hs_, hv_ = h2g_done[ic // 4]
            W("t", hs_, hv_)
            E("t", lambda eng, o=pv(th * 512, 512),
              l=epad_v[:, 124 - ic * 4:252 - ic * 4],
              r=h2[:, ic * 1024 + th * 512: ic * 1024 + (th + 1) * 512],
              st=(ic == 0), sp=(ic == 31):
              eng.matmul(o, l, r, start=st, stop=sp),
              inc=("sT" if (th == 1 and ic == 31) else None))
    hmm_done = sch.cnt["sT"]
    W("s", "sT", hmm_done)
    E("s", lambda eng: eng.activation(
        out=hsb, in_=pv(0, 1024), func=AF.Copy, bias=0.0, scale=3.5),
        inc="sS")
    h75_done = sch.cnt["sS"]

    n4_free = 0
    w1v = {}
    fdone_oc = [0] * 32
    fevac_done = 0
    fsq_free = 0
    for oc in range(NCH):
        slot = oc % 2
        wssem = "dWS0" if slot == 0 else "dWS1"
        vq = vw(ws_off[slot], 4096, U16)
        vbf = vw(ws_off[slot], 4096, BF16)
        if oc >= 2:
            W("g", "sV", w1v[(oc - 2, 3)])
            us_t[slot] = DMA("g", vw(ws_off[slot], 4096, U16), uq[oc],
                             wssem)
        if oc >= 1:
            W("g", "sT", n4_free)
            n4t = DMA("g", vw(n4_off, 4096, BF16, 0, 4), n4[oc], "dN4")
        else:
            n4t = n4t0
        W("v", wssem, us_t[slot])
        if oc >= 2:
            W("v", "sT", fdone_oc[oc - 2])
        E("v", lambda eng, i=vq:
          eng.tensor_scalar(out=i, in0=i, scalar1=mask32_v, scalar2=None,
                            op0=OP.bitwise_and))
        E("v", lambda eng: eng.drain())
        E("v", lambda eng, i=vq, o=vbf:
          eng.tensor_scalar(out=o, in0=i, scalar1=inv2s_v, scalar2=None,
                            op0=OP.mult), inc="sV")
        E("v", lambda eng: eng.drain())

        def emit_main(icg4, oc=oc, slot=slot):
            W("t", "sV", w1v[(oc, icg4)])
            if icg4 == 0:
                if oc == 0:
                    W("t", "sS", h75_done)
                else:
                    W("t", "sS", fevac_done)
            for th in range(2):
                for ic in range(icg4 * 8, icg4 * 8 + 8):
                    E("t", lambda eng, o=pv(th * 512, 512),
                      l=vw(w1_off[slot] + ic * 256, 128, BF16),
                      r=h2[:, ic * 1024 + th * 512:
                           ic * 1024 + (th + 1) * 512],
                      st=(ic == 0):
                      eng.matmul(o, l, r, start=st, stop=False))

        for pr in range(4):
            W("t", "dN4", n4t)
            W("t", "sV", w1v.get((oc, pr - 1), w1v.get((oc - 1, 3), 0)))
            for k2 in range(2):
                E("t", lambda eng, o=pv(1024 + k2 * 512, 512),
                  r=vw(n4_off + (pr * 2 + k2) * 1024, 512, BF16, 0, 4):
                  eng.matmul(o, e4_v, r, start=True, stop=True),
                  inc=("sT" if k2 == 1 else None))
            nexp_done = sch.cnt["sT"]
            if pr == 3:
                n4_free = nexp_done
            W("v", "sT", nexp_done)
            E("v", lambda eng, o=vw(w1_off[slot] + pr * 2048, 1024, BF16),
              i=vbf[:, pr * 1024:(pr + 1) * 1024], n_=pv(1024, 1024):
              eng.tensor_tensor(out=o, in0=i, in1=n_, op=OP.mult),
              inc="sV")
            w1v[(oc, pr)] = sch.cnt["sV"]
            if pr >= 1:
                emit_main(pr - 1)
        emit_main(3)
        W("t", "sV", ntneg_done)
        for th in range(2):
            E("t", lambda eng, o=pv(th * 512, 512),
              l=ntneg[:, oc * 128:(oc + 1) * 128],
              r=hsb[:, th * 512:(th + 1) * 512]:
              eng.matmul(o, l, r, start=False, stop=True),
              inc=("sT" if th == 1 else None))
        fdone_oc[oc] = sch.cnt["sT"]
        if oc == 0:
            W("s", "sV", x2_done[7])
            W("s", "dSP0", spill_tg[6])
            W("s", "dSP1", spill_tg[7])
        W("s", "sT", fdone_oc[oc])
        W("s", "sT", fsq_free)
        for th in range(2):
            E("s", lambda eng, o=fbf[:, oc * 1024 + th * 512:
                                     oc * 1024 + (th + 1) * 512],
              i=pv(th * 512, 512), b_=fbias_v[:, oc:oc + 1]:
              eng.activation(out=o, in_=i, func=AF.Identity, bias=b_,
                             scale=2.0 / 7))
            E("s", lambda eng, o=vw(fsq_off + th * 1024, 512, BF16),
              i=pv(th * 512, 512), b_=fbias_v[:, oc:oc + 1]:
              eng.activation(out=o, in_=i, func=AF.Square, bias=b_,
                             scale=2.0 / 7), inc="sS")
            if th == 0:
                fevac_h0 = sch.cnt["sS"]
        fevac_done = sch.cnt["sS"]
        W("t", "sS", fevac_done)
        if oc == 0:
            W("t", "sV", bc2_bf)
        for th in range(2):
            E("t", lambda eng, o=pv(2048 + th * 512, 512, 0, 1),
              r=fbf[:, oc * 1024 + th * 512: oc * 1024 + (th + 1) * 512],
              st=(oc == 0), sp=(oc == 31):
              eng.matmul(o, onesc_v, r, start=st, stop=sp))
        for th in range(2):
            E("t", lambda eng, o=pv(3072 + th * 512, 512, 0, 1),
              r=vw(fsq_off + th * 1024, 512, BF16), st=(oc == 0),
              sp=(oc == 31):
              eng.matmul(o, onesc_v, r, start=st, stop=sp),
              inc=("sT" if th == 1 else None))
        fsq_free = sch.cnt["sT"]
    f_all_done = fdone_oc[31]

    if "dbg_f" in dbg_d:
        W("y", "sS", fevac_done)
        DMA("y", dbg_d["dbg_f"], fbf, "dT7")

    ln_rows()
    bc3 = bcast_rows(0, 1024)

    fb_off = [R3, O["wk"]]
    ob_off2 = [R2, R2 + 16384]
    tmpf4_off = R2 + 32768
    rb3 = vw(RQ, 1024, BF16)
    mb3 = vw(RQ + 2048, 1024, BF16)
    W("v", "sT", bc3)
    E("v", lambda eng: eng.tensor_copy(rb3, pv(0, 1024)))
    E("v", lambda eng: eng.tensor_copy(mb3, pv(1024, 1024)))
    E("v", lambda eng: eng.drain(), inc="sV")
    bc3_bf = sch.cnt["sV"]
    frl_t = [0, 0]
    out_t = [0, 0]
    of_done = [0] * 8
    for g in range(8):
        buf = g % 2
        frsem = "dFR%d" % buf
        if g == 0:
            W("y", "sV", w1v[(31, 3)])
        if g == 1:
            W("y", "sT", f_all_done)
        if g >= 2:
            W("y", "sV", of_done[g - 2])
        W("y", "dSP%d" % buf, spill_tg[g])
        frl_t[buf] = DMA("y", sb3(vw(fb_off[buf], 4096, F32)),
                         grp_ap(x2sp, g), frsem)
        W("v", frsem, frl_t[buf])
        W("v", "sV", bc3_bf)
        if g == 0:
            W("v", "sT", f_all_done)
        if g >= 2:
            W("v", "dOUT%d" % buf, out_t[buf])
        tmpf4 = vw(tmpf4_off, 4096, BF16)
        tf3 = tmpf4.rearrange("p (c t) -> p c t", c=4)
        fb3 = fbf[:, g * 4096:(g + 1) * 4096].rearrange(
            "p (c t) -> p c t", c=4)
        ob = vw(ob_off2[buf], 4096, F32)
        E("v", lambda eng, o=tf3, i=fb3, b_=_bc(rb3, 4, 1024):
          eng.tensor_tensor(out=o, in0=i, in1=b_, op=OP.mult))
        E("v", lambda eng: eng.drain())
        E("v", lambda eng, o=tf3, b_=_bc(mb3, 4, 1024):
          eng.tensor_tensor(out=o, in0=o, in1=b_, op=OP.subtract))
        E("v", lambda eng: eng.drain())
        E("v", lambda eng, o=ob, t_=tmpf4, x_=vw(fb_off[buf], 4096, F32):
          eng.tensor_tensor(out=o, in0=t_, in1=x_, op=OP.add), inc="sV")
        of_done[g] = sch.cnt["sV"]
        qe = "y" if g % 2 == 0 else "g"
        W(qe, "sV", of_done[g])
        out_t[buf] = DMA(qe, grp_ap(out_d, g), sb3(ob), "dOUT%d" % buf)

    final_dbg = 0
    for i_ in range(2):
        E("y", lambda eng, s_="dOUT%d" % i_, v_=sch.cnt["dOUT%d" % i_]:
          eng.wait_ge(sch.semh[s_], v_))
    for i_ in range(8):
        if sch.cnt["dT%d" % i_]:
            E("y", lambda eng, s_="dT%d" % i_, v_=sch.cnt["dT%d" % i_]:
              eng.wait_ge(sch.semh[s_], v_))

    with nc.Block() as blk:
        @blk.sync
        def _(eng):
            sch.run("y", eng)

        @blk.gpsimd
        def _(eng):
            sch.run("g", eng)

        @blk.tensor
        def _(eng):
            sch.run("t", eng)

        @blk.vector
        def _(eng):
            sch.run("v", eng)

        @blk.scalar
        def _(eng):
            sch.run("s", eng)

    ctx.close()
    return nc, list(dbg_d.keys())


def kernel(L0_attn_x, L0_attn_Wq, L0_attn_Wk, L0_attn_Wv, L0_attn_Wo,
           L0_ffn_weight_q3, L0_ffn_weight_norm, L0_ffn_bias):
    x = np.asarray(L0_attn_x, dtype=np.float32)
    shared = _prep_shared(np.asarray(L0_attn_Wq, np.float32),
                          np.asarray(L0_attn_Wk, np.float32),
                          np.asarray(L0_attn_Wv, np.float32),
                          np.asarray(L0_attn_Wo, np.float32),
                          np.asarray(L0_ffn_weight_q3),
                          np.asarray(L0_ffn_weight_norm, np.float32),
                          np.asarray(L0_ffn_bias, np.float32))
    nc, taps = build_nc(tuple(_dbg_taps))
    in_maps = []
    for c in range(NCORES):
        m = dict(shared)
        m.update(_prep_core(x, c))
        in_maps.append(m)
    res = run_bass_kernel_spmd(nc, in_maps, core_ids=list(range(NCORES)))
    out = np.empty((B, S, D), dtype=np.float32)
    for c in range(NCORES):
        b = c // 4
        T0 = 1024 * (c % 4)
        oc = res.results[c]["out"]
        out[b, T0:T0 + 1024, :] = np.ascontiguousarray(
            oc.transpose(2, 0, 1)).reshape(1024, 4096)
    if taps:
        kernel.dbg = [{t: res.results[c][t] for t in taps}
                      for c in range(NCORES)]
    return out
